# version 12
# baseline (speedup 1.0000x reference)
import numpy as np
import concourse.bass as bass
import concourse.mybir as mybir
from concourse.bass_utils import run_bass_kernel_spmd

F32 = mybir.dt.float32
BF16 = mybir.dt.bfloat16
AX = mybir.AxisListType.X
ALU = mybir.AluOpType
ACT = mybir.ActivationFunctionType

P = 128
D = 1024
NCD = D // P
H_SH = 4096
NEXP = 8
RH = 1024
EPS = 1e-5
KB = 1024


def _act_table_idx(nc, set_name):
    from concourse.hw_specs import get_activation_tables
    for idx, name in enumerate(get_activation_tables(nc.m.arch).keys()):
        if name == set_name:
            return idx
    raise KeyError(set_name)


def build_program(NT=16, debug=False, pass_t=None, DISABLE_HOOK=False):
    T = NT * P
    QB = min(512, T)
    NQB = T // QB
    QT_PER_B = QB // P
    TBLK = min(256, T)
    PASS_T = pass_t if pass_t is not None else min(1024, T)
    NPASS = T // PASS_T
    NTB = PASS_T // TBLK
    TT_PER_TB = TBLK // P
    NACC = PASS_T // P

    nc = bass.Bass("TRN2", target_bir_lowering=False)

    x_d = nc.dram_tensor("x", [T, D], F32, kind="ExternalInput").ap()
    wl_d = nc.dram_tensor("wl", [P, NCD], F32, kind="ExternalInput").ap()
    wr_d = nc.dram_tensor("wr", [P, NCD], F32, kind="ExternalInput").ap()
    w1t_d = nc.dram_tensor("w1t", [D, H_SH], F32, kind="ExternalInput").ap()
    w2t_d = nc.dram_tensor("w2t", [D, H_SH], F32, kind="ExternalInput").ap()
    w3st_d = nc.dram_tensor("w3st", [H_SH, D], F32, kind="ExternalInput").ap()
    w12t_d = nc.dram_tensor("w12t", [NEXP, D, 2 * RH], F32, kind="ExternalInput").ap()
    w3t_d = nc.dram_tensor("w3t", [NEXP, RH, D], F32, kind="ExternalInput").ap()
    rwt_d = nc.dram_tensor("rwt", [D, NEXP], F32, kind="ExternalInput").ap()
    eb16_d = nc.dram_tensor("eb16", [1, NT * NEXP], F32, kind="ExternalInput").ap()
    ones_d = nc.dram_tensor("onesc", [P, P], F32, kind="ExternalInput").ap()
    idn_d = nc.dram_tensor("idnc", [P, P], F32, kind="ExternalInput").ap()
    x2s_d = nc.dram_tensor("x2s", [T, D], F32, kind="Internal").ap()
    gates_hbm = nc.dram_tensor("gates_hbm", [NEXP, T], F32, kind="Internal").ap()
    out_d = nc.dram_tensor("out", [T, D], F32, kind="ExternalOutput").ap()

    dbg = {}
    if debug:
        for name, shape in [("dbg_n1", [T, D]), ("dbg_n2fm", [D, T]),
                            ("dbg_gates", [P, NT * NEXP]), ("dbg_acc", [PASS_T, D]),
                            ("dbg_negb", [1, T]), ("dbg_x2", [T, D])]:
            dbg[name] = nc.dram_tensor(name, shape, F32, kind="ExternalOutput").ap()

    ARENA_B = 207 * KB
    _arena = nc.alloc_sbuf_tensor("arena0", [P, ARENA_B], mybir.dt.uint8)
    SB_BASE = nc.lookup_mloc(_arena).addr

    def sb_at(name, shape, dtype, off):
        assert 0 <= off < ARENA_B, (name, off)
        return nc.alloc_sbuf_tensor_at(name, shape, dtype, offset=SB_BASE + off)

    n1_tm = [sb_at(f"n1tm{i}", [P, D], BF16, i * 2 * KB) for i in range(NT)]
    n1_fm = [sb_at(f"n1fm{c}", [P, T], BF16, 32 * KB + c * (2 * T)) for c in range(NCD)]
    a_fm = [sb_at(f"afm{c}", [P, T], BF16, 64 * KB + c * (2 * T)) for c in range(NCD)]
    ptb = NT * QB * 2
    P_T = [sb_at(f"pt{b}", [P, NT * QB], BF16, 96 * KB + b * ptb) for b in range(2)]
    assert 2 * ptb <= 32 * KB
    n2_fm = [sb_at(f"n2fm{c}", [P, T], BF16, 128 * KB + c * (2 * T)) for c in range(NCD)]

    slotB_w12 = [sb_at(f"sBw12_{c}", [P, 2 * RH], BF16, c * 4 * KB) for c in range(NCD)]
    slotA_w12 = [sb_at(f"sAw12_{c}", [P, 2 * RH], BF16, 32 * KB + c * 4 * KB)
                 for c in range(NCD)]
    slotA_w3 = [sb_at(f"sAw3_{j}", [P, D], BF16, 64 * KB + j * 2 * KB) for j in range(NCD)]
    slotB_w3 = [sb_at(f"sBw3_{j}", [P, D], BF16, 80 * KB + j * 2 * KB) for j in range(NCD)]
    acc = [sb_at(f"acc{i}", [P, D], F32, 96 * KB + i * 4 * KB) for i in range(NACC)]
    assert NACC * 4 * KB <= 32 * KB

    t1 = 160 * KB
    x_r = [sb_at(f"xr{b}", [P, D], F32, t1 + b * 4 * KB) for b in range(2)]
    x2t = [sb_at(f"x2t{b}", [P, D], F32, t1 + 8 * KB + b * 4 * KB) for b in range(2)]
    y1pre = sb_at("y1p", [P, D], F32, t1 + 16 * KB)
    n2tm = [sb_at(f"n2tm{b}", [P, D], BF16, t1 + 20 * KB + b * 2 * KB)
            for b in range(3)]
    s1t = [sb_at(f"s1t{b}", [P, TBLK], F32, t1 + b * KB) for b in range(3)]
    t2t = [sb_at(f"t2t{b}", [P, TBLK], F32, t1 + 3 * KB + b * KB) for b in range(2)]
    Gt = [sb_at(f"Gt{b}", [P, TBLK], BF16, t1 + 5 * KB + b * TBLK * 2) for b in range(4)]
    gate_b = [sb_at(f"gb{b}", [P, TBLK], F32, t1 + 7 * KB + b * KB) for b in range(2)]
    scores_fm = sb_at("scores_fm", [NEXP, T], F32, t1 + 9 * KB)
    gates_fm = sb_at("gates_fm", [NEXP, T], F32, t1 + 9 * KB)
    x2r = [sb_at(f"x2r{b}", [P, D], F32, t1 + 9 * KB + b * 4 * KB) for b in range(2)]
    out_sb = [sb_at(f"osb{b}", [P, D], F32, t1 + 17 * KB + b * 4 * KB) for b in range(2)]

    pp = 186 * KB
    def sb_seq(name, shape, dtype):
        nonlocal pp
        t = sb_at(name, shape, dtype, pp)
        fb = int(np.prod(shape[1:])) * mybir.dt.size(dtype)
        pp += (fb + 63) // 64 * 64
        return t

    epsv = sb_seq("epsv", [P, 1], F32)
    tinyv = sb_seq("tinyv", [P, 1], F32)
    onesf = sb_seq("onesf", [P, P], F32)
    onesb = sb_seq("onesb", [P, P], BF16)
    idnb = sb_seq("idnb", [P, P], BF16)
    idnf = sb_seq("idnf", [P, P], F32)
    wlv = sb_seq("wlv", [P, NCD], F32)
    wrv = sb_seq("wrv", [P, NCD], F32)
    wprod = sb_seq("wprod", [P, NCD], F32)
    rwt_sb = [sb_seq(f"rwt{c}", [P, NEXP], BF16) for c in range(NCD)]
    eb16 = sb_seq("eb16", [1, NT * NEXP], F32)
    bias_rep = sb_seq("bias_rep", [P, NT * NEXP], F32)
    scores_tm = sb_seq("scores_tm", [P, NT * NEXP], F32)
    v_all = sb_seq("v_all", [P, NT * NEXP], F32)
    e1_all = sb_seq("e1_all", [P, NT * NEXP], F32)
    v2_all = sb_seq("v2_all", [P, NT * NEXP], F32)
    e2_all = sb_seq("e2_all", [P, NT * NEXP], F32)
    gates_all = sb_seq("gates_all", [P, NT * NEXP], F32)
    negb_row = sb_seq("negb", [1, T], BF16)
    grow = [sb_seq(f"grow{b}", [1, TBLK], F32) for b in range(2)]
    m1_all = sb_seq("m1_all", [P, NT], F32)
    m2_all = sb_seq("m2_all", [P, NT], F32)
    sqscr = sb_seq("sqscr", [P, D], BF16)
    scratch_row = sb_seq("scratch_row", [1, 512], F32)
    stats = {}
    for nm in ("s_ln1", "ss_ln1", "m_ln1", "r_ln1", "s_lna", "ss_lna", "m_lna",
               "r_lna", "s_ln2", "ss_ln2", "m_ln2", "r_ln2", "s_lnf", "ss_lnf",
               "m_lnf", "r_lnf", "tmpa", "tmpb", "tmpc", "den_r"):
        stats[nm] = sb_seq(nm, [P, NT], F32)
    assert pp <= 207 * KB - 64, f"persist overflow {pp}"

    class Ps:
        def __init__(self):
            self.gs = []

        def alloc(self, name, shape, dtype=F32):
            g = nc.psum_tensor(name, shape, dtype)
            h = g.__enter__()
            self.gs.append(g)
            return h

        def close(self):
            for g in reversed(self.gs):
                g.__exit__(None, None, None)
            self.gs = []

    phB = Ps()
    ntrB = [phB.alloc(f"ntrB{b}", [P, P], BF16) for b in range(2)]
    browps = [phB.alloc(f"browp{i}", [1, min(512, T)], F32)
              for i in range(max(T // 512, 1))]
    phB.close()

    phC = Ps()
    stp = [phC.alloc(f"stp{b}", [P, QB], F32) for b in range(2)]
    aop = phC.alloc("aop", [P, D], F32)
    denp = phC.alloc("denp", [P, 1], F32)
    ntrC = [phC.alloc(f"ntrC{b}", [P, P], BF16) for b in range(2)]
    phC.close()

    phD0 = Ps()
    scoresp = phD0.alloc("scoresp", [NEXP, min(512, T)], F32)
    phD0.close()

    phD = Ps()
    outp = [phD.alloc(f"outp{i}", [P, D], F32) for i in range(TT_PER_TB)]
    h1p = [phD.alloc(f"h1p{b}", [P, 2 * TBLK], F32) for b in range(3)]
    gatesp = phD.alloc("gatesp", [P, TBLK], F32)
    d0b_ps = gatesp
    phD.close()

    sem_guards = []

    def mksem(name):
        g = nc.semaphore(name)
        h = g.__enter__()
        sem_guards.append(g)
        return h

    names = ("s_c s_wA s_wB s_fA s_fB s_st s_exp s_ao s_rec s_ep s_tr "
             "s_trd s_v s_sc s_x2w s_s1 s_ev s_ev2 s_s2 "
             "s_acc s_gb s_gbe s_ot s_dbg s_cg s_gh "
             "s_x0 s_x1 s_xr0 s_xr1 s_sp0 s_sp1 s_x2r0 s_x2r1 "
             "s_out0 s_out1 s_gr0 s_gr1").split()
    S = {n: mksem(n) for n in names}
    s_w = [S["s_wA"], S["s_wB"]]
    s_f = [S["s_fA"], S["s_fB"]]
    s_xP = [S["s_x0"], S["s_x1"]]
    s_xrP = [S["s_xr0"], S["s_xr1"]]
    s_spP = [S["s_sp0"], S["s_sp1"]]
    s_x2rP = [S["s_x2r0"], S["s_x2r1"]]
    s_outP = [S["s_out0"], S["s_out1"]]
    s_grP = [S["s_gr0"], S["s_gr1"]]
    cnt = {n: 0 for n in names}
    _sem_name = {id(S[n]): n for n in names}

    def inc(s, n=1):
        cnt[_sem_name[id(s)]] += n
        return cnt[_sem_name[id(s)]]

    def cur(s):
        return cnt[_sem_name[id(s)]]

    ops = {"sync": [], "gpsimd": [], "tensor": [], "scalar": [], "vector": []}

    def E(eng, fn, drain=False):
        ops[eng].append((fn, drain))

    def W(eng, s, v):
        E(eng, lambda e, s=s, v=v: e.wait_ge(s, v))

    def dma(eng, out_ap, in_ap, s):
        v = inc(s, 16)
        E(eng, lambda e, o=out_ap, i=in_ap, ss=s: e.dma_start(out=o, in_=i).then_inc(ss, 16))
        return v

    x_tiled = x_d.rearrange("(n p) d -> n p d", p=P)
    x2s_tiled = x2s_d.rearrange("(n p) d -> n p d", p=P)
    out_tiled = out_d.rearrange("(n p) d -> n p d", p=P)

    TBL_LNEXP = _act_table_idx(nc, "natural_log_exp_and_others")
    TBL_SIG = _act_table_idx(nc, "sigmoid_and_others")
    TBL_SILU = _act_table_idx(nc, "silu_and_others")

    def load_table(idx):
        E("scalar", lambda e, idx=idx: e.add_instruction(
            mybir.InstLoadActFuncSet(name=nc.get_next_instruction_name(),
                                     act_func_set_id=idx, ins=[], outs=[])))

    load_table(TBL_LNEXP)
    dma("sync", onesf[:], ones_d[:, :], S["s_c"])
    dma("sync", idnf[:], idn_d[:, :], S["s_c"])
    dma("sync", wlv[:], wl_d[:, :], S["s_c"])
    dma("sync", wrv[:], wr_d[:, :], S["s_c"])
    dma("sync", eb16[:], eb16_d[:, :], S["s_c"])
    dma("gpsimd", onesb[:], ones_d[:, :], S["s_cg"])
    dma("gpsimd", idnb[:], idn_d[:, :], S["s_cg"])
    for c in range(NCD):
        dma("gpsimd", rwt_sb[c][:], rwt_d[c * P:(c + 1) * P, :], S["s_cg"])
    c_all = cur(S["s_c"])
    cg_all = cur(S["s_cg"])

    E("vector", lambda e: e.memset(epsv[:], float(EPS)))
    mv = inc(S["s_v"])
    E("vector", lambda e, v=mv: e.memset(tinyv[:], 1e-30).then_inc(S["s_v"], 1))
    eps_ready = mv
    W("scalar", S["s_v"], eps_ready)
    W("vector", S["s_c"], c_all)
    W("vector", S["s_cg"], cg_all)
    E("vector", lambda e: e.tensor_mul(wprod[:], wlv[:], wrv[:]), drain=True)

    def rsqrt_chain(var_ap, r_ap, t1_ap, t2_ap, ready_v):
        W("scalar", S["s_v"], ready_v)
        E("scalar", lambda e, a=var_ap, o=t1_ap: e.activation(o, a, ACT.Ln, bias=epsv[:]))
        sv = inc(S["s_sc"])
        E("scalar", lambda e, a=t1_ap, o=r_ap, v=sv: e.activation(
            o, a, ACT.Exp, scale=-0.5).then_inc(S["s_sc"], 1))
        W("vector", S["s_sc"], sv)

    def ln_stats(src_ap, i, pfx):
        s_, ss_, m_, r_ = (stats[f"s_{pfx}"], stats[f"ss_{pfx}"],
                           stats[f"m_{pfx}"], stats[f"r_{pfx}"])
        E("vector", lambda e, i=i: e.reduce_sum(s_[:, i:i + 1], src_ap, axis=AX), drain=True)
        sv = inc(S["s_sc"])
        E("scalar", lambda e, i=i, v=sv: e.activation(
            sqscr[:], src_ap, ACT.Square, accum_out=ss_[:, i:i + 1]).then_inc(S["s_sc"], 1))
        W("vector", S["s_sc"], sv)
        E("vector", lambda e, i=i: e.tensor_scalar(
            m_[:, i:i + 1], s_[:, i:i + 1], 1.0 / D, None, ALU.mult), drain=True)
        E("vector", lambda e, i=i: e.tensor_scalar(
            stats["tmpa"][:, i:i + 1], ss_[:, i:i + 1], 1.0 / D, None, ALU.mult), drain=True)
        E("vector", lambda e, i=i: e.tensor_mul(
            stats["tmpb"][:, i:i + 1], m_[:, i:i + 1], m_[:, i:i + 1]), drain=True)
        vv = inc(S["s_v"])
        E("vector", lambda e, i=i, v=vv: e.tensor_sub(
            stats["tmpa"][:, i:i + 1], stats["tmpa"][:, i:i + 1],
            stats["tmpb"][:, i:i + 1]).then_inc(S["s_v"], 1))
        rsqrt_chain(stats["tmpa"][:, i:i + 1], r_[:, i:i + 1],
                    stats["tmpb"][:, i:i + 1], stats["tmpc"][:, i:i + 1], vv)

    red_v = []
    sq_v = []
    for i in range(NT):
        if i >= 2:
            W("sync", S["s_v"], red_v[i - 2])
            W("sync", S["s_sc"], sq_v[i - 2])
        dma("sync", x_r[i % 2][:], x_tiled[i, :, :], s_xP[i % 2])
        W("vector", s_xP[i % 2], 16 * (i // 2 + 1))
        W("scalar", s_xP[i % 2], 16 * (i // 2 + 1))
        rv = inc(S["s_v"])
        E("vector", lambda e, i=i, v=rv: e.reduce_sum(
            stats["s_ln1"][:, i:i + 1], x_r[i % 2][:], axis=AX).then_inc(S["s_v"], 1),
          drain=True)
        red_v.append(rv)
        sv = inc(S["s_sc"])
        E("scalar", lambda e, i=i, v=sv: e.activation(
            sqscr[:], x_r[i % 2][:], ACT.Square,
            accum_out=stats["ss_ln1"][:, i:i + 1]).then_inc(S["s_sc"], 1))
        sq_v.append(sv)
    W("vector", S["s_sc"], sq_v[-1])
    E("vector", lambda e: e.tensor_scalar(
        stats["m_ln1"][:], stats["s_ln1"][:], 1.0 / D, None, ALU.mult), drain=True)
    E("vector", lambda e: e.tensor_scalar(
        stats["tmpa"][:], stats["ss_ln1"][:], 1.0 / D, None, ALU.mult), drain=True)
    E("vector", lambda e: e.tensor_mul(
        stats["tmpb"][:], stats["m_ln1"][:], stats["m_ln1"][:]), drain=True)
    bv1 = inc(S["s_v"])
    E("vector", lambda e, v=bv1: e.tensor_sub(
        stats["tmpa"][:], stats["tmpa"][:], stats["tmpb"][:]).then_inc(S["s_v"], 1))
    rsqrt_chain(stats["tmpa"][:], stats["r_ln1"][:], stats["tmpb"][:],
                stats["tmpc"][:], bv1)
    ln1_rdy = cur(S["s_sc"])

    apply_v = []
    tr_after_tile = []
    for i in range(NT):
        if i == 0:
            W("sync", S["s_v"], red_v[-1])
            W("sync", S["s_sc"], sq_v[-1])
        else:
            pass
        if i >= 2:
            W("sync", S["s_v"], apply_v[i - 2])
        dma("sync", x_r[i % 2][:], x_tiled[i, :, :], s_xP[i % 2])
        W("vector", s_xP[i % 2], 16 * (NT // 2 + i // 2 + 1))
        if i == 0:
            W("vector", S["s_sc"], ln1_rdy)
        av = inc(S["s_v"])
        E("vector", lambda e, i=i, v=av: e.tensor_scalar(
            n1_tm[i][:], x_r[i % 2][:], stats["m_ln1"][:, i:i + 1],
            stats["r_ln1"][:, i:i + 1], ALU.subtract, ALU.mult).then_inc(S["s_v"], 1))
        apply_v.append(av)
        W("tensor", S["s_v"], av)
        if i == 0:
            W("tensor", S["s_c"], c_all)
            W("tensor", S["s_cg"], cg_all)
        for c in range(NCD):
            k = i * NCD + c
            if k >= 2:
                W("tensor", S["s_trd"], k - 1)
            tv = inc(S["s_tr"])
            E("tensor", lambda e, i=i, c=c, k=k: e.transpose(
                ntrB[k % 2][:], n1_tm[i][:, c * P:(c + 1) * P], idnb[:]).then_inc(S["s_tr"], 1))
            W("vector", S["s_tr"], tv)
            dv = inc(S["s_trd"])
            E("vector", lambda e, i=i, c=c, k=k: e.tensor_copy(
                n1_fm[c][:, i * P:(i + 1) * P], ntrB[k % 2][:]).then_inc(S["s_trd"], 1))
        tr_after_tile.append(cur(S["s_trd"]))

    E("vector", lambda e: e.drain())
    afm_last = None
    for c in range(NCD):
        afm_last = inc(S["s_v"])
        E("vector", lambda e, c=c, v=afm_last: e.tensor_scalar(
            a_fm[c][:], n1_fm[c][:], wprod[:, c:c + 1], None,
            ALU.mult).then_inc(S["s_v"], 1), drain=True)

    sq_buf = P_T[0]
    NS5 = max(T // 512, 1)
    W5 = min(512, T)
    brow_base = cur(S["s_st"])
    for c in range(NCD):
        if c >= 1:
            W("vector", S["s_st"], brow_base + c)
        sqv = inc(S["s_v"])
        E("vector", lambda e, c=c, v=sqv: e.tensor_mul(
            sq_buf[:, :T], a_fm[c][:], a_fm[c][:]).then_inc(S["s_v"], 1), drain=True)
        W("tensor", S["s_v"], sqv)
        for s5 in range(NS5):
            last = (c == NCD - 1)
            stv = inc(S["s_st"]) if s5 == NS5 - 1 else None
            def mk(c=c, s5=s5, last=last, fin=stv is not None):
                def f(e):
                    inst = e.matmul(browps[s5][:, :W5], onesb[:, 0:1],
                                    sq_buf[:, s5 * 512:s5 * 512 + W5],
                                    start=(c == 0), stop=last, skip_group_check=True)
                    if fin:
                        inst.then_inc(S["s_st"], 1)
                return f
            E("tensor", mk())
    brow_done = cur(S["s_st"])
    W("scalar", S["s_st"], brow_done)
    negb_v = None
    for s5 in range(NS5):
        if negb_v is not None:
            W("scalar", S["s_v"], negb_v)
        E("scalar", lambda e, s5=s5: e.activation(
            scratch_row[:, :W5], browps[s5][:, :W5], ACT.Ln, bias=tinyv[0:1, :]))
        sv = inc(S["s_sc"])
        E("scalar", lambda e, s5=s5, v=sv: e.activation(
            scratch_row[:, :W5], scratch_row[:, :W5], ACT.Exp,
            scale=0.5).then_inc(S["s_sc"], 1))
        W("vector", S["s_sc"], sv)
        negb_v = inc(S["s_v"])
        E("vector", lambda e, s5=s5, v=negb_v: e.tensor_scalar(
            negb_row[:, s5 * 512:s5 * 512 + W5], scratch_row[:, :W5], -32.0, None,
            ALU.mult).then_inc(S["s_v"], 1))

    if debug:
        W("gpsimd", S["s_v"], negb_v)
        for i in range(NT):
            dma("gpsimd", dbg["dbg_n1"].rearrange("(n p) d -> n p d", p=P)[i, :, :],
                n1_tm[i][:], S["s_dbg"])

    exp_done_at_block = []
    gi_tr = []
    ao_done_v = None

    ntr_pending = []
    ntr_tr_done = {}
    ntr_kc = [0]

    def emit_ntr_one():
        gi, n2v = ntr_pending.pop(0)
        W("tensor", S["s_v"], n2v)
        for c in range(NCD):
            kc = ntr_kc[0]
            ntr_kc[0] += 1
            if kc >= 2:
                W("tensor", S["s_trd"], cur(S["s_trd"]) - 1)
            tv = inc(S["s_tr"])
            E("tensor", lambda e, gi=gi, c=c, kc=kc: e.transpose(
                ntrC[kc % 2][:], n2tm[gi % 3][:, c * P:(c + 1) * P],
                idnb[:]).then_inc(S["s_tr"], 1))
            W("vector", S["s_tr"], tv)
            dv = inc(S["s_trd"])
            E("vector", lambda e, gi=gi, c=c, kc=kc: e.tensor_copy(
                n2_fm[c][:, gi * P:(gi + 1) * P], ntrC[kc % 2][:]).then_inc(S["s_trd"], 1))
        ntr_tr_done[gi] = cur(S["s_tr"])
        gi_tr.append(cur(S["s_trd"]))

    def emit_ST(b):
        q0 = b * QB
        buf = P_T[b % 2]
        for it in range(NT):
            k = b * NT + it
            if b == 0 and it == 0:
                W("tensor", S["s_v"], afm_last)
            if k >= 2:
                W("tensor", S["s_exp"], k - 1)
            for c in range(NCD):
                E("tensor", lambda e, c=c, it=it, q0=q0, k=k: e.matmul(
                    stp[k % 2][:], a_fm[c][:, it * P:(it + 1) * P],
                    n1_fm[c][:, q0:q0 + QB], start=(c == 0), stop=False,
                    skip_group_check=True))
            if b == 0 and it == 0:
                W("tensor", S["s_v"], negb_v)
            stv = inc(S["s_st"])
            E("tensor", lambda e, q0=q0, k=k, v=stv: e.matmul(
                stp[k % 2][:], onesb[0:1, :], negb_row[:, q0:q0 + QB],
                start=False, stop=True, skip_group_check=True).then_inc(S["s_st"], 1))
            W("scalar", S["s_st"], stv)
            ev = inc(S["s_exp"])
            E("scalar", lambda e, it=it, k=k, buf=buf, v=ev: e.activation(
                buf[:, it * QB:(it + 1) * QB], stp[k % 2][:], ACT.Exp).then_inc(S["s_exp"], 1))

    def emit_AO(b):
        nonlocal ao_done_v
        buf = P_T[b % 2]
        exp_need = exp_done_at_block[b]
        for qt in range(QT_PER_B):
            gi = b * QT_PER_B + qt
            W("tensor", S["s_exp"], exp_need)
            if gi > 0:
                W("tensor", S["s_ep"], cur(S["s_ep"]))
                W("tensor", S["s_rec"], cur(S["s_rec"]))
            for it in range(NT):
                st0 = buf[:, it * QB + qt * P: it * QB + (qt + 1) * P]
                first = it == 0
                last = it == NT - 1
                E("tensor", lambda e, st0=st0, it=it, first=first, last=last: e.matmul(
                    aop[:, 0:512], st0, n1_tm[it][:, 0:512],
                    start=first, stop=last, skip_group_check=True))
                E("tensor", lambda e, st0=st0, it=it, first=first, last=last: e.matmul(
                    aop[:, 512:1024], st0, n1_tm[it][:, 512:1024],
                    start=first, stop=last, skip_group_check=True))
                aov = inc(S["s_ao"]) if last else None
                def mkd(st0=st0, it=it, first=first, last=last, fin=aov is not None):
                    def f(e):
                        inst = e.matmul(denp[:], st0, onesb[:, 0:1],
                                        start=first, stop=last, skip_group_check=True)
                        if fin:
                            inst.then_inc(S["s_ao"], 1)
                    return f
                E("tensor", mkd())
            aov = cur(S["s_ao"])
            ao_done_v = aov
            if len(ntr_pending) >= 2:
                emit_ntr_one()
            W("vector", S["s_ao"], aov)
            rv = inc(S["s_rec"])
            E("vector", lambda e, gi=gi, v=rv: e.reciprocal(
                stats["den_r"][:, gi % NT:gi % NT + 1], denp[:]).then_inc(S["s_rec"], 1), drain=True)
            epv = inc(S["s_ep"])
            E("vector", lambda e, gi=gi, v=epv: e.tensor_scalar(
                y1pre[:], aop[:], stats["den_r"][:, gi % NT:gi % NT + 1], None,
                ALU.mult).then_inc(S["s_ep"], 1), drain=True)
            W("scalar", S["s_ep"], epv)
            ln_stats(y1pre[:], gi, "lna")
            E("vector", lambda e, gi=gi: e.tensor_scalar(
                y1pre[:], y1pre[:], stats["m_lna"][:, gi:gi + 1],
                stats["r_lna"][:, gi:gi + 1], ALU.subtract, ALU.mult), drain=True)
            W("vector", s_xrP[gi % 2], 16 * (gi // 2 + 1))
            if gi >= 2:
                W("vector", s_spP[gi % 2], 16 * (gi // 2))
            x2v = inc(S["s_x2w"])
            E("vector", lambda e, gi=gi, v=x2v: e.tensor_add(
                x2t[gi % 2][:], x_r[gi % 2][:], y1pre[:]).then_inc(S["s_x2w"], 1), drain=True)
            W("sync", S["s_x2w"], x2v)
            dma("sync", x2s_tiled[gi, :, :], x2t[gi % 2][:], s_spP[gi % 2])
            W("scalar", S["s_x2w"], x2v)
            ln_stats(x2t[gi % 2][:], gi, "ln2")
            if gi >= 3:
                W("vector", S["s_tr"], ntr_tr_done[gi - 3])
            n2v = inc(S["s_v"])
            E("vector", lambda e, gi=gi, v=n2v: e.tensor_scalar(
                n2tm[gi % 3][:], x2t[gi % 2][:], stats["m_ln2"][:, gi:gi + 1],
                stats["r_ln2"][:, gi:gi + 1], ALU.subtract, ALU.mult).then_inc(S["s_v"], 1))
            ntr_pending.append((gi, n2v))

    emit_ST(0)
    exp_done_at_block.append(cur(S["s_exp"]))
    for b in range(NQB):
        if b + 1 < NQB:
            emit_ST(b + 1)
            exp_done_at_block.append(cur(S["s_exp"]))
        for qt in range(QT_PER_B):
            gi = b * QT_PER_B + qt
            W("sync", S["s_v"], apply_v[-1])
            if gi >= 2:
                W("sync", S["s_x2w"], gi - 1)
            dma("sync", x_r[gi % 2][:], x_tiled[gi, :, :], s_xrP[gi % 2])
        emit_AO(b)

    while ntr_pending:
        emit_ntr_one()
    n2_all_v = cur(S["s_trd"])
    sp_all = (cur(S["s_sp0"]), cur(S["s_sp1"]))
    all_exp_done = exp_done_at_block[-1]

    if debug:
        W("gpsimd", S["s_trd"], n2_all_v)
        for c in range(NCD):
            dma("gpsimd", dbg["dbg_n2fm"].rearrange("(c p) t -> c p t", p=P)[c, :, :],
                n2_fm[c][:], S["s_dbg"])
        W("gpsimd", S["s_sp0"], sp_all[0])
        W("gpsimd", S["s_sp1"], sp_all[1])
        for i in range(NT):
            dma("gpsimd", dbg["dbg_x2"].rearrange("(n p) d -> n p d", p=P)[i, :, :],
                x2s_tiled[i, :, :], S["s_dbg"])

    load_table(TBL_SIG)
    for qb in range(NS5):
        if qb == 0:
            W("tensor", S["s_trd"], n2_all_v)
        for c in range(NCD):
            last = c == NCD - 1
            stv = inc(S["s_st"]) if last else None
            def mk(c=c, qb=qb, last=last, fin=stv is not None):
                def f(e):
                    inst = e.matmul(scoresp[:, :W5], rwt_sb[c][:],
                                    n2_fm[c][:, qb * 512:qb * 512 + W5],
                                    start=(c == 0), stop=last, skip_group_check=True)
                    if fin:
                        inst.then_inc(S["s_st"], 1)
                return f
            E("tensor", mk())
        stv = cur(S["s_st"])
        if qb == 0:
            W("scalar", S["s_sp0"], sp_all[0])
            W("scalar", S["s_sp1"], sp_all[1])
        W("scalar", S["s_st"], stv)
        sv = inc(S["s_sc"])
        E("scalar", lambda e, qb=qb, v=sv: e.activation(
            scores_fm[:, qb * 512:qb * 512 + W5], scoresp[:, :W5],
            ACT.Sigmoid).then_inc(S["s_sc"], 1))
        if qb + 1 < NS5:
            W("tensor", S["s_sc"], sv)
    sig_done = cur(S["s_sc"])

    d0b_q = []

    def Q(fn):
        d0b_q.append(fn)

    state = {}

    def _qi_bias():
        W("tensor", S["s_c"], c_all)
        W("tensor", S["s_cg"], cg_all)
        btv = inc(S["s_st"])
        E("tensor", lambda e, v=btv: e.matmul(
            d0b_ps[:, 0:NT * NEXP], onesf[0:1, :], eb16[:, :], start=True,
            stop=True, skip_group_check=True).then_inc(S["s_st"], 1))
        W("vector", S["s_st"], btv)
        bv = inc(S["s_v"])
        E("vector", lambda e, v=bv: e.tensor_copy(
            bias_rep[:], d0b_ps[:, 0:NT * NEXP]).then_inc(S["s_v"], 1), drain=True)
        state["bias_done"] = bv
    Q(_qi_bias)

    def _qi_str(i):
        def f():
            W("tensor", S["s_v"], state["bias_done"])
            W("tensor", S["s_trd"], cur(S["s_trd"]))
            if i == 0:
                W("tensor", S["s_sc"], sig_done)
            tv = inc(S["s_tr"])
            E("tensor", lambda e, i=i: e.transpose(
                d0b_ps[:, 0:NEXP].rearrange("p e -> p e") if False else d0b_ps[:, 0:NEXP],
                scores_fm[:, i * P:(i + 1) * P],
                idnf[0:NEXP, 0:NEXP]).then_inc(S["s_tr"], 1))
            W("vector", S["s_tr"], tv)
            inc(S["s_trd"])
            E("vector", lambda e, i=i: e.tensor_copy(
                scores_tm[:, i * NEXP:(i + 1) * NEXP],
                d0b_ps[:, 0:NEXP]).then_inc(S["s_trd"], 1))
        return f
    for i in range(NT):
        Q(_qi_str(i))

    def _qi_top2_head():
        W("vector", S["s_trd"], cur(S["s_trd"]))
        E("vector", lambda e: e.drain())
        E("vector", lambda e: e.tensor_add(v_all[:], scores_tm[:], bias_rep[:]),
          drain=True)
    Q(_qi_top2_head)

    def _qi_top2(i):
        def f():
            sl = slice(i * NEXP, (i + 1) * NEXP)
            E("vector", lambda e, sl=sl, i=i: e.reduce_max(
                m1_all[:, i:i + 1], v_all[:, sl], axis=AX), drain=True)
            E("vector", lambda e, sl=sl, i=i: e.tensor_scalar(
                e1_all[:, sl], v_all[:, sl], m1_all[:, i:i + 1], None,
                ALU.is_equal), drain=True)
            E("vector", lambda e, sl=sl: e.scalar_tensor_tensor(
                v2_all[:, sl], e1_all[:, sl], -1e30, v_all[:, sl], ALU.mult,
                ALU.add), drain=True)
            E("vector", lambda e, sl=sl, i=i: e.reduce_max(
                m2_all[:, i:i + 1], v2_all[:, sl], axis=AX), drain=True)
            E("vector", lambda e, sl=sl, i=i: e.tensor_scalar(
                e2_all[:, sl], v2_all[:, sl], m2_all[:, i:i + 1], None,
                ALU.is_equal), drain=True)
        return f
    for i in range(NT):
        Q(_qi_top2(i))

    def _qi_gates():
        E("vector", lambda e: e.tensor_add(e1_all[:], e1_all[:], e2_all[:]),
          drain=True)
        gv = inc(S["s_v"])
        E("vector", lambda e, v=gv: e.tensor_mul(
            gates_all[:], e1_all[:], scores_tm[:]).then_inc(S["s_v"], 1),
          drain=True)
        state["gv"] = gv
        if debug:
            W("gpsimd", S["s_v"], gv)
            dma("gpsimd", dbg["dbg_gates"][:, :], gates_all[:], S["s_dbg"])
    Q(_qi_gates)

    def _qi_gtr(i):
        def f():
            W("tensor", S["s_v"], state["gv"])
            W("tensor", S["s_trd"], cur(S["s_trd"]))
            tv = inc(S["s_tr"])
            E("tensor", lambda e, i=i: e.transpose(
                d0b_ps[0:NEXP, 0:P], gates_all[:, i * NEXP:(i + 1) * NEXP],
                idnf[:]).then_inc(S["s_tr"], 1))
            W("vector", S["s_tr"], tv)
            inc(S["s_trd"])
            E("vector", lambda e, i=i: e.tensor_copy(
                gates_fm[:, i * P:(i + 1) * P],
                d0b_ps[0:NEXP, 0:P]).then_inc(S["s_trd"], 1))
        return f
    for i in range(NT):
        Q(_qi_gtr(i))

    def _qi_bounce():
        W("sync", S["s_trd"], cur(S["s_trd"]))
        dma("sync", gates_hbm[:, :], gates_fm[:], S["s_gh"])
    Q(_qi_bounce)

    load_table(TBL_SILU)
    base_chunks = [("shared", k) for k in range(4)] + [("expert", e) for e in range(NEXP)]
    chunks = [(p, kind, idx) for p in range(NPASS) for (kind, idx) in base_chunks]
    NCHUNK_P = len(base_chunks)

    chunk_w_target = [None] * len(chunks)

    def emit_stream(pss0):
        if pss0 == 0:
            W("gpsimd", S["s_exp"], all_exp_done)
        for ci in range(pss0 * NCHUNK_P, (pss0 + 1) * NCHUNK_P):
            _, kind, idx = chunks[ci]
            sl = ci % 2
            if ci == 1:
                W("gpsimd", S["s_ao"], ao_done_v)
                if debug:
                    W("gpsimd", S["s_dbg"], cur(S["s_dbg"]))
            if ci >= 2:
                W("gpsimd", S["s_s2"], (ci - 1) * NTB * NCD * TT_PER_TB)
            w12_t = slotA_w12 if sl == 0 else slotB_w12
            w3_t = slotA_w3 if sl == 0 else slotB_w3
            if kind == "shared":
                k = idx
                for c in range(NCD):
                    dma("gpsimd", w12_t[c][:, 0:RH],
                        w1t_d[c * P:(c + 1) * P, k * RH:(k + 1) * RH], s_w[sl])
                    dma("gpsimd", w12_t[c][:, RH:2 * RH],
                        w2t_d[c * P:(c + 1) * P, k * RH:(k + 1) * RH], s_w[sl])
                for j in range(NCD):
                    dma("gpsimd", w3_t[j][:],
                        w3st_d[k * RH + j * P: k * RH + (j + 1) * P, :], s_w[sl])
            else:
                ei = idx
                for c in range(NCD):
                    dma("gpsimd", w12_t[c][:], w12t_d[ei, c * P:(c + 1) * P, :], s_w[sl])
                for j in range(NCD):
                    dma("gpsimd", w3_t[j][:], w3t_d[ei, j * P:(j + 1) * P, :], s_w[sl])
            chunk_w_target[ci] = cur(s_w[sl])

    jj = 0
    gate_load_cnt = [0]
    acc_done_pass = []

    def emit_pass(pss, mid_hook=None):
        nonlocal jj
        S2_DEPTH = 3
        s2_pending = []

        def emit_s2_fix(p):
            (jj_, j_, w3t_, first_acc_wait, _unused_v, s2v, accs) = p
            if first_acc_wait:
                W("tensor", S["s_acc"], cur(S["s_acc"]))
            for tt in range(TT_PER_TB):
                for half in range(2):
                    lastj = j_ == NCD - 1
                    fin = half == 1
                    def mk2(tt=tt, half=half, jj_=jj_, j_=j_, w3t_=w3t_,
                            lastj=lastj, fin=fin, s2v=s2v):
                        def f(e):
                            inst = e.matmul(
                                outp[tt][:, half * 512:(half + 1) * 512],
                                Gt[jj_ % 4][:, tt * P:(tt + 1) * P],
                                w3t_[j_][:, half * 512:(half + 1) * 512],
                                start=(j_ == 0), stop=lastj, skip_group_check=True)
                            if fin:
                                inst.then_inc(S["s_s2"], 1)
                        return f
                    E("tensor", mk2())
            if accs is not None:
                ci_, tb_ = accs
                W("vector", S["s_s2"], s2v)
                for tt in range(TT_PER_TB):
                    ai = tb_ * TT_PER_TB + tt
                    av = inc(S["s_acc"])
                    if ci_ % NCHUNK_P == 0:
                        E("vector", lambda e, tt=tt, ai=ai, v=av: e.tensor_copy(
                            acc[ai][:], outp[tt][:]).then_inc(S["s_acc"], 1), drain=True)
                    else:
                        E("vector", lambda e, tt=tt, ai=ai, v=av: e.tensor_add(
                            acc[ai][:], acc[ai][:], outp[tt][:]).then_inc(S["s_acc"], 1), drain=True)
                    if ci_ % NCHUNK_P == NCHUNK_P - 1:
                        acc_tile_ready[(ci_ // NCHUNK_P, ai)] = av
                        emit_stats_pre(ci_ // NCHUNK_P, ai)

        for ck in range(NCHUNK_P):
            ci = pss * NCHUNK_P + ck
            _, kind, idx = chunks[ci]
            sl = ci % 2
            w12_t = slotA_w12 if sl == 0 else slotB_w12
            w3_t = slotA_w3 if sl == 0 else slotB_w3
            has_g = kind == "expert"
            if has_g:
                while d0b_q:
                    d0b_q.pop(0)()
            W("tensor", s_w[sl], chunk_w_target[ci])
            if ci == 0:
                W("tensor", S["s_sc"], sig_done)
            for tb in range(NTB):
                if mid_hook is not None and ck == 0:
                    mid_hook(tb)
                tok0 = pss * PASS_T + tb * TBLK
                if has_g:
                    gk = gate_load_cnt[0]
                    gate_load_cnt[0] += 1
                    if gk == 0:
                        W("sync", S["s_gh"], 16)
                        W("sync", S["s_sp0"], sp_all[0])
                        W("sync", S["s_sp1"], sp_all[1])
                        W("sync", S["s_x2w"], cur(S["s_x2w"]))
                    if gk >= 2:
                        W("sync", S["s_gb"], gk - 1)
                    grv = dma("sync", grow[gk % 2][:],
                              gates_hbm[idx:idx + 1, tok0:tok0 + TBLK], s_grP[gk % 2])
                    W("tensor", s_grP[gk % 2], grv)
                    if cur(S["s_gb"]) > 0:
                        W("tensor", S["s_gbe"], cur(S["s_gbe"]))
                    gbv = inc(S["s_gb"])
                    E("tensor", lambda e, gk=gk, v=gbv: e.matmul(
                        gatesp[:], onesf[0:1, :], grow[gk % 2][:],
                        start=True, stop=True, skip_group_check=True).then_inc(S["s_gb"], 1))
                    W("vector", S["s_gb"], gbv)
                    gbev = inc(S["s_gbe"])
                    E("vector", lambda e, tb=tb, v=gbev: e.tensor_copy(
                        gate_b[tb % 2][:], gatesp[:]).then_inc(S["s_gbe"], 1), drain=True)
                for j in range(NCD):
                    if d0b_q:
                        d0b_q.pop(0)()
                    if d0b_q:
                        d0b_q.pop(0)()
                    if jj >= 3:
                        W("tensor", S["s_ev2"], jj - 2)
                    for c in range(NCD):
                        E("tensor", lambda e, c=c, j=j, jj=jj, tok0=tok0, w12_t=w12_t: e.matmul(
                            h1p[jj % 3][:, 0:TBLK], w12_t[c][:, j * P:(j + 1) * P],
                            n2_fm[c][:, tok0:tok0 + TBLK], start=(c == 0), stop=False,
                            skip_group_check=True))
                    for c in range(NCD):
                        last = c == NCD - 1
                        s1v = inc(S["s_s1"]) if last else None
                        def mk1(c=c, j=j, jj=jj, tok0=tok0, w12_t=w12_t, last=last,
                                fin=s1v is not None):
                            def f(e):
                                inst = e.matmul(
                                    h1p[jj % 3][:, TBLK:2 * TBLK],
                                    w12_t[c][:, RH + j * P:RH + (j + 1) * P],
                                    n2_fm[c][:, tok0:tok0 + TBLK],
                                    start=False, stop=last, skip_group_check=True)
                                if fin:
                                    inst.then_inc(S["s_s1"], 1)
                            return f
                        E("tensor", mk1())
                    s1v = cur(S["s_s1"])
                    W("scalar", S["s_s1"], s1v)
                    evv = inc(S["s_ev"])
                    E("scalar", lambda e, jj=jj, v=evv: e.activation(
                        s1t[jj % 3][:], h1p[jj % 3][:, 0:TBLK],
                        ACT.Silu).then_inc(S["s_ev"], 1))
                    W("vector", S["s_ev"], evv)
                    if jj >= 4:
                        W("vector", S["s_s2"], (jj - 3) * TT_PER_TB)
                    if has_g:
                        W("vector", S["s_gbe"], cur(S["s_gbe"]))
                        E("vector", lambda e, jj=jj, tb=tb: e.tensor_mul(
                            t2t[jj % 2][:], h1p[jj % 3][:, TBLK:2 * TBLK],
                            gate_b[tb % 2][:]), drain=True)
                        ev2 = inc(S["s_ev2"])
                        E("vector", lambda e, jj=jj, v=ev2: e.tensor_mul(
                            Gt[jj % 4][:], s1t[jj % 3][:],
                            t2t[jj % 2][:]).then_inc(S["s_ev2"], 1), drain=True)
                    else:
                        ev2 = inc(S["s_ev2"])
                        E("vector", lambda e, jj=jj, v=ev2: e.tensor_mul(
                            Gt[jj % 4][:], s1t[jj % 3][:],
                            h1p[jj % 3][:, TBLK:2 * TBLK]).then_inc(S["s_ev2"], 1), drain=True)
                    s2v = inc(S["s_s2"], TT_PER_TB)
                    first_acc_wait = (j == 0) and not (ci == 0 and tb == 0)
                    accs = (ci, tb) if j == NCD - 1 else None
                    this = (jj, j, w3_t, first_acc_wait, None, s2v, accs)
                    s2_pending.append((this, ev2))
                    if len(s2_pending) > S2_DEPTH:
                        p, pev2 = s2_pending.pop(0)
                        emit_s2_fix(p)
                    jj += 1
        while s2_pending:
            p, pev2 = s2_pending.pop(0)
            W("tensor", S["s_ev2"], pev2)
            emit_s2_fix(p)

    acc_tile_ready = {}
    stats_pre_done = set()

    def emit_stats_pre(pss, ii):
        si = pss * NACC + ii
        stats_pre_done.add((pss, ii))
        rdy_v = acc_tile_ready[(pss, ii)]
        W("vector", S["s_acc"], rdy_v)
        W("scalar", S["s_acc"], rdy_v)
        E("vector", lambda e, ii=ii, si=si: e.reduce_sum(
            stats["s_lnf"][:, si:si + 1], acc[ii][:], axis=AX), drain=True)
        sv = inc(S["s_sc"])
        E("scalar", lambda e, ii=ii, si=si, v=sv: e.activation(
            sqscr[:], acc[ii][:], ACT.Square,
            accum_out=stats["ss_lnf"][:, si:si + 1]).then_inc(S["s_sc"], 1))

    def emit_phase_e_stats(pss):
        load_table(TBL_LNEXP)
        if pss == 0:
            W("vector", S["s_sp0"], sp_all[0])
            W("vector", S["s_sp1"], sp_all[1])
        if debug and pss == 0:
            W("gpsimd", S["s_acc"], acc_done_pass[0])
            for i in range(NACC):
                dma("gpsimd", dbg["dbg_acc"].rearrange("(n p) d -> n p d", p=P)[i, :, :],
                    acc[i][:], S["s_dbg"])
        c0 = pss * NACC
        for ii in range(NACC):
            if (pss, ii) not in stats_pre_done:
                emit_stats_pre(pss, ii)
        W("vector", S["s_sc"], cur(S["s_sc"]))
        sl = slice(c0, c0 + NACC)
        E("vector", lambda e, sl=sl: e.tensor_scalar(
            stats["m_lnf"][:, sl], stats["s_lnf"][:, sl], 1.0 / D, None, ALU.mult),
          drain=True)
        E("vector", lambda e, sl=sl: e.tensor_scalar(
            stats["tmpa"][:, sl], stats["ss_lnf"][:, sl], 1.0 / D, None, ALU.mult),
          drain=True)
        E("vector", lambda e, sl=sl: e.tensor_mul(
            stats["tmpb"][:, sl], stats["m_lnf"][:, sl], stats["m_lnf"][:, sl]),
          drain=True)
        bv = inc(S["s_v"])
        E("vector", lambda e, sl=sl, v=bv: e.tensor_sub(
            stats["tmpa"][:, sl], stats["tmpa"][:, sl],
            stats["tmpb"][:, sl]).then_inc(S["s_v"], 1))
        rsqrt_chain(stats["tmpa"][:, sl], stats["r_lnf"][:, sl],
                    stats["tmpb"][:, sl], stats["tmpc"][:, sl], bv)
        return cur(S["s_sc"])

    def emit_phase_e_tile(pss, ii, rdy):
        if True:
            gi = pss * NACC + ii
            W("sync", S["s_sp0"], sp_all[0])
            W("sync", S["s_sp1"], sp_all[1])
            W("sync", S["s_acc"], acc_done_pass[pss])
            if gi >= 2:
                W("sync", s_outP[gi % 2], 16 * (gi // 2))
            dma("sync", x2r[gi % 2][:], x2s_tiled[gi, :, :], s_x2rP[gi % 2])
            si = gi
            W("vector", S["s_sc"], rdy)
            if gi >= 2:
                W("vector", s_outP[gi % 2], 16 * (gi // 2))
            E("vector", lambda e, ii=ii, si=si, gi=gi: e.tensor_scalar(
                out_sb[gi % 2][:], acc[ii][:], stats["m_lnf"][:, si:si + 1],
                stats["r_lnf"][:, si:si + 1], ALU.subtract, ALU.mult), drain=True)
            W("vector", s_x2rP[gi % 2], 16 * (gi // 2 + 1))
            otv = inc(S["s_ot"])
            E("vector", lambda e, gi=gi, v=otv: e.tensor_add(
                out_sb[gi % 2][:], out_sb[gi % 2][:], x2r[gi % 2][:]).then_inc(S["s_ot"], 1), drain=True)
            W("sync", S["s_ot"], otv)
            dma("sync", out_tiled[gi, :, :], out_sb[gi % 2][:], s_outP[gi % 2])

    def emit_phase_e(pss):
        rdy = emit_phase_e_stats(pss)
        for ii in range(NACC):
            emit_phase_e_tile(pss, ii, rdy)

    for pss in range(NPASS):
        if pss > 0 and debug:
            W("vector", S["s_dbg"], cur(S["s_dbg"]))
        emit_stream(pss)
        if pss > 0 and not DISABLE_HOOK:
            prev = pss - 1
            rdy = emit_phase_e_stats(prev)
            load_table(TBL_SILU)

            def hook(tb, prev=prev, rdy=rdy):
                for ii in range(tb * TT_PER_TB, (tb + 1) * TT_PER_TB):
                    emit_phase_e_tile(prev, ii, rdy)
            emit_pass(pss, mid_hook=hook)
        elif pss > 0:
            emit_phase_e(pss - 1)
            load_table(TBL_SILU)
            emit_pass(pss)
        else:
            emit_pass(pss)
        acc_done_pass.append(cur(S["s_acc"]))
        if pss == NPASS - 1:
            emit_phase_e(pss)

    W("sync", S["s_out0"], 16 * ((NT + 1) // 2))
    W("sync", S["s_out1"], 16 * (NT // 2))
    if debug:
        W("gpsimd", S["s_dbg"], cur(S["s_dbg"]))

    with nc.Block() as block:
        @block.sync
        def _(eng):
            for f, dr in ops["sync"]:
                f(eng)

        @block.gpsimd
        def _(eng):
            for f, dr in ops["gpsimd"]:
                f(eng)

        @block.tensor
        def _(eng):
            for f, dr in ops["tensor"]:
                f(eng)

        @block.scalar
        def _(eng):
            for f, dr in ops["scalar"]:
                f(eng)
                eng.drain()

        @block.vector
        def _(eng):
            for f, dr in ops["vector"]:
                f(eng)
                if dr:
                    eng.drain()

    for g in reversed(sem_guards):
        g.__exit__(None, None, None)
    return nc


def _prep_inputs(L0_attn_x, L0_attn_w_lhs, L0_attn_w_rhs, L0_ffn_w12, L0_ffn_w3,
                 L0_ffn_shared_w1, L0_ffn_shared_w2, L0_ffn_shared_w3,
                 L0_ffn_router_w, L0_ffn_expert_bias, NT=16):
    T = NT * P
    x = np.ascontiguousarray(np.asarray(L0_attn_x, dtype=np.float32))
    w1t = np.ascontiguousarray(np.asarray(L0_ffn_shared_w1, np.float32).T)
    w2t = np.ascontiguousarray(np.asarray(L0_ffn_shared_w2, np.float32).T)
    w3st = np.ascontiguousarray(np.asarray(L0_ffn_shared_w3, np.float32).T)
    w12t = np.ascontiguousarray(np.asarray(L0_ffn_w12, np.float32).transpose(0, 2, 1))
    w3t = np.ascontiguousarray(np.asarray(L0_ffn_w3, np.float32).transpose(0, 2, 1))
    rwt = np.ascontiguousarray(np.asarray(L0_ffn_router_w, np.float32).T)
    wl = np.ascontiguousarray(np.asarray(L0_attn_w_lhs, np.float32).reshape(NCD, P).T)
    wr = np.ascontiguousarray(np.asarray(L0_attn_w_rhs, np.float32).reshape(NCD, P).T)
    eb = np.asarray(L0_ffn_expert_bias, np.float32).reshape(1, NEXP)
    eb16 = np.ascontiguousarray(np.tile(eb, (1, NT)))
    ones = np.ones((P, P), dtype=np.float32)
    idn = np.eye(P, dtype=np.float32)
    in_maps = []
    for i in range(x.shape[0]):
        in_maps.append({
            "x": np.ascontiguousarray(x[i, :T]),
            "wl": wl, "wr": wr, "w1t": w1t, "w2t": w2t, "w3st": w3st,
            "w12t": w12t, "w3t": w3t, "rwt": rwt, "eb16": eb16,
            "onesc": ones, "idnc": idn,
        })
    return in_maps


def kernel(L0_attn_x, L0_attn_w_lhs, L0_attn_w_rhs, L0_ffn_w12, L0_ffn_w3,
           L0_ffn_shared_w1, L0_ffn_shared_w2, L0_ffn_shared_w3,
           L0_ffn_router_w, L0_ffn_expert_bias):
    NT = 16
    in_maps = _prep_inputs(L0_attn_x, L0_attn_w_lhs, L0_attn_w_rhs, L0_ffn_w12,
                           L0_ffn_w3, L0_ffn_shared_w1, L0_ffn_shared_w2,
                           L0_ffn_shared_w3, L0_ffn_router_w,
                           L0_ffn_expert_bias, NT=NT)
    nc = build_program(NT=NT, debug=False)
    res = run_bass_kernel_spmd(nc, in_maps, core_ids=list(range(len(in_maps))))
    out = np.stack([np.asarray(r["out"], dtype=np.float32) for r in res.results])
    return out


# version 13
# speedup vs baseline: 1.0391x; 1.0391x over previous
import numpy as np
import concourse.bass as bass
import concourse.mybir as mybir
from concourse.bass_utils import run_bass_kernel_spmd

F32 = mybir.dt.float32
BF16 = mybir.dt.bfloat16
AX = mybir.AxisListType.X
ALU = mybir.AluOpType
ACT = mybir.ActivationFunctionType

P = 128
D = 1024
NCD = D // P
H_SH = 4096
NEXP = 8
RH = 1024
EPS = 1e-5
KB = 1024


def _act_table_idx(nc, set_name):
    from concourse.hw_specs import get_activation_tables
    for idx, name in enumerate(get_activation_tables(nc.m.arch).keys()):
        if name == set_name:
            return idx
    raise KeyError(set_name)


def build_program(NT=16, debug=False, pass_t=None, DISABLE_HOOK=False):
    T = NT * P
    QB = min(512, T)
    NQB = T // QB
    QT_PER_B = QB // P
    TBLK = min(256, T)
    PASS_T = pass_t if pass_t is not None else min(1024, T)
    NPASS = T // PASS_T
    NTB = PASS_T // TBLK
    TT_PER_TB = TBLK // P
    NACC = PASS_T // P

    nc = bass.Bass("TRN2", target_bir_lowering=False)

    x_d = nc.dram_tensor("x", [T, D], F32, kind="ExternalInput").ap()
    wl_d = nc.dram_tensor("wl", [P, NCD], F32, kind="ExternalInput").ap()
    wr_d = nc.dram_tensor("wr", [P, NCD], F32, kind="ExternalInput").ap()
    w1t_d = nc.dram_tensor("w1t", [D, H_SH], F32, kind="ExternalInput").ap()
    w2t_d = nc.dram_tensor("w2t", [D, H_SH], F32, kind="ExternalInput").ap()
    w3st_d = nc.dram_tensor("w3st", [H_SH, D], F32, kind="ExternalInput").ap()
    w12t_d = nc.dram_tensor("w12t", [NEXP, D, 2 * RH], F32, kind="ExternalInput").ap()
    w3t_d = nc.dram_tensor("w3t", [NEXP, RH, D], F32, kind="ExternalInput").ap()
    rwt_d = nc.dram_tensor("rwt", [D, NEXP], F32, kind="ExternalInput").ap()
    eb16_d = nc.dram_tensor("eb16", [1, NT * NEXP], F32, kind="ExternalInput").ap()
    ones_d = nc.dram_tensor("onesc", [P, P], F32, kind="ExternalInput").ap()
    idn_d = nc.dram_tensor("idnc", [P, P], F32, kind="ExternalInput").ap()
    x2s_d = nc.dram_tensor("x2s", [T, D], F32, kind="Internal").ap()
    gates_hbm = nc.dram_tensor("gates_hbm", [NEXP, T], F32, kind="Internal").ap()
    out_d = nc.dram_tensor("out", [T, D], F32, kind="ExternalOutput").ap()

    dbg = {}
    if debug:
        for name, shape in [("dbg_n1", [T, D]), ("dbg_n2fm", [D, T]),
                            ("dbg_gates", [P, NT * NEXP]), ("dbg_acc", [PASS_T, D]),
                            ("dbg_negb", [1, T]), ("dbg_x2", [T, D])]:
            dbg[name] = nc.dram_tensor(name, shape, F32, kind="ExternalOutput").ap()

    ARENA_B = 207 * KB
    _arena = nc.alloc_sbuf_tensor("arena0", [P, ARENA_B], mybir.dt.uint8)
    SB_BASE = nc.lookup_mloc(_arena).addr

    def sb_at(name, shape, dtype, off):
        assert 0 <= off < ARENA_B, (name, off)
        return nc.alloc_sbuf_tensor_at(name, shape, dtype, offset=SB_BASE + off)

    n1_tm = [sb_at(f"n1tm{i}", [P, D], BF16, i * 2 * KB) for i in range(NT)]
    n1_fm = [sb_at(f"n1fm{c}", [P, T], BF16, 32 * KB + c * (2 * T)) for c in range(NCD)]
    a_fm = [sb_at(f"afm{c}", [P, T], BF16, 64 * KB + c * (2 * T)) for c in range(NCD)]
    ptb = NT * QB * 2
    P_T = [sb_at(f"pt{b}", [P, NT * QB], BF16, 96 * KB + b * ptb) for b in range(2)]
    assert 2 * ptb <= 32 * KB
    n2_fm = [sb_at(f"n2fm{c}", [P, T], BF16, 128 * KB + c * (2 * T)) for c in range(NCD)]

    slotB_w12 = [sb_at(f"sBw12_{c}", [P, 2 * RH], BF16, c * 4 * KB) for c in range(NCD)]
    slotA_w12 = [sb_at(f"sAw12_{c}", [P, 2 * RH], BF16, 32 * KB + c * 4 * KB)
                 for c in range(NCD)]
    slotA_w3 = [sb_at(f"sAw3_{j}", [P, D], BF16, 64 * KB + j * 2 * KB) for j in range(NCD)]
    slotB_w3 = [sb_at(f"sBw3_{j}", [P, D], BF16, 80 * KB + j * 2 * KB) for j in range(NCD)]
    acc = [sb_at(f"acc{i}", [P, D], F32, 96 * KB + i * 4 * KB) for i in range(NACC)]
    assert NACC * 4 * KB <= 32 * KB

    t1 = 160 * KB
    x_r = [sb_at(f"xr{b}", [P, D], F32, t1 + b * 4 * KB) for b in range(2)]
    x2t = [sb_at(f"x2t{b}", [P, D], F32, t1 + 8 * KB + b * 4 * KB) for b in range(2)]
    y1pre = sb_at("y1p", [P, D], F32, t1 + 16 * KB)
    n2tm = [sb_at(f"n2tm{b}", [P, D], BF16, t1 + 20 * KB + b * 2 * KB)
            for b in range(3)]
    s1t = [sb_at(f"s1t{b}", [P, TBLK], F32, t1 + b * KB) for b in range(3)]
    t2t = [sb_at(f"t2t{b}", [P, TBLK], F32, t1 + 3 * KB + b * KB) for b in range(2)]
    Gt = [sb_at(f"Gt{b}", [P, TBLK], BF16, t1 + 5 * KB + b * TBLK * 2) for b in range(4)]
    gate_b = [sb_at(f"gb{b}", [P, TBLK], F32, t1 + 7 * KB + b * KB) for b in range(2)]
    scores_fm = sb_at("scores_fm", [NEXP, T], F32, t1 + 9 * KB)
    gates_fm = sb_at("gates_fm", [NEXP, T], F32, t1 + 9 * KB)
    x2r = [sb_at(f"x2r{b}", [P, D], F32, t1 + 9 * KB + b * 4 * KB) for b in range(2)]
    out_sb = [sb_at(f"osb{b}", [P, D], F32, t1 + 17 * KB + b * 4 * KB) for b in range(2)]

    pp = 186 * KB
    def sb_seq(name, shape, dtype):
        nonlocal pp
        t = sb_at(name, shape, dtype, pp)
        fb = int(np.prod(shape[1:])) * mybir.dt.size(dtype)
        pp += (fb + 63) // 64 * 64
        return t

    epsv = sb_seq("epsv", [P, 1], F32)
    tinyv = sb_seq("tinyv", [P, 1], F32)
    onesf = sb_seq("onesf", [P, P], F32)
    onesb = sb_seq("onesb", [P, P], BF16)
    idnb = sb_seq("idnb", [P, P], BF16)
    idnf = sb_seq("idnf", [P, P], F32)
    wlv = sb_seq("wlv", [P, NCD], F32)
    wrv = sb_seq("wrv", [P, NCD], F32)
    wprod = sb_seq("wprod", [P, NCD], F32)
    rwt_sb = [sb_seq(f"rwt{c}", [P, NEXP], BF16) for c in range(NCD)]
    eb16 = sb_seq("eb16", [1, NT * NEXP], F32)
    bias_rep = sb_seq("bias_rep", [P, NT * NEXP], F32)
    scores_tm = sb_seq("scores_tm", [P, NT * NEXP], F32)
    v_all = sb_seq("v_all", [P, NT * NEXP], F32)
    e1_all = sb_seq("e1_all", [P, NT * NEXP], F32)
    v2_all = sb_seq("v2_all", [P, NT * NEXP], F32)
    e2_all = sb_seq("e2_all", [P, NT * NEXP], F32)
    gates_all = sb_seq("gates_all", [P, NT * NEXP], F32)
    negb_row = sb_seq("negb", [1, T], BF16)
    grow = [sb_seq(f"grow{b}", [1, TBLK], F32) for b in range(2)]
    m1_all = sb_seq("m1_all", [P, NT], F32)
    m2_all = sb_seq("m2_all", [P, NT], F32)
    sqscr = sb_seq("sqscr", [P, D], BF16)
    scratch_row = sb_seq("scratch_row", [1, 512], F32)
    stats = {}
    for nm in ("s_ln1", "ss_ln1", "m_ln1", "r_ln1", "s_lna", "ss_lna", "m_lna",
               "r_lna", "s_ln2", "ss_ln2", "m_ln2", "r_ln2", "s_lnf", "ss_lnf",
               "m_lnf", "r_lnf", "tmpa", "tmpb", "tmpc", "den_r"):
        stats[nm] = sb_seq(nm, [P, NT], F32)
    assert pp <= 207 * KB - 64, f"persist overflow {pp}"

    class Ps:
        def __init__(self):
            self.gs = []

        def alloc(self, name, shape, dtype=F32):
            g = nc.psum_tensor(name, shape, dtype)
            h = g.__enter__()
            self.gs.append(g)
            return h

        def close(self):
            for g in reversed(self.gs):
                g.__exit__(None, None, None)
            self.gs = []

    phB = Ps()
    ntrB = [phB.alloc(f"ntrB{b}", [P, P], BF16) for b in range(2)]
    browps = [phB.alloc(f"browp{i}", [1, min(512, T)], F32)
              for i in range(max(T // 512, 1))]
    phB.close()

    phC = Ps()
    stp = [phC.alloc(f"stp{b}", [P, QB], F32) for b in range(2)]
    aop = phC.alloc("aop", [P, D], F32)
    denp = phC.alloc("denp", [P, 1], F32)
    ntrC = [phC.alloc(f"ntrC{b}", [P, P], BF16) for b in range(2)]
    phC.close()

    phD0 = Ps()
    scoresp = phD0.alloc("scoresp", [NEXP, min(512, T)], F32)
    phD0.close()

    phD = Ps()
    outp = [phD.alloc(f"outp{i}", [P, D], F32) for i in range(TT_PER_TB)]
    h1p = [phD.alloc(f"h1p{b}", [P, 2 * TBLK], F32) for b in range(3)]
    gatesp = phD.alloc("gatesp", [P, TBLK], F32)
    d0b_ps = gatesp
    phD.close()

    sem_guards = []

    def mksem(name):
        g = nc.semaphore(name)
        h = g.__enter__()
        sem_guards.append(g)
        return h

    names = ("s_c s_wA s_wB s_fA s_fB s_st s_exp s_ao s_rec s_ep s_tr "
             "s_trd s_v s_sc s_x2w s_s1 s_ev s_ev2 s_s2 "
             "s_acc s_gb s_gbe s_ot s_dbg s_cg s_gh "
             "s_x0 s_x1 s_xr0 s_xr1 s_sp0 s_sp1 s_x2r0 s_x2r1 "
             "s_out0 s_out1 s_gr0 s_gr1").split()
    S = {n: mksem(n) for n in names}
    s_w = [S["s_wA"], S["s_wB"]]
    s_f = [S["s_fA"], S["s_fB"]]
    s_xP = [S["s_x0"], S["s_x1"]]
    s_xrP = [S["s_xr0"], S["s_xr1"]]
    s_spP = [S["s_sp0"], S["s_sp1"]]
    s_x2rP = [S["s_x2r0"], S["s_x2r1"]]
    s_outP = [S["s_out0"], S["s_out1"]]
    s_grP = [S["s_gr0"], S["s_gr1"]]
    cnt = {n: 0 for n in names}
    _sem_name = {id(S[n]): n for n in names}

    def inc(s, n=1):
        cnt[_sem_name[id(s)]] += n
        return cnt[_sem_name[id(s)]]

    def cur(s):
        return cnt[_sem_name[id(s)]]

    ops = {"sync": [], "gpsimd": [], "tensor": [], "scalar": [], "vector": []}

    def E(eng, fn, drain=False):
        ops[eng].append((fn, drain))

    def W(eng, s, v):
        E(eng, lambda e, s=s, v=v: e.wait_ge(s, v))

    def dma(eng, out_ap, in_ap, s):
        v = inc(s, 16)
        E(eng, lambda e, o=out_ap, i=in_ap, ss=s: e.dma_start(out=o, in_=i).then_inc(ss, 16))
        return v

    x_tiled = x_d.rearrange("(n p) d -> n p d", p=P)
    x2s_tiled = x2s_d.rearrange("(n p) d -> n p d", p=P)
    out_tiled = out_d.rearrange("(n p) d -> n p d", p=P)

    TBL_LNEXP = _act_table_idx(nc, "natural_log_exp_and_others")
    TBL_SIG = _act_table_idx(nc, "sigmoid_and_others")
    TBL_SILU = _act_table_idx(nc, "silu_and_others")

    def load_table(idx):
        E("scalar", lambda e, idx=idx: e.add_instruction(
            mybir.InstLoadActFuncSet(name=nc.get_next_instruction_name(),
                                     act_func_set_id=idx, ins=[], outs=[])))

    load_table(TBL_LNEXP)
    dma("sync", onesf[:], ones_d[:, :], S["s_c"])
    dma("sync", idnf[:], idn_d[:, :], S["s_c"])
    dma("sync", wlv[:], wl_d[:, :], S["s_c"])
    dma("sync", wrv[:], wr_d[:, :], S["s_c"])
    dma("sync", eb16[:], eb16_d[:, :], S["s_c"])
    dma("gpsimd", onesb[:], ones_d[:, :], S["s_cg"])
    dma("gpsimd", idnb[:], idn_d[:, :], S["s_cg"])
    for c in range(NCD):
        dma("gpsimd", rwt_sb[c][:], rwt_d[c * P:(c + 1) * P, :], S["s_cg"])
    c_all = cur(S["s_c"])
    cg_all = cur(S["s_cg"])

    E("vector", lambda e: e.memset(epsv[:], float(EPS)))
    mv = inc(S["s_v"])
    E("vector", lambda e, v=mv: e.memset(tinyv[:], 1e-30).then_inc(S["s_v"], 1))
    eps_ready = mv
    W("scalar", S["s_v"], eps_ready)
    W("vector", S["s_c"], c_all)
    W("vector", S["s_cg"], cg_all)
    E("vector", lambda e: e.tensor_mul(wprod[:], wlv[:], wrv[:]), drain=True)

    def rsqrt_chain(var_ap, r_ap, t1_ap, t2_ap, ready_v):
        W("scalar", S["s_v"], ready_v)
        E("scalar", lambda e, a=var_ap, o=t1_ap: e.activation(o, a, ACT.Ln, bias=epsv[:]))
        sv = inc(S["s_sc"])
        E("scalar", lambda e, a=t1_ap, o=r_ap, v=sv: e.activation(
            o, a, ACT.Exp, scale=-0.5).then_inc(S["s_sc"], 1))
        W("vector", S["s_sc"], sv)

    def ln_stats(src_ap, i, pfx):
        s_, ss_, m_, r_ = (stats[f"s_{pfx}"], stats[f"ss_{pfx}"],
                           stats[f"m_{pfx}"], stats[f"r_{pfx}"])
        E("vector", lambda e, i=i: e.reduce_sum(s_[:, i:i + 1], src_ap, axis=AX), drain=True)
        sv = inc(S["s_sc"])
        E("scalar", lambda e, i=i, v=sv: e.activation(
            sqscr[:], src_ap, ACT.Square, accum_out=ss_[:, i:i + 1]).then_inc(S["s_sc"], 1))
        W("vector", S["s_sc"], sv)
        E("vector", lambda e, i=i: e.tensor_scalar(
            m_[:, i:i + 1], s_[:, i:i + 1], 1.0 / D, None, ALU.mult), drain=True)
        E("vector", lambda e, i=i: e.tensor_scalar(
            stats["tmpa"][:, i:i + 1], ss_[:, i:i + 1], 1.0 / D, None, ALU.mult), drain=True)
        E("vector", lambda e, i=i: e.tensor_mul(
            stats["tmpb"][:, i:i + 1], m_[:, i:i + 1], m_[:, i:i + 1]), drain=True)
        vv = inc(S["s_v"])
        E("vector", lambda e, i=i, v=vv: e.tensor_sub(
            stats["tmpa"][:, i:i + 1], stats["tmpa"][:, i:i + 1],
            stats["tmpb"][:, i:i + 1]).then_inc(S["s_v"], 1))
        rsqrt_chain(stats["tmpa"][:, i:i + 1], r_[:, i:i + 1],
                    stats["tmpb"][:, i:i + 1], stats["tmpc"][:, i:i + 1], vv)

    red_v = []
    sq_v = []
    for i in range(NT):
        if i >= 2:
            W("sync", S["s_v"], red_v[i - 2])
            W("sync", S["s_sc"], sq_v[i - 2])
        dma("sync", x_r[i % 2][:], x_tiled[i, :, :], s_xP[i % 2])
        W("vector", s_xP[i % 2], 16 * (i // 2 + 1))
        W("scalar", s_xP[i % 2], 16 * (i // 2 + 1))
        rv = inc(S["s_v"])
        E("vector", lambda e, i=i, v=rv: e.reduce_sum(
            stats["s_ln1"][:, i:i + 1], x_r[i % 2][:], axis=AX).then_inc(S["s_v"], 1),
          drain=True)
        red_v.append(rv)
        sv = inc(S["s_sc"])
        E("scalar", lambda e, i=i, v=sv: e.activation(
            sqscr[:], x_r[i % 2][:], ACT.Square,
            accum_out=stats["ss_ln1"][:, i:i + 1]).then_inc(S["s_sc"], 1))
        sq_v.append(sv)
    W("vector", S["s_sc"], sq_v[-1])
    E("vector", lambda e: e.tensor_scalar(
        stats["m_ln1"][:], stats["s_ln1"][:], 1.0 / D, None, ALU.mult), drain=True)
    E("vector", lambda e: e.tensor_scalar(
        stats["tmpa"][:], stats["ss_ln1"][:], 1.0 / D, None, ALU.mult), drain=True)
    E("vector", lambda e: e.tensor_mul(
        stats["tmpb"][:], stats["m_ln1"][:], stats["m_ln1"][:]), drain=True)
    bv1 = inc(S["s_v"])
    E("vector", lambda e, v=bv1: e.tensor_sub(
        stats["tmpa"][:], stats["tmpa"][:], stats["tmpb"][:]).then_inc(S["s_v"], 1))
    rsqrt_chain(stats["tmpa"][:], stats["r_ln1"][:], stats["tmpb"][:],
                stats["tmpc"][:], bv1)
    ln1_rdy = cur(S["s_sc"])

    apply_v = []
    tr_after_tile = []
    for i in range(NT):
        if i == 0:
            W("sync", S["s_v"], red_v[-1])
            W("sync", S["s_sc"], sq_v[-1])
        else:
            pass
        if i >= 2:
            W("sync", S["s_v"], apply_v[i - 2])
        dma("sync", x_r[i % 2][:], x_tiled[i, :, :], s_xP[i % 2])
        W("vector", s_xP[i % 2], 16 * (NT // 2 + i // 2 + 1))
        if i == 0:
            W("vector", S["s_sc"], ln1_rdy)
        av = inc(S["s_v"])
        E("vector", lambda e, i=i, v=av: e.tensor_scalar(
            n1_tm[i][:], x_r[i % 2][:], stats["m_ln1"][:, i:i + 1],
            stats["r_ln1"][:, i:i + 1], ALU.subtract, ALU.mult).then_inc(S["s_v"], 1))
        apply_v.append(av)
        W("tensor", S["s_v"], av)
        if i == 0:
            W("tensor", S["s_c"], c_all)
            W("tensor", S["s_cg"], cg_all)
        for c in range(NCD):
            k = i * NCD + c
            if k >= 2:
                W("tensor", S["s_trd"], k - 1)
            tv = inc(S["s_tr"])
            E("tensor", lambda e, i=i, c=c, k=k: e.transpose(
                ntrB[k % 2][:], n1_tm[i][:, c * P:(c + 1) * P], idnb[:]).then_inc(S["s_tr"], 1))
            W("vector", S["s_tr"], tv)
            dv = inc(S["s_trd"])
            E("vector", lambda e, i=i, c=c, k=k: e.tensor_copy(
                n1_fm[c][:, i * P:(i + 1) * P], ntrB[k % 2][:]).then_inc(S["s_trd"], 1))
        tr_after_tile.append(cur(S["s_trd"]))

    E("vector", lambda e: e.drain())
    afm_last = None
    for c in range(NCD):
        afm_last = inc(S["s_v"])
        E("vector", lambda e, c=c, v=afm_last: e.tensor_scalar(
            a_fm[c][:], n1_fm[c][:], wprod[:, c:c + 1], None,
            ALU.mult).then_inc(S["s_v"], 1), drain=True)

    sq_buf = P_T[0]
    NS5 = max(T // 512, 1)
    W5 = min(512, T)
    brow_base = cur(S["s_st"])
    for c in range(NCD):
        if c >= 1:
            W("vector", S["s_st"], brow_base + c)
        sqv = inc(S["s_v"])
        E("vector", lambda e, c=c, v=sqv: e.tensor_mul(
            sq_buf[:, :T], a_fm[c][:], a_fm[c][:]).then_inc(S["s_v"], 1), drain=True)
        W("tensor", S["s_v"], sqv)
        for s5 in range(NS5):
            last = (c == NCD - 1)
            stv = inc(S["s_st"]) if s5 == NS5 - 1 else None
            def mk(c=c, s5=s5, last=last, fin=stv is not None):
                def f(e):
                    inst = e.matmul(browps[s5][:, :W5], onesb[:, 0:1],
                                    sq_buf[:, s5 * 512:s5 * 512 + W5],
                                    start=(c == 0), stop=last, skip_group_check=True)
                    if fin:
                        inst.then_inc(S["s_st"], 1)
                return f
            E("tensor", mk())
    brow_done = cur(S["s_st"])
    W("scalar", S["s_st"], brow_done)
    negb_v = None
    for s5 in range(NS5):
        if negb_v is not None:
            W("scalar", S["s_v"], negb_v)
        E("scalar", lambda e, s5=s5: e.activation(
            scratch_row[:, :W5], browps[s5][:, :W5], ACT.Ln, bias=tinyv[0:1, :]))
        sv = inc(S["s_sc"])
        E("scalar", lambda e, s5=s5, v=sv: e.activation(
            scratch_row[:, :W5], scratch_row[:, :W5], ACT.Exp,
            scale=0.5).then_inc(S["s_sc"], 1))
        W("vector", S["s_sc"], sv)
        negb_v = inc(S["s_v"])
        E("vector", lambda e, s5=s5, v=negb_v: e.tensor_scalar(
            negb_row[:, s5 * 512:s5 * 512 + W5], scratch_row[:, :W5], -32.0, None,
            ALU.mult).then_inc(S["s_v"], 1))

    if debug:
        W("gpsimd", S["s_v"], negb_v)
        for i in range(NT):
            dma("gpsimd", dbg["dbg_n1"].rearrange("(n p) d -> n p d", p=P)[i, :, :],
                n1_tm[i][:], S["s_dbg"])

    exp_done_at_block = []
    gi_tr = []
    ao_done_v = None

    ntr_pending = []
    ntr_tr_done = {}
    ntr_kc = [0]

    def emit_ntr_one():
        gi, n2v = ntr_pending.pop(0)
        W("tensor", S["s_v"], n2v)
        for c in range(NCD):
            kc = ntr_kc[0]
            ntr_kc[0] += 1
            if kc >= 2:
                W("tensor", S["s_trd"], cur(S["s_trd"]) - 1)
            tv = inc(S["s_tr"])
            E("tensor", lambda e, gi=gi, c=c, kc=kc: e.transpose(
                ntrC[kc % 2][:], n2tm[gi % 3][:, c * P:(c + 1) * P],
                idnb[:]).then_inc(S["s_tr"], 1))
            W("vector", S["s_tr"], tv)
            dv = inc(S["s_trd"])
            E("vector", lambda e, gi=gi, c=c, kc=kc: e.tensor_copy(
                n2_fm[c][:, gi * P:(gi + 1) * P], ntrC[kc % 2][:]).then_inc(S["s_trd"], 1))
        ntr_tr_done[gi] = cur(S["s_tr"])
        gi_tr.append(cur(S["s_trd"]))

    def emit_ST(b):
        q0 = b * QB
        buf = P_T[b % 2]
        for it in range(NT):
            k = b * NT + it
            if b == 0 and it == 0:
                W("tensor", S["s_v"], afm_last)
            if k >= 2:
                W("tensor", S["s_exp"], k - 1)
            for c in range(NCD):
                E("tensor", lambda e, c=c, it=it, q0=q0, k=k: e.matmul(
                    stp[k % 2][:], a_fm[c][:, it * P:(it + 1) * P],
                    n1_fm[c][:, q0:q0 + QB], start=(c == 0), stop=False,
                    skip_group_check=True))
            if b == 0 and it == 0:
                W("tensor", S["s_v"], negb_v)
            stv = inc(S["s_st"])
            E("tensor", lambda e, q0=q0, k=k, v=stv: e.matmul(
                stp[k % 2][:], onesb[0:1, :], negb_row[:, q0:q0 + QB],
                start=False, stop=True, skip_group_check=True).then_inc(S["s_st"], 1))
            W("scalar", S["s_st"], stv)
            ev = inc(S["s_exp"])
            E("scalar", lambda e, it=it, k=k, buf=buf, v=ev: e.activation(
                buf[:, it * QB:(it + 1) * QB], stp[k % 2][:], ACT.Exp).then_inc(S["s_exp"], 1))

    def emit_AO(b):
        nonlocal ao_done_v
        buf = P_T[b % 2]
        exp_need = exp_done_at_block[b]
        for qt in range(QT_PER_B):
            gi = b * QT_PER_B + qt
            W("tensor", S["s_exp"], exp_need)
            if gi > 0:
                W("tensor", S["s_ep"], cur(S["s_ep"]))
                W("tensor", S["s_rec"], cur(S["s_rec"]))
            for it in range(NT):
                st0 = buf[:, it * QB + qt * P: it * QB + (qt + 1) * P]
                first = it == 0
                last = it == NT - 1
                E("tensor", lambda e, st0=st0, it=it, first=first, last=last: e.matmul(
                    aop[:, 0:512], st0, n1_tm[it][:, 0:512],
                    start=first, stop=last, skip_group_check=True))
                E("tensor", lambda e, st0=st0, it=it, first=first, last=last: e.matmul(
                    aop[:, 512:1024], st0, n1_tm[it][:, 512:1024],
                    start=first, stop=last, skip_group_check=True))
                aov = inc(S["s_ao"]) if last else None
                def mkd(st0=st0, it=it, first=first, last=last, fin=aov is not None):
                    def f(e):
                        inst = e.matmul(denp[:], st0, onesb[:, 0:1],
                                        start=first, stop=last, skip_group_check=True)
                        if fin:
                            inst.then_inc(S["s_ao"], 1)
                    return f
                E("tensor", mkd())
            aov = cur(S["s_ao"])
            ao_done_v = aov
            if len(ntr_pending) >= 2:
                emit_ntr_one()
            W("vector", S["s_ao"], aov)
            rv = inc(S["s_rec"])
            E("vector", lambda e, gi=gi, v=rv: e.reciprocal(
                stats["den_r"][:, gi % NT:gi % NT + 1], denp[:]).then_inc(S["s_rec"], 1), drain=True)
            epv = inc(S["s_ep"])
            E("vector", lambda e, gi=gi, v=epv: e.tensor_scalar(
                y1pre[:], aop[:], stats["den_r"][:, gi % NT:gi % NT + 1], None,
                ALU.mult).then_inc(S["s_ep"], 1), drain=True)
            W("scalar", S["s_ep"], epv)
            ln_stats(y1pre[:], gi, "lna")
            E("vector", lambda e, gi=gi: e.tensor_scalar(
                y1pre[:], y1pre[:], stats["m_lna"][:, gi:gi + 1],
                stats["r_lna"][:, gi:gi + 1], ALU.subtract, ALU.mult), drain=True)
            W("vector", s_xrP[gi % 2], 16 * (gi // 2 + 1))
            if gi >= 2:
                W("vector", s_spP[gi % 2], 16 * (gi // 2))
            x2v = inc(S["s_x2w"])
            E("vector", lambda e, gi=gi, v=x2v: e.tensor_add(
                x2t[gi % 2][:], x_r[gi % 2][:], y1pre[:]).then_inc(S["s_x2w"], 1), drain=True)
            W("sync", S["s_x2w"], x2v)
            dma("sync", x2s_tiled[gi, :, :], x2t[gi % 2][:], s_spP[gi % 2])
            W("scalar", S["s_x2w"], x2v)
            ln_stats(x2t[gi % 2][:], gi, "ln2")
            if gi >= 3:
                W("vector", S["s_tr"], ntr_tr_done[gi - 3])
            n2v = inc(S["s_v"])
            E("vector", lambda e, gi=gi, v=n2v: e.tensor_scalar(
                n2tm[gi % 3][:], x2t[gi % 2][:], stats["m_ln2"][:, gi:gi + 1],
                stats["r_ln2"][:, gi:gi + 1], ALU.subtract, ALU.mult).then_inc(S["s_v"], 1))
            ntr_pending.append((gi, n2v))

    emit_ST(0)
    exp_done_at_block.append(cur(S["s_exp"]))
    for b in range(NQB):
        if b + 1 < NQB:
            emit_ST(b + 1)
            exp_done_at_block.append(cur(S["s_exp"]))
        for qt in range(QT_PER_B):
            gi = b * QT_PER_B + qt
            W("sync", S["s_v"], apply_v[-1])
            if gi >= 2:
                W("sync", S["s_x2w"], gi - 1)
            dma("sync", x_r[gi % 2][:], x_tiled[gi, :, :], s_xrP[gi % 2])
        emit_AO(b)

    while ntr_pending:
        emit_ntr_one()
    n2_all_v = cur(S["s_trd"])
    sp_all = (cur(S["s_sp0"]), cur(S["s_sp1"]))
    all_exp_done = exp_done_at_block[-1]

    if debug:
        W("gpsimd", S["s_trd"], n2_all_v)
        for c in range(NCD):
            dma("gpsimd", dbg["dbg_n2fm"].rearrange("(c p) t -> c p t", p=P)[c, :, :],
                n2_fm[c][:], S["s_dbg"])
        W("gpsimd", S["s_sp0"], sp_all[0])
        W("gpsimd", S["s_sp1"], sp_all[1])
        for i in range(NT):
            dma("gpsimd", dbg["dbg_x2"].rearrange("(n p) d -> n p d", p=P)[i, :, :],
                x2s_tiled[i, :, :], S["s_dbg"])

    load_table(TBL_SIG)
    for qb in range(NS5):
        if qb == 0:
            W("tensor", S["s_trd"], n2_all_v)
        for c in range(NCD):
            last = c == NCD - 1
            stv = inc(S["s_st"]) if last else None
            def mk(c=c, qb=qb, last=last, fin=stv is not None):
                def f(e):
                    inst = e.matmul(scoresp[:, :W5], rwt_sb[c][:],
                                    n2_fm[c][:, qb * 512:qb * 512 + W5],
                                    start=(c == 0), stop=last, skip_group_check=True)
                    if fin:
                        inst.then_inc(S["s_st"], 1)
                return f
            E("tensor", mk())
        stv = cur(S["s_st"])
        if qb == 0:
            W("scalar", S["s_sp0"], sp_all[0])
            W("scalar", S["s_sp1"], sp_all[1])
        W("scalar", S["s_st"], stv)
        sv = inc(S["s_sc"])
        E("scalar", lambda e, qb=qb, v=sv: e.activation(
            scores_fm[:, qb * 512:qb * 512 + W5], scoresp[:, :W5],
            ACT.Sigmoid).then_inc(S["s_sc"], 1))
        if qb + 1 < NS5:
            W("tensor", S["s_sc"], sv)
    sig_done = cur(S["s_sc"])

    d0b_q = []

    def Q(fn):
        d0b_q.append(fn)

    state = {}

    def _qi_bias():
        W("tensor", S["s_c"], c_all)
        W("tensor", S["s_cg"], cg_all)
        btv = inc(S["s_st"])
        E("tensor", lambda e, v=btv: e.matmul(
            d0b_ps[:, 0:NT * NEXP], onesf[0:1, :], eb16[:, :], start=True,
            stop=True, skip_group_check=True).then_inc(S["s_st"], 1))
        W("vector", S["s_st"], btv)
        bv = inc(S["s_v"])
        E("vector", lambda e, v=bv: e.tensor_copy(
            bias_rep[:], d0b_ps[:, 0:NT * NEXP]).then_inc(S["s_v"], 1), drain=True)
        state["bias_done"] = bv
    Q(_qi_bias)

    def _qi_str(i):
        def f():
            W("tensor", S["s_v"], state["bias_done"])
            W("tensor", S["s_trd"], cur(S["s_trd"]))
            if i == 0:
                W("tensor", S["s_sc"], sig_done)
            tv = inc(S["s_tr"])
            E("tensor", lambda e, i=i: e.transpose(
                d0b_ps[:, 0:NEXP].rearrange("p e -> p e") if False else d0b_ps[:, 0:NEXP],
                scores_fm[:, i * P:(i + 1) * P],
                idnf[0:NEXP, 0:NEXP]).then_inc(S["s_tr"], 1))
            W("vector", S["s_tr"], tv)
            inc(S["s_trd"])
            E("vector", lambda e, i=i: e.tensor_copy(
                scores_tm[:, i * NEXP:(i + 1) * NEXP],
                d0b_ps[:, 0:NEXP]).then_inc(S["s_trd"], 1))
        return f
    for i in range(NT):
        Q(_qi_str(i))

    def _qi_top2_head():
        W("vector", S["s_trd"], cur(S["s_trd"]))
        E("vector", lambda e: e.drain())
        E("vector", lambda e: e.tensor_add(v_all[:], scores_tm[:], bias_rep[:]),
          drain=True)
    Q(_qi_top2_head)

    def _qi_top2(i):
        def f():
            sl = slice(i * NEXP, (i + 1) * NEXP)
            E("vector", lambda e, sl=sl, i=i: e.reduce_max(
                m1_all[:, i:i + 1], v_all[:, sl], axis=AX), drain=True)
            E("vector", lambda e, sl=sl, i=i: e.tensor_scalar(
                e1_all[:, sl], v_all[:, sl], m1_all[:, i:i + 1], None,
                ALU.is_equal), drain=True)
            E("vector", lambda e, sl=sl: e.scalar_tensor_tensor(
                v2_all[:, sl], e1_all[:, sl], -1e30, v_all[:, sl], ALU.mult,
                ALU.add), drain=True)
            E("vector", lambda e, sl=sl, i=i: e.reduce_max(
                m2_all[:, i:i + 1], v2_all[:, sl], axis=AX), drain=True)
            E("vector", lambda e, sl=sl, i=i: e.tensor_scalar(
                e2_all[:, sl], v2_all[:, sl], m2_all[:, i:i + 1], None,
                ALU.is_equal), drain=True)
        return f
    for i in range(NT):
        Q(_qi_top2(i))

    def _qi_gates():
        E("vector", lambda e: e.tensor_add(e1_all[:], e1_all[:], e2_all[:]),
          drain=True)
        gv = inc(S["s_v"])
        E("vector", lambda e, v=gv: e.tensor_mul(
            gates_all[:], e1_all[:], scores_tm[:]).then_inc(S["s_v"], 1),
          drain=True)
        state["gv"] = gv
        if debug:
            W("gpsimd", S["s_v"], gv)
            dma("gpsimd", dbg["dbg_gates"][:, :], gates_all[:], S["s_dbg"])
    Q(_qi_gates)

    def _qi_gtr(i):
        def f():
            W("tensor", S["s_v"], state["gv"])
            W("tensor", S["s_trd"], cur(S["s_trd"]))
            tv = inc(S["s_tr"])
            E("tensor", lambda e, i=i: e.transpose(
                d0b_ps[0:NEXP, 0:P], gates_all[:, i * NEXP:(i + 1) * NEXP],
                idnf[:]).then_inc(S["s_tr"], 1))
            W("vector", S["s_tr"], tv)
            inc(S["s_trd"])
            E("vector", lambda e, i=i: e.tensor_copy(
                gates_fm[:, i * P:(i + 1) * P],
                d0b_ps[0:NEXP, 0:P]).then_inc(S["s_trd"], 1))
        return f
    for i in range(NT):
        Q(_qi_gtr(i))

    def _qi_bounce():
        W("sync", S["s_trd"], cur(S["s_trd"]))
        dma("sync", gates_hbm[:, :], gates_fm[:], S["s_gh"])
    Q(_qi_bounce)

    load_table(TBL_SILU)
    base_chunks = [("shared", k) for k in range(4)] + [("expert", e) for e in range(NEXP)]
    chunks = [(p, kind, idx) for p in range(NPASS) for (kind, idx) in base_chunks]
    NCHUNK_P = len(base_chunks)

    chunk_w_target = [None] * len(chunks)

    def emit_stream(pss0):
        if pss0 == 0:
            W("gpsimd", S["s_exp"], all_exp_done)
        for ci in range(pss0 * NCHUNK_P, (pss0 + 1) * NCHUNK_P):
            _, kind, idx = chunks[ci]
            sl = ci % 2
            if ci == 1:
                W("gpsimd", S["s_ao"], ao_done_v)
                if debug:
                    W("gpsimd", S["s_dbg"], cur(S["s_dbg"]))
            if ci >= 2:
                W("gpsimd", S["s_s2"], (ci - 1) * NTB * NCD * TT_PER_TB)
            w12_t = slotA_w12 if sl == 0 else slotB_w12
            w3_t = slotA_w3 if sl == 0 else slotB_w3
            if kind == "shared":
                k = idx
                for c in range(NCD):
                    dma("gpsimd", w12_t[c][:, 0:RH],
                        w1t_d[c * P:(c + 1) * P, k * RH:(k + 1) * RH], s_w[sl])
                    dma("gpsimd", w12_t[c][:, RH:2 * RH],
                        w2t_d[c * P:(c + 1) * P, k * RH:(k + 1) * RH], s_w[sl])
                for j in range(NCD):
                    dma("gpsimd", w3_t[j][:],
                        w3st_d[k * RH + j * P: k * RH + (j + 1) * P, :], s_w[sl])
            else:
                ei = idx
                for c in range(NCD):
                    dma("gpsimd", w12_t[c][:], w12t_d[ei, c * P:(c + 1) * P, :], s_w[sl])
                for j in range(NCD):
                    dma("gpsimd", w3_t[j][:], w3t_d[ei, j * P:(j + 1) * P, :], s_w[sl])
            chunk_w_target[ci] = cur(s_w[sl])

    jj = 0
    gate_load_cnt = [0]
    acc_done_pass = []

    def emit_pass(pss, mid_hook=None):
        nonlocal jj
        S2_DEPTH = 3
        s2_pending = []
        acc_q = []

        def emit_s2_fix(p):
            (jj_, j_, w3t_, first_acc_wait, _unused_v, s2v, accs) = p
            for tt in range(TT_PER_TB):
                for half in range(2):
                    if first_acc_wait:
                        W("tensor", S["s_acc"], acc_free.get((tt, half), 0))
                    lastj = j_ == NCD - 1
                    fin = half == 1
                    def mk2(tt=tt, half=half, jj_=jj_, j_=j_, w3t_=w3t_,
                            lastj=lastj, fin=fin, s2v=s2v):
                        def f(e):
                            inst = e.matmul(
                                outp[tt][:, half * 512:(half + 1) * 512],
                                Gt[jj_ % 4][:, tt * P:(tt + 1) * P],
                                w3t_[j_][:, half * 512:(half + 1) * 512],
                                start=(j_ == 0), stop=lastj, skip_group_check=True)
                            if fin:
                                inst.then_inc(S["s_s2"], 1)
                        return f
                    E("tensor", mk2())
            if accs is not None:
                ci_, tb_ = accs

                def emit_accs(ci_=ci_, tb_=tb_, s2v=s2v):
                    W("vector", S["s_s2"], s2v)
                    for tt in range(TT_PER_TB):
                        ai = tb_ * TT_PER_TB + tt
                        for half in range(2):
                            av = inc(S["s_acc"])
                            acc_free[(tt, half)] = av
                            hs = slice(half * 512, (half + 1) * 512)
                            if ci_ % NCHUNK_P == 0:
                                E("vector", lambda e, tt=tt, ai=ai, hs=hs, v=av: e.tensor_copy(
                                    acc[ai][:, hs], outp[tt][:, hs]).then_inc(S["s_acc"], 1), drain=True)
                            else:
                                E("vector", lambda e, tt=tt, ai=ai, hs=hs, v=av: e.tensor_add(
                                    acc[ai][:, hs], acc[ai][:, hs], outp[tt][:, hs]).then_inc(S["s_acc"], 1), drain=True)
                        if ci_ % NCHUNK_P == NCHUNK_P - 1:
                            acc_tile_ready[(ci_ // NCHUNK_P, ai)] = av
                            emit_stats_pre(ci_ // NCHUNK_P, ai)
                acc_q.append(emit_accs)

        for ck in range(NCHUNK_P):
            ci = pss * NCHUNK_P + ck
            _, kind, idx = chunks[ci]
            sl = ci % 2
            w12_t = slotA_w12 if sl == 0 else slotB_w12
            w3_t = slotA_w3 if sl == 0 else slotB_w3
            has_g = kind == "expert"
            if has_g:
                while d0b_q:
                    d0b_q.pop(0)()
            W("tensor", s_w[sl], chunk_w_target[ci])
            if ci == 0:
                W("tensor", S["s_sc"], sig_done)
            for tb in range(NTB):
                if mid_hook is not None and ck == 0:
                    mid_hook(tb)
                tok0 = pss * PASS_T + tb * TBLK
                if has_g:
                    gk = gate_load_cnt[0]
                    gate_load_cnt[0] += 1
                    if gk == 0:
                        W("sync", S["s_gh"], 16)
                        W("sync", S["s_sp0"], sp_all[0])
                        W("sync", S["s_sp1"], sp_all[1])
                        W("sync", S["s_x2w"], cur(S["s_x2w"]))
                    if gk >= 2:
                        W("sync", S["s_gb"], gk - 1)
                    grv = dma("sync", grow[gk % 2][:],
                              gates_hbm[idx:idx + 1, tok0:tok0 + TBLK], s_grP[gk % 2])
                    W("tensor", s_grP[gk % 2], grv)
                    if cur(S["s_gb"]) > 0:
                        W("tensor", S["s_gbe"], cur(S["s_gbe"]))
                    gbv = inc(S["s_gb"])
                    E("tensor", lambda e, gk=gk, v=gbv: e.matmul(
                        gatesp[:], onesf[0:1, :], grow[gk % 2][:],
                        start=True, stop=True, skip_group_check=True).then_inc(S["s_gb"], 1))
                    W("vector", S["s_gb"], gbv)
                    gbev = inc(S["s_gbe"])
                    E("vector", lambda e, tb=tb, v=gbev: e.tensor_copy(
                        gate_b[tb % 2][:], gatesp[:]).then_inc(S["s_gbe"], 1), drain=True)
                for j in range(NCD):
                    if d0b_q:
                        d0b_q.pop(0)()
                    if d0b_q:
                        d0b_q.pop(0)()
                    if jj >= 3:
                        W("tensor", S["s_ev2"], jj - 2)
                    for c in range(NCD):
                        E("tensor", lambda e, c=c, j=j, jj=jj, tok0=tok0, w12_t=w12_t: e.matmul(
                            h1p[jj % 3][:, 0:TBLK], w12_t[c][:, j * P:(j + 1) * P],
                            n2_fm[c][:, tok0:tok0 + TBLK], start=(c == 0), stop=False,
                            skip_group_check=True))
                    for c in range(NCD):
                        last = c == NCD - 1
                        s1v = inc(S["s_s1"]) if last else None
                        def mk1(c=c, j=j, jj=jj, tok0=tok0, w12_t=w12_t, last=last,
                                fin=s1v is not None):
                            def f(e):
                                inst = e.matmul(
                                    h1p[jj % 3][:, TBLK:2 * TBLK],
                                    w12_t[c][:, RH + j * P:RH + (j + 1) * P],
                                    n2_fm[c][:, tok0:tok0 + TBLK],
                                    start=False, stop=last, skip_group_check=True)
                                if fin:
                                    inst.then_inc(S["s_s1"], 1)
                            return f
                        E("tensor", mk1())
                    s1v = cur(S["s_s1"])
                    ev2 = inc(S["s_ev2"])
                    s2v = inc(S["s_s2"], TT_PER_TB)
                    first_acc_wait = (j == 0) and not (ci == 0 and tb == 0)
                    accs = (ci, tb) if j == NCD - 1 else None
                    this = (jj, j, w3_t, first_acc_wait, None, s2v, accs)
                    s2_pending.append((this, ev2))
                    if len(s2_pending) > S2_DEPTH:
                        p, pev2 = s2_pending.pop(0)
                        emit_s2_fix(p)
                    W("scalar", S["s_s1"], s1v)
                    evv = inc(S["s_ev"])
                    E("scalar", lambda e, jj=jj, v=evv: e.activation(
                        s1t[jj % 3][:], h1p[jj % 3][:, 0:TBLK],
                        ACT.Silu).then_inc(S["s_ev"], 1))
                    while acc_q:
                        acc_q.pop(0)()
                    W("vector", S["s_ev"], evv)
                    if jj >= 4:
                        W("vector", S["s_s2"], (jj - 3) * TT_PER_TB)
                    if has_g:
                        W("vector", S["s_gbe"], cur(S["s_gbe"]))
                        E("vector", lambda e, jj=jj, tb=tb: e.tensor_mul(
                            t2t[jj % 2][:], h1p[jj % 3][:, TBLK:2 * TBLK],
                            gate_b[tb % 2][:]), drain=True)
                        E("vector", lambda e, jj=jj, v=ev2: e.tensor_mul(
                            Gt[jj % 4][:], s1t[jj % 3][:],
                            t2t[jj % 2][:]).then_inc(S["s_ev2"], 1), drain=True)
                    else:
                        E("vector", lambda e, jj=jj, v=ev2: e.tensor_mul(
                            Gt[jj % 4][:], s1t[jj % 3][:],
                            h1p[jj % 3][:, TBLK:2 * TBLK]).then_inc(S["s_ev2"], 1), drain=True)
                    jj += 1
        while s2_pending:
            p, pev2 = s2_pending.pop(0)
            W("tensor", S["s_ev2"], pev2)
            emit_s2_fix(p)
        while acc_q:
            acc_q.pop(0)()

    acc_tile_ready = {}
    acc_free = {}
    stats_pre_done = set()

    def emit_stats_pre(pss, ii):
        si = pss * NACC + ii
        stats_pre_done.add((pss, ii))
        rdy_v = acc_tile_ready[(pss, ii)]
        W("vector", S["s_acc"], rdy_v)
        W("scalar", S["s_acc"], rdy_v)
        E("vector", lambda e, ii=ii, si=si: e.reduce_sum(
            stats["s_lnf"][:, si:si + 1], acc[ii][:], axis=AX), drain=True)
        sv = inc(S["s_sc"])
        E("scalar", lambda e, ii=ii, si=si, v=sv: e.activation(
            sqscr[:], acc[ii][:], ACT.Square,
            accum_out=stats["ss_lnf"][:, si:si + 1]).then_inc(S["s_sc"], 1))

    def emit_phase_e_stats(pss):
        load_table(TBL_LNEXP)
        if pss == 0:
            W("vector", S["s_sp0"], sp_all[0])
            W("vector", S["s_sp1"], sp_all[1])
        if debug and pss == 0:
            W("gpsimd", S["s_acc"], acc_done_pass[0])
            for i in range(NACC):
                dma("gpsimd", dbg["dbg_acc"].rearrange("(n p) d -> n p d", p=P)[i, :, :],
                    acc[i][:], S["s_dbg"])
        c0 = pss * NACC
        for ii in range(NACC):
            if (pss, ii) not in stats_pre_done:
                emit_stats_pre(pss, ii)
        W("vector", S["s_sc"], cur(S["s_sc"]))
        sl = slice(c0, c0 + NACC)
        E("vector", lambda e, sl=sl: e.tensor_scalar(
            stats["m_lnf"][:, sl], stats["s_lnf"][:, sl], 1.0 / D, None, ALU.mult),
          drain=True)
        E("vector", lambda e, sl=sl: e.tensor_scalar(
            stats["tmpa"][:, sl], stats["ss_lnf"][:, sl], 1.0 / D, None, ALU.mult),
          drain=True)
        E("vector", lambda e, sl=sl: e.tensor_mul(
            stats["tmpb"][:, sl], stats["m_lnf"][:, sl], stats["m_lnf"][:, sl]),
          drain=True)
        bv = inc(S["s_v"])
        E("vector", lambda e, sl=sl, v=bv: e.tensor_sub(
            stats["tmpa"][:, sl], stats["tmpa"][:, sl],
            stats["tmpb"][:, sl]).then_inc(S["s_v"], 1))
        rsqrt_chain(stats["tmpa"][:, sl], stats["r_lnf"][:, sl],
                    stats["tmpb"][:, sl], stats["tmpc"][:, sl], bv)
        return cur(S["s_sc"])

    def emit_phase_e_tile(pss, ii, rdy):
        if True:
            gi = pss * NACC + ii
            W("sync", S["s_sp0"], sp_all[0])
            W("sync", S["s_sp1"], sp_all[1])
            W("sync", S["s_acc"], acc_done_pass[pss])
            if gi >= 2:
                W("sync", s_outP[gi % 2], 16 * (gi // 2))
            dma("sync", x2r[gi % 2][:], x2s_tiled[gi, :, :], s_x2rP[gi % 2])
            si = gi
            W("vector", S["s_sc"], rdy)
            if gi >= 2:
                W("vector", s_outP[gi % 2], 16 * (gi // 2))
            E("vector", lambda e, ii=ii, si=si, gi=gi: e.tensor_scalar(
                out_sb[gi % 2][:], acc[ii][:], stats["m_lnf"][:, si:si + 1],
                stats["r_lnf"][:, si:si + 1], ALU.subtract, ALU.mult), drain=True)
            W("vector", s_x2rP[gi % 2], 16 * (gi // 2 + 1))
            otv = inc(S["s_ot"])
            E("vector", lambda e, gi=gi, v=otv: e.tensor_add(
                out_sb[gi % 2][:], out_sb[gi % 2][:], x2r[gi % 2][:]).then_inc(S["s_ot"], 1), drain=True)
            W("sync", S["s_ot"], otv)
            dma("sync", out_tiled[gi, :, :], out_sb[gi % 2][:], s_outP[gi % 2])

    def emit_phase_e(pss):
        rdy = emit_phase_e_stats(pss)
        for ii in range(NACC):
            emit_phase_e_tile(pss, ii, rdy)

    for pss in range(NPASS):
        if pss > 0 and debug:
            W("vector", S["s_dbg"], cur(S["s_dbg"]))
        emit_stream(pss)
        if pss > 0 and not DISABLE_HOOK:
            prev = pss - 1
            rdy = emit_phase_e_stats(prev)
            load_table(TBL_SILU)

            def hook(tb, prev=prev, rdy=rdy):
                for ii in range(tb * TT_PER_TB, (tb + 1) * TT_PER_TB):
                    emit_phase_e_tile(prev, ii, rdy)
            emit_pass(pss, mid_hook=hook)
        elif pss > 0:
            emit_phase_e(pss - 1)
            load_table(TBL_SILU)
            emit_pass(pss)
        else:
            emit_pass(pss)
        acc_done_pass.append(cur(S["s_acc"]))
        if pss == NPASS - 1:
            emit_phase_e(pss)

    W("sync", S["s_out0"], 16 * ((NT + 1) // 2))
    W("sync", S["s_out1"], 16 * (NT // 2))
    if debug:
        W("gpsimd", S["s_dbg"], cur(S["s_dbg"]))

    with nc.Block() as block:
        @block.sync
        def _(eng):
            for f, dr in ops["sync"]:
                f(eng)

        @block.gpsimd
        def _(eng):
            for f, dr in ops["gpsimd"]:
                f(eng)

        @block.tensor
        def _(eng):
            for f, dr in ops["tensor"]:
                f(eng)

        @block.scalar
        def _(eng):
            for f, dr in ops["scalar"]:
                f(eng)
                eng.drain()

        @block.vector
        def _(eng):
            for f, dr in ops["vector"]:
                f(eng)
                if dr:
                    eng.drain()

    for g in reversed(sem_guards):
        g.__exit__(None, None, None)
    return nc


def _prep_inputs(L0_attn_x, L0_attn_w_lhs, L0_attn_w_rhs, L0_ffn_w12, L0_ffn_w3,
                 L0_ffn_shared_w1, L0_ffn_shared_w2, L0_ffn_shared_w3,
                 L0_ffn_router_w, L0_ffn_expert_bias, NT=16):
    T = NT * P
    x = np.ascontiguousarray(np.asarray(L0_attn_x, dtype=np.float32))
    w1t = np.ascontiguousarray(np.asarray(L0_ffn_shared_w1, np.float32).T)
    w2t = np.ascontiguousarray(np.asarray(L0_ffn_shared_w2, np.float32).T)
    w3st = np.ascontiguousarray(np.asarray(L0_ffn_shared_w3, np.float32).T)
    w12t = np.ascontiguousarray(np.asarray(L0_ffn_w12, np.float32).transpose(0, 2, 1))
    w3t = np.ascontiguousarray(np.asarray(L0_ffn_w3, np.float32).transpose(0, 2, 1))
    rwt = np.ascontiguousarray(np.asarray(L0_ffn_router_w, np.float32).T)
    wl = np.ascontiguousarray(np.asarray(L0_attn_w_lhs, np.float32).reshape(NCD, P).T)
    wr = np.ascontiguousarray(np.asarray(L0_attn_w_rhs, np.float32).reshape(NCD, P).T)
    eb = np.asarray(L0_ffn_expert_bias, np.float32).reshape(1, NEXP)
    eb16 = np.ascontiguousarray(np.tile(eb, (1, NT)))
    ones = np.ones((P, P), dtype=np.float32)
    idn = np.eye(P, dtype=np.float32)
    in_maps = []
    for i in range(x.shape[0]):
        in_maps.append({
            "x": np.ascontiguousarray(x[i, :T]),
            "wl": wl, "wr": wr, "w1t": w1t, "w2t": w2t, "w3st": w3st,
            "w12t": w12t, "w3t": w3t, "rwt": rwt, "eb16": eb16,
            "onesc": ones, "idnc": idn,
        })
    return in_maps


def kernel(L0_attn_x, L0_attn_w_lhs, L0_attn_w_rhs, L0_ffn_w12, L0_ffn_w3,
           L0_ffn_shared_w1, L0_ffn_shared_w2, L0_ffn_shared_w3,
           L0_ffn_router_w, L0_ffn_expert_bias):
    NT = 16
    in_maps = _prep_inputs(L0_attn_x, L0_attn_w_lhs, L0_attn_w_rhs, L0_ffn_w12,
                           L0_ffn_w3, L0_ffn_shared_w1, L0_ffn_shared_w2,
                           L0_ffn_shared_w3, L0_ffn_router_w,
                           L0_ffn_expert_bias, NT=NT)
    nc = build_program(NT=NT, debug=False)
    res = run_bass_kernel_spmd(nc, in_maps, core_ids=list(range(len(in_maps))))
    out = np.stack([np.asarray(r["out"], dtype=np.float32) for r in res.results])
    return out


# version 14
# speedup vs baseline: 1.0572x; 1.0174x over previous
import numpy as np
import concourse.bass as bass
import concourse.mybir as mybir
from concourse.bass_utils import run_bass_kernel_spmd

F32 = mybir.dt.float32
BF16 = mybir.dt.bfloat16
AX = mybir.AxisListType.X
ALU = mybir.AluOpType
ACT = mybir.ActivationFunctionType

P = 128
D = 1024
NCD = D // P
H_SH = 4096
NEXP = 8
RH = 1024
EPS = 1e-5
KB = 1024


def _act_table_idx(nc, set_name):
    from concourse.hw_specs import get_activation_tables
    for idx, name in enumerate(get_activation_tables(nc.m.arch).keys()):
        if name == set_name:
            return idx
    raise KeyError(set_name)


def build_program(NT=16, debug=False, pass_t=None, DISABLE_HOOK=False):
    T = NT * P
    QB = min(512, T)
    NQB = T // QB
    QT_PER_B = QB // P
    TBLK = min(256, T)
    PASS_T = pass_t if pass_t is not None else min(1024, T)
    NPASS = T // PASS_T
    NTB = PASS_T // TBLK
    TT_PER_TB = TBLK // P
    NACC = PASS_T // P

    nc = bass.Bass("TRN2", target_bir_lowering=False)

    x_d = nc.dram_tensor("x", [T, D], F32, kind="ExternalInput").ap()
    wl_d = nc.dram_tensor("wl", [P, NCD], F32, kind="ExternalInput").ap()
    wr_d = nc.dram_tensor("wr", [P, NCD], F32, kind="ExternalInput").ap()
    w1t_d = nc.dram_tensor("w1t", [D, H_SH], F32, kind="ExternalInput").ap()
    w2t_d = nc.dram_tensor("w2t", [D, H_SH], F32, kind="ExternalInput").ap()
    w3st_d = nc.dram_tensor("w3st", [H_SH, D], F32, kind="ExternalInput").ap()
    w12t_d = nc.dram_tensor("w12t", [NEXP, D, 2 * RH], F32, kind="ExternalInput").ap()
    w3t_d = nc.dram_tensor("w3t", [NEXP, RH, D], F32, kind="ExternalInput").ap()
    rwt_d = nc.dram_tensor("rwt", [D, NEXP], F32, kind="ExternalInput").ap()
    eb16_d = nc.dram_tensor("eb16", [1, NT * NEXP], F32, kind="ExternalInput").ap()
    ones_d = nc.dram_tensor("onesc", [P, P], F32, kind="ExternalInput").ap()
    idn_d = nc.dram_tensor("idnc", [P, P], F32, kind="ExternalInput").ap()
    x2s_d = nc.dram_tensor("x2s", [T, D], F32, kind="Internal").ap()
    gates_hbm = nc.dram_tensor("gates_hbm", [NEXP, T], F32, kind="Internal").ap()
    out_d = nc.dram_tensor("out", [T, D], F32, kind="ExternalOutput").ap()

    dbg = {}
    if debug:
        for name, shape in [("dbg_n1", [T, D]), ("dbg_n2fm", [D, T]),
                            ("dbg_gates", [P, NT * NEXP]), ("dbg_acc", [PASS_T, D]),
                            ("dbg_negb", [1, T]), ("dbg_x2", [T, D])]:
            dbg[name] = nc.dram_tensor(name, shape, F32, kind="ExternalOutput").ap()

    ARENA_B = 207 * KB
    _arena = nc.alloc_sbuf_tensor("arena0", [P, ARENA_B], mybir.dt.uint8)
    SB_BASE = nc.lookup_mloc(_arena).addr

    def sb_at(name, shape, dtype, off):
        assert 0 <= off < ARENA_B, (name, off)
        return nc.alloc_sbuf_tensor_at(name, shape, dtype, offset=SB_BASE + off)

    n1_tm = [sb_at(f"n1tm{i}", [P, D], BF16, i * 2 * KB) for i in range(NT)]
    n1_fm = [sb_at(f"n1fm{c}", [P, T], BF16, 32 * KB + c * (2 * T)) for c in range(NCD)]
    a_fm = [sb_at(f"afm{c}", [P, T], BF16, 64 * KB + c * (2 * T)) for c in range(NCD)]
    ptb = NT * QB * 2
    P_T = [sb_at(f"pt{b}", [P, NT * QB], BF16, 96 * KB + b * ptb) for b in range(2)]
    assert 2 * ptb <= 32 * KB
    n2_fm = [sb_at(f"n2fm{c}", [P, T], BF16, 128 * KB + c * (2 * T)) for c in range(NCD)]

    slotB_w12 = [sb_at(f"sBw12_{c}", [P, 2 * RH], BF16, c * 4 * KB) for c in range(NCD)]
    slotA_w12 = [sb_at(f"sAw12_{c}", [P, 2 * RH], BF16, 32 * KB + c * 4 * KB)
                 for c in range(NCD)]
    slotA_w3 = [sb_at(f"sAw3_{j}", [P, D], BF16, 64 * KB + j * 2 * KB) for j in range(NCD)]
    slotB_w3 = [sb_at(f"sBw3_{j}", [P, D], BF16, 80 * KB + j * 2 * KB) for j in range(NCD)]
    acc = [sb_at(f"acc{i}", [P, D], F32, 96 * KB + i * 4 * KB) for i in range(NACC)]
    assert NACC * 4 * KB <= 32 * KB

    t1 = 160 * KB
    x_r = [sb_at(f"xr{b}", [P, D], F32, t1 + b * 4 * KB) for b in range(2)]
    x2t = [sb_at(f"x2t{b}", [P, D], F32, t1 + 8 * KB + b * 4 * KB) for b in range(2)]
    y1pre = sb_at("y1p", [P, D], F32, t1 + 16 * KB)
    n2tm = [sb_at(f"n2tm{b}", [P, D], BF16, t1 + 20 * KB + b * 2 * KB)
            for b in range(3)]
    s1t = [sb_at(f"s1t{b}", [P, TBLK], F32, t1 + b * KB) for b in range(3)]
    t2t = [sb_at(f"t2t{b}", [P, TBLK], F32, t1 + 3 * KB + b * KB) for b in range(2)]
    Gt = [sb_at(f"Gt{b}", [P, TBLK], BF16, t1 + 5 * KB + b * TBLK * 2) for b in range(4)]
    gate_b = [sb_at(f"gb{b}", [P, TBLK], F32, t1 + 7 * KB + b * KB) for b in range(2)]
    scores_fm = sb_at("scores_fm", [NEXP, T], F32, t1 + 9 * KB)
    gates_fm = sb_at("gates_fm", [NEXP, T], F32, t1 + 9 * KB)
    x2r = [sb_at(f"x2r{b}", [P, D], F32, t1 + 9 * KB + b * 4 * KB) for b in range(2)]
    out_sb = [sb_at(f"osb{b}", [P, D], F32, t1 + 17 * KB + b * 4 * KB) for b in range(2)]

    pp = 186 * KB
    def sb_seq(name, shape, dtype):
        nonlocal pp
        t = sb_at(name, shape, dtype, pp)
        fb = int(np.prod(shape[1:])) * mybir.dt.size(dtype)
        pp += (fb + 63) // 64 * 64
        return t

    epsv = sb_seq("epsv", [P, 1], F32)
    tinyv = sb_seq("tinyv", [P, 1], F32)
    onesf = sb_seq("onesf", [P, P], F32)
    onesb = sb_seq("onesb", [P, P], BF16)
    idnb = sb_seq("idnb", [P, P], BF16)
    idnf = sb_seq("idnf", [P, P], F32)
    wlv = sb_seq("wlv", [P, NCD], F32)
    wrv = sb_seq("wrv", [P, NCD], F32)
    wprod = sb_seq("wprod", [P, NCD], F32)
    rwt_sb = [sb_seq(f"rwt{c}", [P, NEXP], BF16) for c in range(NCD)]
    eb16 = sb_seq("eb16", [1, NT * NEXP], F32)
    bias_rep = sb_seq("bias_rep", [P, NT * NEXP], F32)
    scores_tm = sb_seq("scores_tm", [P, NT * NEXP], F32)
    v_all = sb_seq("v_all", [P, NT * NEXP], F32)
    e1_all = sb_seq("e1_all", [P, NT * NEXP], F32)
    v2_all = sb_seq("v2_all", [P, NT * NEXP], F32)
    e2_all = sb_seq("e2_all", [P, NT * NEXP], F32)
    gates_all = sb_seq("gates_all", [P, NT * NEXP], F32)
    negb_row = sb_seq("negb", [1, T], BF16)
    grow = [sb_seq(f"grow{b}", [1, TBLK], F32) for b in range(2)]
    m1_all = sb_seq("m1_all", [P, NT], F32)
    m2_all = sb_seq("m2_all", [P, NT], F32)
    sqscr = sb_seq("sqscr", [P, D], BF16)
    scratch_row = sb_seq("scratch_row", [1, 512], F32)
    stats = {}
    for nm in ("s_ln1", "ss_ln1", "m_ln1", "r_ln1", "s_lna", "ss_lna", "m_lna",
               "r_lna", "s_ln2", "ss_ln2", "m_ln2", "r_ln2", "s_lnf", "ss_lnf",
               "m_lnf", "r_lnf", "tmpa", "tmpb", "tmpc", "den_r"):
        stats[nm] = sb_seq(nm, [P, NT], F32)
    assert pp <= 207 * KB - 64, f"persist overflow {pp}"

    class Ps:
        def __init__(self):
            self.gs = []

        def alloc(self, name, shape, dtype=F32):
            g = nc.psum_tensor(name, shape, dtype)
            h = g.__enter__()
            self.gs.append(g)
            return h

        def close(self):
            for g in reversed(self.gs):
                g.__exit__(None, None, None)
            self.gs = []

    phB = Ps()
    ntrB = [phB.alloc(f"ntrB{b}", [P, P], BF16) for b in range(2)]
    browps = [phB.alloc(f"browp{i}", [1, min(512, T)], F32)
              for i in range(max(T // 512, 1))]
    phB.close()

    phC = Ps()
    stp = [phC.alloc(f"stp{b}", [P, QB], F32) for b in range(2)]
    aop = phC.alloc("aop", [P, D], F32)
    denp = phC.alloc("denp", [P, 1], F32)
    ntrC = [phC.alloc(f"ntrC{b}", [P, P], BF16) for b in range(2)]
    phC.close()

    phD0 = Ps()
    scoresp = phD0.alloc("scoresp", [NEXP, min(512, T)], F32)
    phD0.close()

    phD = Ps()
    outp = [phD.alloc(f"outp{i}", [P, D], F32) for i in range(TT_PER_TB)]
    h1p = [phD.alloc(f"h1p{b}", [P, 2 * TBLK], F32) for b in range(3)]
    gatesp = phD.alloc("gatesp", [P, TBLK], F32)
    d0b_ps = gatesp
    phD.close()

    sem_guards = []

    def mksem(name):
        g = nc.semaphore(name)
        h = g.__enter__()
        sem_guards.append(g)
        return h

    names = ("s_c s_wA s_wB s_fA s_fB s_st s_exp s_ao s_rec s_ep s_tr "
             "s_trd s_v s_sc s_x2w s_s1 s_ev s_ev2 s_s2 "
             "s_acc s_gb s_gbe s_ot s_dbg s_cg s_gh "
             "s_x0 s_x1 s_xr0 s_xr1 s_sp0 s_sp1 s_x2r0 s_x2r1 "
             "s_out0 s_out1 s_gr0 s_gr1").split()
    S = {n: mksem(n) for n in names}
    s_w = [S["s_wA"], S["s_wB"]]
    s_f = [S["s_fA"], S["s_fB"]]
    s_xP = [S["s_x0"], S["s_x1"]]
    s_xrP = [S["s_xr0"], S["s_xr1"]]
    s_spP = [S["s_sp0"], S["s_sp1"]]
    s_x2rP = [S["s_x2r0"], S["s_x2r1"]]
    s_outP = [S["s_out0"], S["s_out1"]]
    s_grP = [S["s_gr0"], S["s_gr1"]]
    cnt = {n: 0 for n in names}
    _sem_name = {id(S[n]): n for n in names}

    def inc(s, n=1):
        cnt[_sem_name[id(s)]] += n
        return cnt[_sem_name[id(s)]]

    def cur(s):
        return cnt[_sem_name[id(s)]]

    ops = {"sync": [], "gpsimd": [], "tensor": [], "scalar": [], "vector": []}

    def E(eng, fn, drain=False):
        ops[eng].append((fn, drain))

    def W(eng, s, v):
        E(eng, lambda e, s=s, v=v: e.wait_ge(s, v))

    def dma(eng, out_ap, in_ap, s):
        v = inc(s, 16)
        E(eng, lambda e, o=out_ap, i=in_ap, ss=s: e.dma_start(out=o, in_=i).then_inc(ss, 16))
        return v

    x_tiled = x_d.rearrange("(n p) d -> n p d", p=P)
    x2s_tiled = x2s_d.rearrange("(n p) d -> n p d", p=P)
    out_tiled = out_d.rearrange("(n p) d -> n p d", p=P)

    TBL_LNEXP = _act_table_idx(nc, "natural_log_exp_and_others")
    TBL_SIG = _act_table_idx(nc, "sigmoid_and_others")
    TBL_SILU = _act_table_idx(nc, "silu_and_others")

    def load_table(idx):
        E("scalar", lambda e, idx=idx: e.add_instruction(
            mybir.InstLoadActFuncSet(name=nc.get_next_instruction_name(),
                                     act_func_set_id=idx, ins=[], outs=[])))

    load_table(TBL_LNEXP)
    dma("sync", onesf[:], ones_d[:, :], S["s_c"])
    dma("sync", idnf[:], idn_d[:, :], S["s_c"])
    dma("sync", wlv[:], wl_d[:, :], S["s_c"])
    dma("sync", wrv[:], wr_d[:, :], S["s_c"])
    dma("sync", eb16[:], eb16_d[:, :], S["s_c"])
    dma("gpsimd", onesb[:], ones_d[:, :], S["s_cg"])
    dma("gpsimd", idnb[:], idn_d[:, :], S["s_cg"])
    for c in range(NCD):
        dma("gpsimd", rwt_sb[c][:], rwt_d[c * P:(c + 1) * P, :], S["s_cg"])
    c_all = cur(S["s_c"])
    cg_all = cur(S["s_cg"])

    E("vector", lambda e: e.memset(epsv[:], float(EPS)))
    mv = inc(S["s_v"])
    E("vector", lambda e, v=mv: e.memset(tinyv[:], 1e-30).then_inc(S["s_v"], 1))
    eps_ready = mv
    W("scalar", S["s_v"], eps_ready)
    W("vector", S["s_c"], c_all)
    W("vector", S["s_cg"], cg_all)
    E("vector", lambda e: e.tensor_mul(wprod[:], wlv[:], wrv[:]), drain=True)

    def rsqrt_chain(var_ap, r_ap, t1_ap, t2_ap, ready_v):
        W("scalar", S["s_v"], ready_v)
        E("scalar", lambda e, a=var_ap, o=t1_ap: e.activation(o, a, ACT.Ln, bias=epsv[:]))
        sv = inc(S["s_sc"])
        E("scalar", lambda e, a=t1_ap, o=r_ap, v=sv: e.activation(
            o, a, ACT.Exp, scale=-0.5).then_inc(S["s_sc"], 1))
        W("vector", S["s_sc"], sv)

    def ln_stats(src_ap, i, pfx):
        s_, ss_, m_, r_ = (stats[f"s_{pfx}"], stats[f"ss_{pfx}"],
                           stats[f"m_{pfx}"], stats[f"r_{pfx}"])
        E("vector", lambda e, i=i: e.reduce_sum(s_[:, i:i + 1], src_ap, axis=AX), drain=True)
        sv = inc(S["s_sc"])
        E("scalar", lambda e, i=i, v=sv: e.activation(
            sqscr[:], src_ap, ACT.Square, accum_out=ss_[:, i:i + 1]).then_inc(S["s_sc"], 1))
        W("vector", S["s_sc"], sv)
        E("vector", lambda e, i=i: e.tensor_scalar(
            m_[:, i:i + 1], s_[:, i:i + 1], 1.0 / D, None, ALU.mult), drain=True)
        E("vector", lambda e, i=i: e.tensor_scalar(
            stats["tmpa"][:, i:i + 1], ss_[:, i:i + 1], 1.0 / D, None, ALU.mult), drain=True)
        E("vector", lambda e, i=i: e.tensor_mul(
            stats["tmpb"][:, i:i + 1], m_[:, i:i + 1], m_[:, i:i + 1]), drain=True)
        vv = inc(S["s_v"])
        E("vector", lambda e, i=i, v=vv: e.tensor_sub(
            stats["tmpa"][:, i:i + 1], stats["tmpa"][:, i:i + 1],
            stats["tmpb"][:, i:i + 1]).then_inc(S["s_v"], 1))
        rsqrt_chain(stats["tmpa"][:, i:i + 1], r_[:, i:i + 1],
                    stats["tmpb"][:, i:i + 1], stats["tmpc"][:, i:i + 1], vv)

    red_v = []
    sq_v = []
    for i in range(NT):
        if i >= 2:
            W("sync", S["s_v"], red_v[i - 2])
            W("sync", S["s_sc"], sq_v[i - 2])
        dma("sync", x_r[i % 2][:], x_tiled[i, :, :], s_xP[i % 2])
        W("vector", s_xP[i % 2], 16 * (i // 2 + 1))
        W("scalar", s_xP[i % 2], 16 * (i // 2 + 1))
        rv = inc(S["s_v"])
        E("vector", lambda e, i=i, v=rv: e.reduce_sum(
            stats["s_ln1"][:, i:i + 1], x_r[i % 2][:], axis=AX).then_inc(S["s_v"], 1),
          drain=True)
        red_v.append(rv)
        sv = inc(S["s_sc"])
        E("scalar", lambda e, i=i, v=sv: e.activation(
            sqscr[:], x_r[i % 2][:], ACT.Square,
            accum_out=stats["ss_ln1"][:, i:i + 1]).then_inc(S["s_sc"], 1))
        sq_v.append(sv)
    W("vector", S["s_sc"], sq_v[-1])
    E("vector", lambda e: e.tensor_scalar(
        stats["m_ln1"][:], stats["s_ln1"][:], 1.0 / D, None, ALU.mult), drain=True)
    E("vector", lambda e: e.tensor_scalar(
        stats["tmpa"][:], stats["ss_ln1"][:], 1.0 / D, None, ALU.mult), drain=True)
    E("vector", lambda e: e.tensor_mul(
        stats["tmpb"][:], stats["m_ln1"][:], stats["m_ln1"][:]), drain=True)
    bv1 = inc(S["s_v"])
    E("vector", lambda e, v=bv1: e.tensor_sub(
        stats["tmpa"][:], stats["tmpa"][:], stats["tmpb"][:]).then_inc(S["s_v"], 1))
    rsqrt_chain(stats["tmpa"][:], stats["r_ln1"][:], stats["tmpb"][:],
                stats["tmpc"][:], bv1)
    ln1_rdy = cur(S["s_sc"])

    apply_v = []
    tr_after_tile = []
    for i in range(NT):
        if i == 0:
            W("sync", S["s_v"], red_v[-1])
            W("sync", S["s_sc"], sq_v[-1])
        else:
            pass
        if i >= 2:
            W("sync", S["s_v"], apply_v[i - 2])
        dma("sync", x_r[i % 2][:], x_tiled[i, :, :], s_xP[i % 2])
        W("vector", s_xP[i % 2], 16 * (NT // 2 + i // 2 + 1))
        if i == 0:
            W("vector", S["s_sc"], ln1_rdy)
        av = inc(S["s_v"])
        E("vector", lambda e, i=i, v=av: e.tensor_scalar(
            n1_tm[i][:], x_r[i % 2][:], stats["m_ln1"][:, i:i + 1],
            stats["r_ln1"][:, i:i + 1], ALU.subtract, ALU.mult).then_inc(S["s_v"], 1))
        apply_v.append(av)
        W("tensor", S["s_v"], av)
        if i == 0:
            W("tensor", S["s_c"], c_all)
            W("tensor", S["s_cg"], cg_all)
        for c in range(NCD):
            k = i * NCD + c
            if k >= 2:
                W("tensor", S["s_trd"], k - 1)
            tv = inc(S["s_tr"])
            E("tensor", lambda e, i=i, c=c, k=k: e.transpose(
                ntrB[k % 2][:], n1_tm[i][:, c * P:(c + 1) * P], idnb[:]).then_inc(S["s_tr"], 1))
            W("vector", S["s_tr"], tv)
            dv = inc(S["s_trd"])
            E("vector", lambda e, i=i, c=c, k=k: e.tensor_copy(
                n1_fm[c][:, i * P:(i + 1) * P], ntrB[k % 2][:]).then_inc(S["s_trd"], 1))
        tr_after_tile.append(cur(S["s_trd"]))

    E("vector", lambda e: e.drain())
    afm_last = None
    for c in range(NCD):
        afm_last = inc(S["s_v"])
        E("vector", lambda e, c=c, v=afm_last: e.tensor_scalar(
            a_fm[c][:], n1_fm[c][:], wprod[:, c:c + 1], None,
            ALU.mult).then_inc(S["s_v"], 1), drain=True)

    sq_buf = P_T[0]
    NS5 = max(T // 512, 1)
    W5 = min(512, T)
    brow_base = cur(S["s_st"])
    for c in range(NCD):
        if c >= 1:
            W("vector", S["s_st"], brow_base + c)
        sqv = inc(S["s_v"])
        E("vector", lambda e, c=c, v=sqv: e.tensor_mul(
            sq_buf[:, :T], a_fm[c][:], a_fm[c][:]).then_inc(S["s_v"], 1), drain=True)
        W("tensor", S["s_v"], sqv)
        for s5 in range(NS5):
            last = (c == NCD - 1)
            stv = inc(S["s_st"]) if s5 == NS5 - 1 else None
            def mk(c=c, s5=s5, last=last, fin=stv is not None):
                def f(e):
                    inst = e.matmul(browps[s5][:, :W5], onesb[:, 0:1],
                                    sq_buf[:, s5 * 512:s5 * 512 + W5],
                                    start=(c == 0), stop=last, skip_group_check=True)
                    if fin:
                        inst.then_inc(S["s_st"], 1)
                return f
            E("tensor", mk())
    brow_done = cur(S["s_st"])
    W("scalar", S["s_st"], brow_done)
    negb_v = None
    for s5 in range(NS5):
        if negb_v is not None:
            W("scalar", S["s_v"], negb_v)
        E("scalar", lambda e, s5=s5: e.activation(
            scratch_row[:, :W5], browps[s5][:, :W5], ACT.Ln, bias=tinyv[0:1, :]))
        sv = inc(S["s_sc"])
        E("scalar", lambda e, s5=s5, v=sv: e.activation(
            scratch_row[:, :W5], scratch_row[:, :W5], ACT.Exp,
            scale=0.5).then_inc(S["s_sc"], 1))
        W("vector", S["s_sc"], sv)
        negb_v = inc(S["s_v"])
        E("vector", lambda e, s5=s5, v=negb_v: e.tensor_scalar(
            negb_row[:, s5 * 512:s5 * 512 + W5], scratch_row[:, :W5], -32.0, None,
            ALU.mult).then_inc(S["s_v"], 1))

    if debug:
        W("gpsimd", S["s_v"], negb_v)
        for i in range(NT):
            dma("gpsimd", dbg["dbg_n1"].rearrange("(n p) d -> n p d", p=P)[i, :, :],
                n1_tm[i][:], S["s_dbg"])

    exp_done_at_block = []
    gi_tr = []
    ao_done_v = None

    ntr_pending = []
    ntr_tr_done = {}
    ntr_kc = [0]

    def emit_ntr_one():
        gi, n2v = ntr_pending.pop(0)
        W("tensor", S["s_v"], n2v)
        for c in range(NCD):
            kc = ntr_kc[0]
            ntr_kc[0] += 1
            if kc >= 2:
                W("tensor", S["s_trd"], cur(S["s_trd"]) - 1)
            tv = inc(S["s_tr"])
            E("tensor", lambda e, gi=gi, c=c, kc=kc: e.transpose(
                ntrC[kc % 2][:], n2tm[gi % 3][:, c * P:(c + 1) * P],
                idnb[:]).then_inc(S["s_tr"], 1))
            W("vector", S["s_tr"], tv)
            dv = inc(S["s_trd"])
            E("vector", lambda e, gi=gi, c=c, kc=kc: e.tensor_copy(
                n2_fm[c][:, gi * P:(gi + 1) * P], ntrC[kc % 2][:]).then_inc(S["s_trd"], 1))
        ntr_tr_done[gi] = cur(S["s_tr"])
        gi_tr.append(cur(S["s_trd"]))

    def emit_ST(b):
        q0 = b * QB
        buf = P_T[b % 2]
        for it in range(NT):
            k = b * NT + it
            if b == 0 and it == 0:
                W("tensor", S["s_v"], afm_last)
            if k >= 2:
                W("tensor", S["s_exp"], k - 1)
            for c in range(NCD):
                E("tensor", lambda e, c=c, it=it, q0=q0, k=k: e.matmul(
                    stp[k % 2][:], a_fm[c][:, it * P:(it + 1) * P],
                    n1_fm[c][:, q0:q0 + QB], start=(c == 0), stop=False,
                    skip_group_check=True))
            if b == 0 and it == 0:
                W("tensor", S["s_v"], negb_v)
            stv = inc(S["s_st"])
            E("tensor", lambda e, q0=q0, k=k, v=stv: e.matmul(
                stp[k % 2][:], onesb[0:1, :], negb_row[:, q0:q0 + QB],
                start=False, stop=True, skip_group_check=True).then_inc(S["s_st"], 1))
            W("scalar", S["s_st"], stv)
            ev = inc(S["s_exp"])
            E("scalar", lambda e, it=it, k=k, buf=buf, v=ev: e.activation(
                buf[:, it * QB:(it + 1) * QB], stp[k % 2][:], ACT.Exp).then_inc(S["s_exp"], 1))

    def emit_AO(b):
        nonlocal ao_done_v
        buf = P_T[b % 2]
        exp_need = exp_done_at_block[b]
        for qt in range(QT_PER_B):
            gi = b * QT_PER_B + qt
            W("tensor", S["s_exp"], exp_need)
            if gi > 0:
                W("tensor", S["s_ep"], cur(S["s_ep"]))
                W("tensor", S["s_rec"], cur(S["s_rec"]))
            for it in range(NT):
                st0 = buf[:, it * QB + qt * P: it * QB + (qt + 1) * P]
                first = it == 0
                last = it == NT - 1
                E("tensor", lambda e, st0=st0, it=it, first=first, last=last: e.matmul(
                    aop[:, 0:512], st0, n1_tm[it][:, 0:512],
                    start=first, stop=last, skip_group_check=True))
                E("tensor", lambda e, st0=st0, it=it, first=first, last=last: e.matmul(
                    aop[:, 512:1024], st0, n1_tm[it][:, 512:1024],
                    start=first, stop=last, skip_group_check=True))
                aov = inc(S["s_ao"]) if last else None
                def mkd(st0=st0, it=it, first=first, last=last, fin=aov is not None):
                    def f(e):
                        inst = e.matmul(denp[:], st0, onesb[:, 0:1],
                                        start=first, stop=last, skip_group_check=True)
                        if fin:
                            inst.then_inc(S["s_ao"], 1)
                    return f
                E("tensor", mkd())
            aov = cur(S["s_ao"])
            ao_done_v = aov
            if len(ntr_pending) >= 2:
                emit_ntr_one()
            W("vector", S["s_ao"], aov)
            rv = inc(S["s_rec"])
            E("vector", lambda e, gi=gi, v=rv: e.reciprocal(
                stats["den_r"][:, gi % NT:gi % NT + 1], denp[:]).then_inc(S["s_rec"], 1), drain=True)
            epv = inc(S["s_ep"])
            E("vector", lambda e, gi=gi, v=epv: e.tensor_scalar(
                y1pre[:], aop[:], stats["den_r"][:, gi % NT:gi % NT + 1], None,
                ALU.mult).then_inc(S["s_ep"], 1), drain=True)
            W("scalar", S["s_ep"], epv)
            ln_stats(y1pre[:], gi, "lna")
            E("vector", lambda e, gi=gi: e.tensor_scalar(
                y1pre[:], y1pre[:], stats["m_lna"][:, gi:gi + 1],
                stats["r_lna"][:, gi:gi + 1], ALU.subtract, ALU.mult), drain=True)
            W("vector", s_xrP[gi % 2], 16 * (gi // 2 + 1))
            if gi >= 2:
                W("vector", s_spP[gi % 2], 16 * (gi // 2))
            x2v = inc(S["s_x2w"])
            E("vector", lambda e, gi=gi, v=x2v: e.tensor_add(
                x2t[gi % 2][:], x_r[gi % 2][:], y1pre[:]).then_inc(S["s_x2w"], 1), drain=True)
            W("sync", S["s_x2w"], x2v)
            dma("sync", x2s_tiled[gi, :, :], x2t[gi % 2][:], s_spP[gi % 2])
            W("scalar", S["s_x2w"], x2v)
            ln_stats(x2t[gi % 2][:], gi, "ln2")
            if gi >= 3:
                W("vector", S["s_tr"], ntr_tr_done[gi - 3])
            n2v = inc(S["s_v"])
            E("vector", lambda e, gi=gi, v=n2v: e.tensor_scalar(
                n2tm[gi % 3][:], x2t[gi % 2][:], stats["m_ln2"][:, gi:gi + 1],
                stats["r_ln2"][:, gi:gi + 1], ALU.subtract, ALU.mult).then_inc(S["s_v"], 1))
            ntr_pending.append((gi, n2v))

    emit_ST(0)
    exp_done_at_block.append(cur(S["s_exp"]))
    for b in range(NQB):
        if b + 1 < NQB:
            emit_ST(b + 1)
            exp_done_at_block.append(cur(S["s_exp"]))
        for qt in range(QT_PER_B):
            gi = b * QT_PER_B + qt
            W("sync", S["s_v"], apply_v[-1])
            if gi >= 2:
                W("sync", S["s_x2w"], gi - 1)
            dma("sync", x_r[gi % 2][:], x_tiled[gi, :, :], s_xrP[gi % 2])
        emit_AO(b)

    while ntr_pending:
        emit_ntr_one()
    n2_all_v = cur(S["s_trd"])
    sp_all = (cur(S["s_sp0"]), cur(S["s_sp1"]))
    all_exp_done = exp_done_at_block[-1]

    if debug:
        W("gpsimd", S["s_trd"], n2_all_v)
        for c in range(NCD):
            dma("gpsimd", dbg["dbg_n2fm"].rearrange("(c p) t -> c p t", p=P)[c, :, :],
                n2_fm[c][:], S["s_dbg"])
        W("gpsimd", S["s_sp0"], sp_all[0])
        W("gpsimd", S["s_sp1"], sp_all[1])
        for i in range(NT):
            dma("gpsimd", dbg["dbg_x2"].rearrange("(n p) d -> n p d", p=P)[i, :, :],
                x2s_tiled[i, :, :], S["s_dbg"])

    load_table(TBL_SIG)
    for qb in range(NS5):
        if qb == 0:
            W("tensor", S["s_trd"], n2_all_v)
        for c in range(NCD):
            last = c == NCD - 1
            stv = inc(S["s_st"]) if last else None
            def mk(c=c, qb=qb, last=last, fin=stv is not None):
                def f(e):
                    inst = e.matmul(scoresp[:, :W5], rwt_sb[c][:],
                                    n2_fm[c][:, qb * 512:qb * 512 + W5],
                                    start=(c == 0), stop=last, skip_group_check=True)
                    if fin:
                        inst.then_inc(S["s_st"], 1)
                return f
            E("tensor", mk())
        stv = cur(S["s_st"])
        if qb == 0:
            W("scalar", S["s_sp0"], sp_all[0])
            W("scalar", S["s_sp1"], sp_all[1])
        W("scalar", S["s_st"], stv)
        sv = inc(S["s_sc"])
        E("scalar", lambda e, qb=qb, v=sv: e.activation(
            scores_fm[:, qb * 512:qb * 512 + W5], scoresp[:, :W5],
            ACT.Sigmoid).then_inc(S["s_sc"], 1))
        if qb + 1 < NS5:
            W("tensor", S["s_sc"], sv)
    sig_done = cur(S["s_sc"])

    d0b_q = []

    def Q(fn):
        d0b_q.append(fn)

    state = {}

    def _qi_bias():
        W("tensor", S["s_c"], c_all)
        W("tensor", S["s_cg"], cg_all)
        btv = inc(S["s_st"])
        E("tensor", lambda e, v=btv: e.matmul(
            d0b_ps[:, 0:NT * NEXP], onesf[0:1, :], eb16[:, :], start=True,
            stop=True, skip_group_check=True).then_inc(S["s_st"], 1))
        W("vector", S["s_st"], btv)
        bv = inc(S["s_v"])
        E("vector", lambda e, v=bv: e.tensor_copy(
            bias_rep[:], d0b_ps[:, 0:NT * NEXP]).then_inc(S["s_v"], 1), drain=True)
        state["bias_done"] = bv
    Q(_qi_bias)

    def _qi_str(i):
        def f():
            W("tensor", S["s_v"], state["bias_done"])
            W("tensor", S["s_trd"], cur(S["s_trd"]))
            if i == 0:
                W("tensor", S["s_sc"], sig_done)
            tv = inc(S["s_tr"])
            E("tensor", lambda e, i=i: e.transpose(
                d0b_ps[:, 0:NEXP].rearrange("p e -> p e") if False else d0b_ps[:, 0:NEXP],
                scores_fm[:, i * P:(i + 1) * P],
                idnf[0:NEXP, 0:NEXP]).then_inc(S["s_tr"], 1))
            W("vector", S["s_tr"], tv)
            inc(S["s_trd"])
            E("vector", lambda e, i=i: e.tensor_copy(
                scores_tm[:, i * NEXP:(i + 1) * NEXP],
                d0b_ps[:, 0:NEXP]).then_inc(S["s_trd"], 1))
        return f
    for i in range(NT):
        Q(_qi_str(i))

    def _qi_top2_head():
        W("vector", S["s_trd"], cur(S["s_trd"]))
        E("vector", lambda e: e.drain())
        E("vector", lambda e: e.tensor_add(v_all[:], scores_tm[:], bias_rep[:]),
          drain=True)
    Q(_qi_top2_head)

    def _qi_top2(i):
        def f():
            sl = slice(i * NEXP, (i + 1) * NEXP)
            E("vector", lambda e, sl=sl, i=i: e.reduce_max(
                m1_all[:, i:i + 1], v_all[:, sl], axis=AX), drain=True)
            E("vector", lambda e, sl=sl, i=i: e.tensor_scalar(
                e1_all[:, sl], v_all[:, sl], m1_all[:, i:i + 1], None,
                ALU.is_equal), drain=True)
            E("vector", lambda e, sl=sl: e.scalar_tensor_tensor(
                v2_all[:, sl], e1_all[:, sl], -1e30, v_all[:, sl], ALU.mult,
                ALU.add), drain=True)
            E("vector", lambda e, sl=sl, i=i: e.reduce_max(
                m2_all[:, i:i + 1], v2_all[:, sl], axis=AX), drain=True)
            E("vector", lambda e, sl=sl, i=i: e.tensor_scalar(
                e2_all[:, sl], v2_all[:, sl], m2_all[:, i:i + 1], None,
                ALU.is_equal), drain=True)
        return f
    for i in range(NT):
        Q(_qi_top2(i))

    def _qi_gates():
        E("vector", lambda e: e.tensor_add(e1_all[:], e1_all[:], e2_all[:]),
          drain=True)
        gv = inc(S["s_v"])
        E("vector", lambda e, v=gv: e.tensor_mul(
            gates_all[:], e1_all[:], scores_tm[:]).then_inc(S["s_v"], 1),
          drain=True)
        state["gv"] = gv
        if debug:
            W("gpsimd", S["s_v"], gv)
            dma("gpsimd", dbg["dbg_gates"][:, :], gates_all[:], S["s_dbg"])
    Q(_qi_gates)

    def _qi_gtr(i):
        def f():
            W("tensor", S["s_v"], state["gv"])
            W("tensor", S["s_trd"], cur(S["s_trd"]))
            tv = inc(S["s_tr"])
            E("tensor", lambda e, i=i: e.transpose(
                d0b_ps[0:NEXP, 0:P], gates_all[:, i * NEXP:(i + 1) * NEXP],
                idnf[:]).then_inc(S["s_tr"], 1))
            W("vector", S["s_tr"], tv)
            inc(S["s_trd"])
            E("vector", lambda e, i=i: e.tensor_copy(
                gates_fm[:, i * P:(i + 1) * P],
                d0b_ps[0:NEXP, 0:P]).then_inc(S["s_trd"], 1))
        return f
    for i in range(NT):
        Q(_qi_gtr(i))

    def _qi_bounce():
        W("sync", S["s_trd"], cur(S["s_trd"]))
        dma("sync", gates_hbm[:, :], gates_fm[:], S["s_gh"])
    Q(_qi_bounce)

    load_table(TBL_SILU)
    base_chunks = [("shared", k) for k in range(4)] + [("expert", e) for e in range(NEXP)]
    chunks = [(p, kind, idx) for p in range(NPASS) for (kind, idx) in base_chunks]
    NCHUNK_P = len(base_chunks)

    chunk_w_target = [None] * len(chunks)

    def emit_stream(pss0):
        if pss0 == 0:
            W("gpsimd", S["s_exp"], all_exp_done)
        for ci in range(pss0 * NCHUNK_P, (pss0 + 1) * NCHUNK_P):
            _, kind, idx = chunks[ci]
            sl = ci % 2
            if ci == 1:
                W("gpsimd", S["s_ao"], ao_done_v)
                if debug:
                    W("gpsimd", S["s_dbg"], cur(S["s_dbg"]))
            if ci >= 2:
                W("gpsimd", S["s_s2"], (ci - 1) * NTB * NCD * TT_PER_TB)
            w12_t = slotA_w12 if sl == 0 else slotB_w12
            w3_t = slotA_w3 if sl == 0 else slotB_w3
            if kind == "shared":
                k = idx
                for c in range(NCD):
                    dma("gpsimd", w12_t[c][:, 0:RH],
                        w1t_d[c * P:(c + 1) * P, k * RH:(k + 1) * RH], s_w[sl])
                    dma("gpsimd", w12_t[c][:, RH:2 * RH],
                        w2t_d[c * P:(c + 1) * P, k * RH:(k + 1) * RH], s_w[sl])
                for j in range(NCD):
                    dma("gpsimd", w3_t[j][:],
                        w3st_d[k * RH + j * P: k * RH + (j + 1) * P, :], s_w[sl])
            else:
                ei = idx
                for c in range(NCD):
                    dma("gpsimd", w12_t[c][:], w12t_d[ei, c * P:(c + 1) * P, :], s_w[sl])
                for j in range(NCD):
                    dma("gpsimd", w3_t[j][:], w3t_d[ei, j * P:(j + 1) * P, :], s_w[sl])
            chunk_w_target[ci] = cur(s_w[sl])

    jj = 0
    gate_load_cnt = [0]
    acc_done_pass = []

    def emit_pass(pss, mid_hook=None, lastck_hook=None):
        nonlocal jj
        S2_DEPTH = 3
        s2_pending = []
        acc_q = []

        def emit_s2_fix(p):
            (jj_, j_, w3t_, first_acc_wait, _unused_v, s2v, accs) = p
            for tt in range(TT_PER_TB):
                for half in range(2):
                    if first_acc_wait:
                        W("tensor", S["s_acc"], acc_free.get((tt, half), 0))
                    lastj = j_ == NCD - 1
                    fin = half == 1
                    def mk2(tt=tt, half=half, jj_=jj_, j_=j_, w3t_=w3t_,
                            lastj=lastj, fin=fin, s2v=s2v):
                        def f(e):
                            inst = e.matmul(
                                outp[tt][:, half * 512:(half + 1) * 512],
                                Gt[jj_ % 4][:, tt * P:(tt + 1) * P],
                                w3t_[j_][:, half * 512:(half + 1) * 512],
                                start=(j_ == 0), stop=lastj, skip_group_check=True)
                            if fin:
                                inst.then_inc(S["s_s2"], 1)
                        return f
                    E("tensor", mk2())
            if accs is not None:
                ci_, tb_ = accs

                def emit_accs(ci_=ci_, tb_=tb_, s2v=s2v):
                    W("vector", S["s_s2"], s2v)
                    for tt in range(TT_PER_TB):
                        ai = tb_ * TT_PER_TB + tt
                        for half in range(2):
                            av = inc(S["s_acc"])
                            acc_free[(tt, half)] = av
                            hs = slice(half * 512, (half + 1) * 512)
                            if ci_ % NCHUNK_P == 0:
                                E("vector", lambda e, tt=tt, ai=ai, hs=hs, v=av: e.tensor_copy(
                                    acc[ai][:, hs], outp[tt][:, hs]).then_inc(S["s_acc"], 1), drain=True)
                            else:
                                E("vector", lambda e, tt=tt, ai=ai, hs=hs, v=av: e.tensor_add(
                                    acc[ai][:, hs], acc[ai][:, hs], outp[tt][:, hs]).then_inc(S["s_acc"], 1), drain=True)
                        if ci_ % NCHUNK_P == NCHUNK_P - 1:
                            acc_tile_ready[(ci_ // NCHUNK_P, ai)] = av
                            emit_stats_pre(ci_ // NCHUNK_P, ai)
                acc_q.append(emit_accs)

        for ck in range(NCHUNK_P):
            ci = pss * NCHUNK_P + ck
            _, kind, idx = chunks[ci]
            sl = ci % 2
            w12_t = slotA_w12 if sl == 0 else slotB_w12
            w3_t = slotA_w3 if sl == 0 else slotB_w3
            has_g = kind == "expert"
            if has_g:
                while d0b_q:
                    d0b_q.pop(0)()
            W("tensor", s_w[sl], chunk_w_target[ci])
            if ci == 0:
                W("tensor", S["s_sc"], sig_done)
            if lastck_hook is not None and ck == NCHUNK_P - 1:
                lastck_hook()
            for tb in range(NTB):
                tok0 = pss * PASS_T + tb * TBLK
                if has_g:
                    gk = gate_load_cnt[0]
                    gate_load_cnt[0] += 1
                    if gk == 0:
                        W("sync", S["s_gh"], 16)
                        W("sync", S["s_sp0"], sp_all[0])
                        W("sync", S["s_sp1"], sp_all[1])
                        W("sync", S["s_x2w"], cur(S["s_x2w"]))
                    if gk >= 2:
                        W("sync", S["s_gb"], gk - 1)
                    grv = dma("sync", grow[gk % 2][:],
                              gates_hbm[idx:idx + 1, tok0:tok0 + TBLK], s_grP[gk % 2])
                    W("tensor", s_grP[gk % 2], grv)
                    if cur(S["s_gb"]) > 0:
                        W("tensor", S["s_gbe"], cur(S["s_gbe"]))
                    gbv = inc(S["s_gb"])
                    E("tensor", lambda e, gk=gk, v=gbv: e.matmul(
                        gatesp[:], onesf[0:1, :], grow[gk % 2][:],
                        start=True, stop=True, skip_group_check=True).then_inc(S["s_gb"], 1))
                    W("vector", S["s_gb"], gbv)
                    gbev = inc(S["s_gbe"])
                    E("vector", lambda e, tb=tb, v=gbev: e.tensor_copy(
                        gate_b[tb % 2][:], gatesp[:]).then_inc(S["s_gbe"], 1), drain=True)
                for j in range(NCD):
                    if d0b_q:
                        d0b_q.pop(0)()
                    if d0b_q:
                        d0b_q.pop(0)()
                    if mid_hook is not None and ck == 0 and j == 4:
                        mid_hook(tb)
                    if jj >= 3:
                        W("tensor", S["s_ev2"], jj - 2)
                    for c in range(NCD):
                        E("tensor", lambda e, c=c, j=j, jj=jj, tok0=tok0, w12_t=w12_t: e.matmul(
                            h1p[jj % 3][:, 0:TBLK], w12_t[c][:, j * P:(j + 1) * P],
                            n2_fm[c][:, tok0:tok0 + TBLK], start=(c == 0), stop=False,
                            skip_group_check=True))
                    for c in range(NCD):
                        last = c == NCD - 1
                        s1v = inc(S["s_s1"]) if last else None
                        def mk1(c=c, j=j, jj=jj, tok0=tok0, w12_t=w12_t, last=last,
                                fin=s1v is not None):
                            def f(e):
                                inst = e.matmul(
                                    h1p[jj % 3][:, TBLK:2 * TBLK],
                                    w12_t[c][:, RH + j * P:RH + (j + 1) * P],
                                    n2_fm[c][:, tok0:tok0 + TBLK],
                                    start=False, stop=last, skip_group_check=True)
                                if fin:
                                    inst.then_inc(S["s_s1"], 1)
                            return f
                        E("tensor", mk1())
                    s1v = cur(S["s_s1"])
                    ev2 = inc(S["s_ev2"])
                    s2v = inc(S["s_s2"], TT_PER_TB)
                    first_acc_wait = (j == 0) and not (ci == 0 and tb == 0)
                    accs = (ci, tb) if j == NCD - 1 else None
                    this = (jj, j, w3_t, first_acc_wait, None, s2v, accs)
                    s2_pending.append((this, ev2))
                    if len(s2_pending) > S2_DEPTH:
                        p, pev2 = s2_pending.pop(0)
                        emit_s2_fix(p)
                    W("scalar", S["s_s1"], s1v)
                    evv = inc(S["s_ev"])
                    E("scalar", lambda e, jj=jj, v=evv: e.activation(
                        s1t[jj % 3][:], h1p[jj % 3][:, 0:TBLK],
                        ACT.Silu).then_inc(S["s_ev"], 1))
                    while acc_q:
                        acc_q.pop(0)()
                    W("vector", S["s_ev"], evv)
                    if jj >= 4:
                        W("vector", S["s_s2"], (jj - 3) * TT_PER_TB)
                    if has_g:
                        W("vector", S["s_gbe"], cur(S["s_gbe"]))
                        E("vector", lambda e, jj=jj, tb=tb: e.tensor_mul(
                            t2t[jj % 2][:], h1p[jj % 3][:, TBLK:2 * TBLK],
                            gate_b[tb % 2][:]), drain=True)
                        E("vector", lambda e, jj=jj, v=ev2: e.tensor_mul(
                            Gt[jj % 4][:], s1t[jj % 3][:],
                            t2t[jj % 2][:]).then_inc(S["s_ev2"], 1), drain=True)
                    else:
                        E("vector", lambda e, jj=jj, v=ev2: e.tensor_mul(
                            Gt[jj % 4][:], s1t[jj % 3][:],
                            h1p[jj % 3][:, TBLK:2 * TBLK]).then_inc(S["s_ev2"], 1), drain=True)
                    jj += 1
        while s2_pending:
            p, pev2 = s2_pending.pop(0)
            W("tensor", S["s_ev2"], pev2)
            emit_s2_fix(p)
        while acc_q:
            acc_q.pop(0)()

    acc_tile_ready = {}
    acc_free = {}
    stats_pre_done = set()

    def emit_stats_pre(pss, ii):
        si = pss * NACC + ii
        stats_pre_done.add((pss, ii))
        rdy_v = acc_tile_ready[(pss, ii)]
        W("vector", S["s_acc"], rdy_v)
        W("scalar", S["s_acc"], rdy_v)
        E("vector", lambda e, ii=ii, si=si: e.reduce_sum(
            stats["s_lnf"][:, si:si + 1], acc[ii][:], axis=AX), drain=True)
        sv = inc(S["s_sc"])
        E("scalar", lambda e, ii=ii, si=si, v=sv: e.activation(
            sqscr[:], acc[ii][:], ACT.Square,
            accum_out=stats["ss_lnf"][:, si:si + 1]).then_inc(S["s_sc"], 1))

    def emit_phase_e_stats(pss):
        load_table(TBL_LNEXP)
        if pss == 0:
            W("vector", S["s_sp0"], sp_all[0])
            W("vector", S["s_sp1"], sp_all[1])
        if debug and pss == 0:
            W("gpsimd", S["s_acc"], acc_done_pass[0])
            for i in range(NACC):
                dma("gpsimd", dbg["dbg_acc"].rearrange("(n p) d -> n p d", p=P)[i, :, :],
                    acc[i][:], S["s_dbg"])
        c0 = pss * NACC
        for ii in range(NACC):
            if (pss, ii) not in stats_pre_done:
                emit_stats_pre(pss, ii)
        W("vector", S["s_sc"], cur(S["s_sc"]))
        sl = slice(c0, c0 + NACC)
        E("vector", lambda e, sl=sl: e.tensor_scalar(
            stats["m_lnf"][:, sl], stats["s_lnf"][:, sl], 1.0 / D, None, ALU.mult),
          drain=True)
        E("vector", lambda e, sl=sl: e.tensor_scalar(
            stats["tmpa"][:, sl], stats["ss_lnf"][:, sl], 1.0 / D, None, ALU.mult),
          drain=True)
        E("vector", lambda e, sl=sl: e.tensor_mul(
            stats["tmpb"][:, sl], stats["m_lnf"][:, sl], stats["m_lnf"][:, sl]),
          drain=True)
        bv = inc(S["s_v"])
        E("vector", lambda e, sl=sl, v=bv: e.tensor_sub(
            stats["tmpa"][:, sl], stats["tmpa"][:, sl],
            stats["tmpb"][:, sl]).then_inc(S["s_v"], 1))
        rsqrt_chain(stats["tmpa"][:, sl], stats["r_lnf"][:, sl],
                    stats["tmpb"][:, sl], stats["tmpc"][:, sl], bv)
        return cur(S["s_sc"])

    x2r_issued = set()

    def issue_x2r(gi):
        if gi in x2r_issued or gi >= NPASS * NACC:
            return
        x2r_issued.add(gi)
        if gi < 2:
            W("sync", S["s_gh"], 16)
            W("sync", S["s_sp0"], sp_all[0])
            W("sync", S["s_sp1"], sp_all[1])
        dma("sync", x2r[gi % 2][:], x2s_tiled[gi, :, :], s_x2rP[gi % 2])

    def emit_phase_e_tile(pss, ii, rdy):
        if True:
            gi = pss * NACC + ii
            assert gi in x2r_issued, f"x2r tile {gi} not prefetched"
            si = gi
            W("vector", S["s_sc"], rdy)
            if gi >= 2:
                W("vector", s_outP[gi % 2], 16 * (gi // 2))
            E("vector", lambda e, ii=ii, si=si, gi=gi: e.tensor_scalar(
                out_sb[gi % 2][:], acc[ii][:], stats["m_lnf"][:, si:si + 1],
                stats["r_lnf"][:, si:si + 1], ALU.subtract, ALU.mult), drain=True)
            W("vector", s_x2rP[gi % 2], 16 * (gi // 2 + 1))
            otv = inc(S["s_ot"])
            E("vector", lambda e, gi=gi, v=otv: e.tensor_add(
                out_sb[gi % 2][:], out_sb[gi % 2][:], x2r[gi % 2][:]).then_inc(S["s_ot"], 1), drain=True)
            W("sync", S["s_ot"], otv)
            dma("sync", out_tiled[gi, :, :], out_sb[gi % 2][:], s_outP[gi % 2])
            issue_x2r(gi + 2)

    def emit_phase_e(pss):
        rdy = emit_phase_e_stats(pss)
        for ii in range(NACC):
            emit_phase_e_tile(pss, ii, rdy)

    for pss in range(NPASS):
        if pss > 0 and debug:
            W("vector", S["s_dbg"], cur(S["s_dbg"]))
        emit_stream(pss)
        if pss > 0 and not DISABLE_HOOK:
            prev = pss - 1
            rdy = emit_phase_e_stats(prev)
            load_table(TBL_SILU)

            def hook(tb, prev=prev, rdy=rdy):
                for ii in range(tb * TT_PER_TB, (tb + 1) * TT_PER_TB):
                    emit_phase_e_tile(prev, ii, rdy)
            emit_pass(pss, mid_hook=hook)
        elif pss > 0:
            emit_phase_e(pss - 1)
            load_table(TBL_SILU)
            emit_pass(pss)
        else:
            emit_pass(pss, lastck_hook=lambda: (issue_x2r(0), issue_x2r(1)))
        acc_done_pass.append(cur(S["s_acc"]))
        if pss == NPASS - 1:
            emit_phase_e(pss)

    W("sync", S["s_out0"], 16 * ((NT + 1) // 2))
    W("sync", S["s_out1"], 16 * (NT // 2))
    if debug:
        W("gpsimd", S["s_dbg"], cur(S["s_dbg"]))

    with nc.Block() as block:
        @block.sync
        def _(eng):
            for f, dr in ops["sync"]:
                f(eng)

        @block.gpsimd
        def _(eng):
            for f, dr in ops["gpsimd"]:
                f(eng)

        @block.tensor
        def _(eng):
            for f, dr in ops["tensor"]:
                f(eng)

        @block.scalar
        def _(eng):
            for f, dr in ops["scalar"]:
                f(eng)
                eng.drain()

        @block.vector
        def _(eng):
            for f, dr in ops["vector"]:
                f(eng)
                if dr:
                    eng.drain()

    for g in reversed(sem_guards):
        g.__exit__(None, None, None)
    return nc


def _prep_inputs(L0_attn_x, L0_attn_w_lhs, L0_attn_w_rhs, L0_ffn_w12, L0_ffn_w3,
                 L0_ffn_shared_w1, L0_ffn_shared_w2, L0_ffn_shared_w3,
                 L0_ffn_router_w, L0_ffn_expert_bias, NT=16):
    T = NT * P
    x = np.ascontiguousarray(np.asarray(L0_attn_x, dtype=np.float32))
    w1t = np.ascontiguousarray(np.asarray(L0_ffn_shared_w1, np.float32).T)
    w2t = np.ascontiguousarray(np.asarray(L0_ffn_shared_w2, np.float32).T)
    w3st = np.ascontiguousarray(np.asarray(L0_ffn_shared_w3, np.float32).T)
    w12t = np.ascontiguousarray(np.asarray(L0_ffn_w12, np.float32).transpose(0, 2, 1))
    w3t = np.ascontiguousarray(np.asarray(L0_ffn_w3, np.float32).transpose(0, 2, 1))
    rwt = np.ascontiguousarray(np.asarray(L0_ffn_router_w, np.float32).T)
    wl = np.ascontiguousarray(np.asarray(L0_attn_w_lhs, np.float32).reshape(NCD, P).T)
    wr = np.ascontiguousarray(np.asarray(L0_attn_w_rhs, np.float32).reshape(NCD, P).T)
    eb = np.asarray(L0_ffn_expert_bias, np.float32).reshape(1, NEXP)
    eb16 = np.ascontiguousarray(np.tile(eb, (1, NT)))
    ones = np.ones((P, P), dtype=np.float32)
    idn = np.eye(P, dtype=np.float32)
    in_maps = []
    for i in range(x.shape[0]):
        in_maps.append({
            "x": np.ascontiguousarray(x[i, :T]),
            "wl": wl, "wr": wr, "w1t": w1t, "w2t": w2t, "w3st": w3st,
            "w12t": w12t, "w3t": w3t, "rwt": rwt, "eb16": eb16,
            "onesc": ones, "idnc": idn,
        })
    return in_maps


def kernel(L0_attn_x, L0_attn_w_lhs, L0_attn_w_rhs, L0_ffn_w12, L0_ffn_w3,
           L0_ffn_shared_w1, L0_ffn_shared_w2, L0_ffn_shared_w3,
           L0_ffn_router_w, L0_ffn_expert_bias):
    NT = 16
    in_maps = _prep_inputs(L0_attn_x, L0_attn_w_lhs, L0_attn_w_rhs, L0_ffn_w12,
                           L0_ffn_w3, L0_ffn_shared_w1, L0_ffn_shared_w2,
                           L0_ffn_shared_w3, L0_ffn_router_w,
                           L0_ffn_expert_bias, NT=NT)
    nc = build_program(NT=NT, debug=False)
    res = run_bass_kernel_spmd(nc, in_maps, core_ids=list(range(len(in_maps))))
    out = np.stack([np.asarray(r["out"], dtype=np.float32) for r in res.results])
    return out


# version 15
# speedup vs baseline: 1.0593x; 1.0021x over previous
import numpy as np
import concourse.bass as bass
import concourse.mybir as mybir
from concourse.bass_utils import run_bass_kernel_spmd

F32 = mybir.dt.float32
BF16 = mybir.dt.bfloat16
AX = mybir.AxisListType.X
ALU = mybir.AluOpType
ACT = mybir.ActivationFunctionType

P = 128
D = 1024
NCD = D // P
H_SH = 4096
NEXP = 8
RH = 1024
EPS = 1e-5
KB = 1024


def _act_table_idx(nc, set_name):
    from concourse.hw_specs import get_activation_tables
    for idx, name in enumerate(get_activation_tables(nc.m.arch).keys()):
        if name == set_name:
            return idx
    raise KeyError(set_name)


def build_program(NT=16, debug=False, pass_t=None, DISABLE_HOOK=False):
    T = NT * P
    QB = min(512, T)
    NQB = T // QB
    QT_PER_B = QB // P
    TBLK = min(256, T)
    PASS_T = pass_t if pass_t is not None else min(1024, T)
    NPASS = T // PASS_T
    NTB = PASS_T // TBLK
    TT_PER_TB = TBLK // P
    NACC = PASS_T // P

    nc = bass.Bass("TRN2", target_bir_lowering=False)

    x_d = nc.dram_tensor("x", [T, D], F32, kind="ExternalInput").ap()
    wl_d = nc.dram_tensor("wl", [P, NCD], F32, kind="ExternalInput").ap()
    wr_d = nc.dram_tensor("wr", [P, NCD], F32, kind="ExternalInput").ap()
    w1t_d = nc.dram_tensor("w1t", [D, H_SH], F32, kind="ExternalInput").ap()
    w2t_d = nc.dram_tensor("w2t", [D, H_SH], F32, kind="ExternalInput").ap()
    w3st_d = nc.dram_tensor("w3st", [H_SH, D], F32, kind="ExternalInput").ap()
    w12t_d = nc.dram_tensor("w12t", [NEXP, D, 2 * RH], F32, kind="ExternalInput").ap()
    w3t_d = nc.dram_tensor("w3t", [NEXP, RH, D], F32, kind="ExternalInput").ap()
    rwt_d = nc.dram_tensor("rwt", [D, NEXP], F32, kind="ExternalInput").ap()
    eb16_d = nc.dram_tensor("eb16", [1, NT * NEXP], F32, kind="ExternalInput").ap()
    ones_d = nc.dram_tensor("onesc", [P, P], F32, kind="ExternalInput").ap()
    idn_d = nc.dram_tensor("idnc", [P, P], F32, kind="ExternalInput").ap()
    x2s_d = nc.dram_tensor("x2s", [T, D], F32, kind="Internal").ap()
    gates_hbm = nc.dram_tensor("gates_hbm", [NEXP, T], F32, kind="Internal").ap()
    out_d = nc.dram_tensor("out", [T, D], F32, kind="ExternalOutput").ap()

    dbg = {}
    if debug:
        for name, shape in [("dbg_n1", [T, D]), ("dbg_n2fm", [D, T]),
                            ("dbg_gates", [P, NT * NEXP]), ("dbg_acc", [PASS_T, D]),
                            ("dbg_negb", [1, T]), ("dbg_x2", [T, D])]:
            dbg[name] = nc.dram_tensor(name, shape, F32, kind="ExternalOutput").ap()

    ARENA_B = 207 * KB
    _arena = nc.alloc_sbuf_tensor("arena0", [P, ARENA_B], mybir.dt.uint8)
    SB_BASE = nc.lookup_mloc(_arena).addr

    def sb_at(name, shape, dtype, off):
        assert 0 <= off < ARENA_B, (name, off)
        return nc.alloc_sbuf_tensor_at(name, shape, dtype, offset=SB_BASE + off)

    n1_tm = [sb_at(f"n1tm{i}", [P, D], BF16, i * 2 * KB) for i in range(NT)]
    n1_fm = [sb_at(f"n1fm{c}", [P, T], BF16, 32 * KB + c * (2 * T)) for c in range(NCD)]
    a_fm = [sb_at(f"afm{c}", [P, T], BF16, 64 * KB + c * (2 * T)) for c in range(NCD)]
    ptb = NT * QB * 2
    P_T = [sb_at(f"pt{b}", [P, NT * QB], BF16, 96 * KB + b * ptb) for b in range(2)]
    assert 2 * ptb <= 32 * KB
    n2_fm = [sb_at(f"n2fm{c}", [P, T], BF16, 128 * KB + c * (2 * T)) for c in range(NCD)]

    slotB_w12 = [sb_at(f"sBw12_{c}", [P, 2 * RH], BF16, c * 4 * KB) for c in range(NCD)]
    slotA_w12 = [sb_at(f"sAw12_{c}", [P, 2 * RH], BF16, 32 * KB + c * 4 * KB)
                 for c in range(NCD)]
    slotA_w3 = [sb_at(f"sAw3_{j}", [P, D], BF16, 64 * KB + j * 2 * KB) for j in range(NCD)]
    slotB_w3 = [sb_at(f"sBw3_{j}", [P, D], BF16, 80 * KB + j * 2 * KB) for j in range(NCD)]
    acc = [sb_at(f"acc{i}", [P, D], F32, 96 * KB + i * 4 * KB) for i in range(NACC)]
    assert NACC * 4 * KB <= 32 * KB

    t1 = 160 * KB
    x_r = [sb_at(f"xr{b}", [P, D], F32, t1 + b * 4 * KB) for b in range(2)]
    x2t = [sb_at(f"x2t{b}", [P, D], F32, t1 + 8 * KB + b * 4 * KB) for b in range(2)]
    y1pre = sb_at("y1p", [P, D], F32, t1 + 16 * KB)
    n2tm = [sb_at(f"n2tm{b}", [P, D], BF16, t1 + 20 * KB + b * 2 * KB)
            for b in range(3)]
    s1t = [sb_at(f"s1t{b}", [P, TBLK], F32, t1 + b * KB) for b in range(3)]
    t2t = [sb_at(f"t2t{b}", [P, TBLK], F32, t1 + 3 * KB + b * KB) for b in range(2)]
    Gt = [sb_at(f"Gt{b}", [P, TBLK], BF16, t1 + 5 * KB + b * TBLK * 2) for b in range(4)]
    gate_b = [sb_at(f"gb{b}", [P, TBLK], F32, t1 + 7 * KB + b * KB) for b in range(2)]
    scores_fm = sb_at("scores_fm", [NEXP, T], F32, t1 + 9 * KB)
    gates_fm = sb_at("gates_fm", [NEXP, T], F32, t1 + 9 * KB)
    x2r = [sb_at(f"x2r{b}", [P, D], F32, t1 + 9 * KB + b * 4 * KB) for b in range(2)]
    out_sb = [sb_at(f"osb{b}", [P, D], F32, t1 + 17 * KB + b * 4 * KB) for b in range(2)]

    pp = 186 * KB
    def sb_seq(name, shape, dtype):
        nonlocal pp
        t = sb_at(name, shape, dtype, pp)
        fb = int(np.prod(shape[1:])) * mybir.dt.size(dtype)
        pp += (fb + 63) // 64 * 64
        return t

    epsv = sb_seq("epsv", [P, 1], F32)
    tinyv = sb_seq("tinyv", [P, 1], F32)
    onesf = sb_seq("onesf", [P, P], F32)
    onesb = sb_seq("onesb", [P, P], BF16)
    idnb = sb_seq("idnb", [P, P], BF16)
    idnf = sb_seq("idnf", [P, P], F32)
    wlv = sb_seq("wlv", [P, NCD], F32)
    wrv = sb_seq("wrv", [P, NCD], F32)
    wprod = sb_seq("wprod", [P, NCD], F32)
    rwt_sb = [sb_seq(f"rwt{c}", [P, NEXP], BF16) for c in range(NCD)]
    eb16 = sb_seq("eb16", [1, NT * NEXP], F32)
    bias_rep = sb_seq("bias_rep", [P, NT * NEXP], F32)
    scores_tm = sb_seq("scores_tm", [P, NT * NEXP], F32)
    v_all = sb_seq("v_all", [P, NT * NEXP], F32)
    e1_all = sb_seq("e1_all", [P, NT * NEXP], F32)
    v2_all = sb_seq("v2_all", [P, NT * NEXP], F32)
    e2_all = sb_seq("e2_all", [P, NT * NEXP], F32)
    gates_all = sb_seq("gates_all", [P, NT * NEXP], F32)
    negb_row = sb_seq("negb", [1, T], BF16)
    grow = [sb_seq(f"grow{b}", [1, TBLK], F32) for b in range(2)]
    m1_all = sb_seq("m1_all", [P, NT], F32)
    m2_all = sb_seq("m2_all", [P, NT], F32)
    sqscr = sb_seq("sqscr", [P, D], BF16)
    scratch_row = sb_seq("scratch_row", [1, 512], F32)
    stats = {}
    for nm in ("s_ln1", "ss_ln1", "m_ln1", "r_ln1", "s_lna", "ss_lna", "m_lna",
               "r_lna", "s_ln2", "ss_ln2", "m_ln2", "r_ln2", "s_lnf", "ss_lnf",
               "m_lnf", "r_lnf", "tmpa", "tmpb", "tmpc", "den_r"):
        stats[nm] = sb_seq(nm, [P, NT], F32)
    assert pp <= 207 * KB - 64, f"persist overflow {pp}"

    class Ps:
        def __init__(self):
            self.gs = []

        def alloc(self, name, shape, dtype=F32):
            g = nc.psum_tensor(name, shape, dtype)
            h = g.__enter__()
            self.gs.append(g)
            return h

        def close(self):
            for g in reversed(self.gs):
                g.__exit__(None, None, None)
            self.gs = []

    phB = Ps()
    ntrB = [phB.alloc(f"ntrB{b}", [P, P], BF16) for b in range(2)]
    browps = [phB.alloc(f"browp{i}", [1, min(512, T)], F32)
              for i in range(max(T // 512, 1))]
    phB.close()

    phC = Ps()
    stp = [phC.alloc(f"stp{b}", [P, QB], F32) for b in range(2)]
    aop = phC.alloc("aop", [P, D], F32)
    denp = phC.alloc("denp", [P, 1], F32)
    ntrC = [phC.alloc(f"ntrC{b}", [P, P], BF16) for b in range(2)]
    phC.close()

    phD0 = Ps()
    scoresp = phD0.alloc("scoresp", [NEXP, min(512, T)], F32)
    phD0.close()

    phD = Ps()
    outp = [phD.alloc(f"outp{i}", [P, D], F32) for i in range(TT_PER_TB)]
    h1p = [phD.alloc(f"h1p{b}", [P, 2 * TBLK], F32) for b in range(3)]
    gatesp = phD.alloc("gatesp", [P, TBLK], F32)
    d0b_ps = gatesp
    phD.close()

    sem_guards = []

    def mksem(name):
        g = nc.semaphore(name)
        h = g.__enter__()
        sem_guards.append(g)
        return h

    names = ("s_c s_wA s_wB s_fA s_fB s_st s_exp s_ao s_rec s_ep s_tr "
             "s_trd s_v s_sc s_x2w s_s1 s_ev s_ev2 s_s2 "
             "s_acc s_gb s_gbe s_ot s_dbg s_cg s_gh "
             "s_x0 s_x1 s_xr0 s_xr1 s_sp0 s_sp1 s_x2r0 s_x2r1 "
             "s_out0 s_out1 s_gr0 s_gr1").split()
    S = {n: mksem(n) for n in names}
    s_w = [S["s_wA"], S["s_wB"]]
    s_f = [S["s_fA"], S["s_fB"]]
    s_xP = [S["s_x0"], S["s_x1"]]
    s_xrP = [S["s_xr0"], S["s_xr1"]]
    s_spP = [S["s_sp0"], S["s_sp1"]]
    s_x2rP = [S["s_x2r0"], S["s_x2r1"]]
    s_outP = [S["s_out0"], S["s_out1"]]
    s_grP = [S["s_gr0"], S["s_gr1"]]
    cnt = {n: 0 for n in names}
    _sem_name = {id(S[n]): n for n in names}

    def inc(s, n=1):
        cnt[_sem_name[id(s)]] += n
        return cnt[_sem_name[id(s)]]

    def cur(s):
        return cnt[_sem_name[id(s)]]

    ops = {"sync": [], "gpsimd": [], "tensor": [], "scalar": [], "vector": []}

    def E(eng, fn, drain=False):
        ops[eng].append((fn, drain))

    def W(eng, s, v):
        E(eng, lambda e, s=s, v=v: e.wait_ge(s, v))

    def dma(eng, out_ap, in_ap, s):
        v = inc(s, 16)
        E(eng, lambda e, o=out_ap, i=in_ap, ss=s: e.dma_start(out=o, in_=i).then_inc(ss, 16))
        return v

    x_tiled = x_d.rearrange("(n p) d -> n p d", p=P)
    x2s_tiled = x2s_d.rearrange("(n p) d -> n p d", p=P)
    out_tiled = out_d.rearrange("(n p) d -> n p d", p=P)

    TBL_LNEXP = _act_table_idx(nc, "natural_log_exp_and_others")
    TBL_SIG = _act_table_idx(nc, "sigmoid_and_others")
    TBL_SILU = _act_table_idx(nc, "silu_and_others")

    def load_table(idx):
        E("scalar", lambda e, idx=idx: e.add_instruction(
            mybir.InstLoadActFuncSet(name=nc.get_next_instruction_name(),
                                     act_func_set_id=idx, ins=[], outs=[])))

    load_table(TBL_LNEXP)
    dma("sync", onesf[:], ones_d[:, :], S["s_c"])
    dma("sync", idnf[:], idn_d[:, :], S["s_c"])
    dma("sync", wlv[:], wl_d[:, :], S["s_c"])
    dma("sync", wrv[:], wr_d[:, :], S["s_c"])
    dma("sync", eb16[:], eb16_d[:, :], S["s_c"])
    dma("gpsimd", onesb[:], ones_d[:, :], S["s_cg"])
    dma("gpsimd", idnb[:], idn_d[:, :], S["s_cg"])
    for c in range(NCD):
        dma("gpsimd", rwt_sb[c][:], rwt_d[c * P:(c + 1) * P, :], S["s_cg"])
    c_all = cur(S["s_c"])
    cg_all = cur(S["s_cg"])

    E("vector", lambda e: e.memset(epsv[:], float(EPS)))
    mv = inc(S["s_v"])
    E("vector", lambda e, v=mv: e.memset(tinyv[:], 1e-30).then_inc(S["s_v"], 1))
    eps_ready = mv
    W("scalar", S["s_v"], eps_ready)
    W("vector", S["s_c"], c_all)
    W("vector", S["s_cg"], cg_all)
    E("vector", lambda e: e.tensor_mul(wprod[:], wlv[:], wrv[:]), drain=True)

    def rsqrt_chain(var_ap, r_ap, t1_ap, t2_ap, ready_v):
        W("scalar", S["s_v"], ready_v)
        E("scalar", lambda e, a=var_ap, o=t1_ap: e.activation(o, a, ACT.Ln, bias=epsv[:]))
        sv = inc(S["s_sc"])
        E("scalar", lambda e, a=t1_ap, o=r_ap, v=sv: e.activation(
            o, a, ACT.Exp, scale=-0.5).then_inc(S["s_sc"], 1))
        W("vector", S["s_sc"], sv)

    def ln_stats(src_ap, i, pfx):
        s_, ss_, m_, r_ = (stats[f"s_{pfx}"], stats[f"ss_{pfx}"],
                           stats[f"m_{pfx}"], stats[f"r_{pfx}"])
        E("vector", lambda e, i=i: e.reduce_sum(s_[:, i:i + 1], src_ap, axis=AX), drain=True)
        sv = inc(S["s_sc"])
        E("scalar", lambda e, i=i, v=sv: e.activation(
            sqscr[:], src_ap, ACT.Square, accum_out=ss_[:, i:i + 1]).then_inc(S["s_sc"], 1))
        W("vector", S["s_sc"], sv)
        E("vector", lambda e, i=i: e.tensor_scalar(
            m_[:, i:i + 1], s_[:, i:i + 1], 1.0 / D, None, ALU.mult), drain=True)
        E("vector", lambda e, i=i: e.tensor_scalar(
            stats["tmpa"][:, i:i + 1], ss_[:, i:i + 1], 1.0 / D, None, ALU.mult), drain=True)
        E("vector", lambda e, i=i: e.tensor_mul(
            stats["tmpb"][:, i:i + 1], m_[:, i:i + 1], m_[:, i:i + 1]), drain=True)
        vv = inc(S["s_v"])
        E("vector", lambda e, i=i, v=vv: e.tensor_sub(
            stats["tmpa"][:, i:i + 1], stats["tmpa"][:, i:i + 1],
            stats["tmpb"][:, i:i + 1]).then_inc(S["s_v"], 1))
        rsqrt_chain(stats["tmpa"][:, i:i + 1], r_[:, i:i + 1],
                    stats["tmpb"][:, i:i + 1], stats["tmpc"][:, i:i + 1], vv)

    red_v = []
    sq_v = []
    for i in range(NT):
        if i >= 2:
            W("sync", S["s_v"], red_v[i - 2])
            W("sync", S["s_sc"], sq_v[i - 2])
        dma("sync", x_r[i % 2][:], x_tiled[i, :, :], s_xP[i % 2])
        W("vector", s_xP[i % 2], 16 * (i // 2 + 1))
        W("scalar", s_xP[i % 2], 16 * (i // 2 + 1))
        rv = inc(S["s_v"])
        E("vector", lambda e, i=i, v=rv: e.reduce_sum(
            stats["s_ln1"][:, i:i + 1], x_r[i % 2][:], axis=AX).then_inc(S["s_v"], 1),
          drain=True)
        red_v.append(rv)
        sv = inc(S["s_sc"])
        E("scalar", lambda e, i=i, v=sv: e.activation(
            sqscr[:], x_r[i % 2][:], ACT.Square,
            accum_out=stats["ss_ln1"][:, i:i + 1]).then_inc(S["s_sc"], 1))
        sq_v.append(sv)
    W("vector", S["s_sc"], sq_v[-1])
    E("vector", lambda e: e.tensor_scalar(
        stats["m_ln1"][:], stats["s_ln1"][:], 1.0 / D, None, ALU.mult), drain=True)
    E("vector", lambda e: e.tensor_scalar(
        stats["tmpa"][:], stats["ss_ln1"][:], 1.0 / D, None, ALU.mult), drain=True)
    E("vector", lambda e: e.tensor_mul(
        stats["tmpb"][:], stats["m_ln1"][:], stats["m_ln1"][:]), drain=True)
    bv1 = inc(S["s_v"])
    E("vector", lambda e, v=bv1: e.tensor_sub(
        stats["tmpa"][:], stats["tmpa"][:], stats["tmpb"][:]).then_inc(S["s_v"], 1))
    rsqrt_chain(stats["tmpa"][:], stats["r_ln1"][:], stats["tmpb"][:],
                stats["tmpc"][:], bv1)
    ln1_rdy = cur(S["s_sc"])

    apply_v = []
    tr_after_tile = []
    for i in range(NT):
        if i == 0:
            W("sync", S["s_v"], red_v[-1])
            W("sync", S["s_sc"], sq_v[-1])
        else:
            pass
        if i >= 2:
            W("sync", S["s_v"], apply_v[i - 2])
        dma("sync", x_r[i % 2][:], x_tiled[i, :, :], s_xP[i % 2])
        W("vector", s_xP[i % 2], 16 * (NT // 2 + i // 2 + 1))
        if i == 0:
            W("vector", S["s_sc"], ln1_rdy)
        av = inc(S["s_v"])
        E("vector", lambda e, i=i, v=av: e.tensor_scalar(
            n1_tm[i][:], x_r[i % 2][:], stats["m_ln1"][:, i:i + 1],
            stats["r_ln1"][:, i:i + 1], ALU.subtract, ALU.mult).then_inc(S["s_v"], 1))
        apply_v.append(av)
        W("tensor", S["s_v"], av)
        if i == 0:
            W("tensor", S["s_c"], c_all)
            W("tensor", S["s_cg"], cg_all)
        for c in range(NCD):
            k = i * NCD + c
            if k >= 2:
                W("tensor", S["s_trd"], k - 1)
            tv = inc(S["s_tr"])
            E("tensor", lambda e, i=i, c=c, k=k: e.transpose(
                ntrB[k % 2][:], n1_tm[i][:, c * P:(c + 1) * P], idnb[:]).then_inc(S["s_tr"], 1))
            W("vector", S["s_tr"], tv)
            dv = inc(S["s_trd"])
            E("vector", lambda e, i=i, c=c, k=k: e.tensor_copy(
                n1_fm[c][:, i * P:(i + 1) * P], ntrB[k % 2][:]).then_inc(S["s_trd"], 1))
        tr_after_tile.append(cur(S["s_trd"]))

    E("vector", lambda e: e.drain())
    afm_last = None
    for c in range(NCD):
        afm_last = inc(S["s_v"])
        E("vector", lambda e, c=c, v=afm_last: e.tensor_scalar(
            a_fm[c][:], n1_fm[c][:], wprod[:, c:c + 1], None,
            ALU.mult).then_inc(S["s_v"], 1), drain=True)

    sq_buf = P_T[0]
    NS5 = max(T // 512, 1)
    W5 = min(512, T)
    brow_base = cur(S["s_st"])
    for c in range(NCD):
        if c >= 1:
            W("vector", S["s_st"], brow_base + c)
        sqv = inc(S["s_v"])
        E("vector", lambda e, c=c, v=sqv: e.tensor_mul(
            sq_buf[:, :T], a_fm[c][:], a_fm[c][:]).then_inc(S["s_v"], 1), drain=True)
        W("tensor", S["s_v"], sqv)
        for s5 in range(NS5):
            last = (c == NCD - 1)
            stv = inc(S["s_st"]) if s5 == NS5 - 1 else None
            def mk(c=c, s5=s5, last=last, fin=stv is not None):
                def f(e):
                    inst = e.matmul(browps[s5][:, :W5], onesb[:, 0:1],
                                    sq_buf[:, s5 * 512:s5 * 512 + W5],
                                    start=(c == 0), stop=last, skip_group_check=True)
                    if fin:
                        inst.then_inc(S["s_st"], 1)
                return f
            E("tensor", mk())
    brow_done = cur(S["s_st"])
    W("scalar", S["s_st"], brow_done)
    negb_v = None
    for s5 in range(NS5):
        if negb_v is not None:
            W("scalar", S["s_v"], negb_v)
        E("scalar", lambda e, s5=s5: e.activation(
            scratch_row[:, :W5], browps[s5][:, :W5], ACT.Ln, bias=tinyv[0:1, :]))
        sv = inc(S["s_sc"])
        E("scalar", lambda e, s5=s5, v=sv: e.activation(
            scratch_row[:, :W5], scratch_row[:, :W5], ACT.Exp,
            scale=0.5).then_inc(S["s_sc"], 1))
        W("vector", S["s_sc"], sv)
        negb_v = inc(S["s_v"])
        E("vector", lambda e, s5=s5, v=negb_v: e.tensor_scalar(
            negb_row[:, s5 * 512:s5 * 512 + W5], scratch_row[:, :W5], -32.0, None,
            ALU.mult).then_inc(S["s_v"], 1))

    if debug:
        W("gpsimd", S["s_v"], negb_v)
        for i in range(NT):
            dma("gpsimd", dbg["dbg_n1"].rearrange("(n p) d -> n p d", p=P)[i, :, :],
                n1_tm[i][:], S["s_dbg"])

    exp_done_at_block = []
    gi_tr = []
    ao_done_v = None

    ntr_pending = []
    ntr_tr_done = {}
    ntr_kc = [0]

    def emit_ntr_one():
        gi, n2v = ntr_pending.pop(0)
        W("tensor", S["s_v"], n2v)
        for c in range(NCD):
            kc = ntr_kc[0]
            ntr_kc[0] += 1
            if kc >= 2:
                W("tensor", S["s_trd"], cur(S["s_trd"]) - 1)
            tv = inc(S["s_tr"])
            E("tensor", lambda e, gi=gi, c=c, kc=kc: e.transpose(
                ntrC[kc % 2][:], n2tm[gi % 3][:, c * P:(c + 1) * P],
                idnb[:]).then_inc(S["s_tr"], 1))
            W("vector", S["s_tr"], tv)
            dv = inc(S["s_trd"])
            E("vector", lambda e, gi=gi, c=c, kc=kc: e.tensor_copy(
                n2_fm[c][:, gi * P:(gi + 1) * P], ntrC[kc % 2][:]).then_inc(S["s_trd"], 1))
        ntr_tr_done[gi] = cur(S["s_tr"])
        gi_tr.append(cur(S["s_trd"]))

    def emit_ST(b):
        q0 = b * QB
        buf = P_T[b % 2]
        for it in range(NT):
            k = b * NT + it
            if b == 0 and it == 0:
                W("tensor", S["s_v"], afm_last)
            if k >= 2:
                W("tensor", S["s_exp"], k - 1)
            for c in range(NCD):
                E("tensor", lambda e, c=c, it=it, q0=q0, k=k: e.matmul(
                    stp[k % 2][:], a_fm[c][:, it * P:(it + 1) * P],
                    n1_fm[c][:, q0:q0 + QB], start=(c == 0), stop=False,
                    skip_group_check=True))
            if b == 0 and it == 0:
                W("tensor", S["s_v"], negb_v)
            stv = inc(S["s_st"])
            E("tensor", lambda e, q0=q0, k=k, v=stv: e.matmul(
                stp[k % 2][:], onesb[0:1, :], negb_row[:, q0:q0 + QB],
                start=False, stop=True, skip_group_check=True).then_inc(S["s_st"], 1))
            W("scalar", S["s_st"], stv)
            ev = inc(S["s_exp"])
            E("scalar", lambda e, it=it, k=k, buf=buf, v=ev: e.activation(
                buf[:, it * QB:(it + 1) * QB], stp[k % 2][:], ACT.Exp).then_inc(S["s_exp"], 1))

    def emit_AO(b):
        nonlocal ao_done_v
        buf = P_T[b % 2]
        exp_need = exp_done_at_block[b]
        for qt in range(QT_PER_B):
            gi = b * QT_PER_B + qt
            W("tensor", S["s_exp"], exp_need)
            if gi > 0:
                W("tensor", S["s_ep"], cur(S["s_ep"]))
                W("tensor", S["s_rec"], cur(S["s_rec"]))
            for it in range(NT):
                st0 = buf[:, it * QB + qt * P: it * QB + (qt + 1) * P]
                first = it == 0
                last = it == NT - 1
                E("tensor", lambda e, st0=st0, it=it, first=first, last=last: e.matmul(
                    aop[:, 0:512], st0, n1_tm[it][:, 0:512],
                    start=first, stop=last, skip_group_check=True))
                E("tensor", lambda e, st0=st0, it=it, first=first, last=last: e.matmul(
                    aop[:, 512:1024], st0, n1_tm[it][:, 512:1024],
                    start=first, stop=last, skip_group_check=True))
                aov = inc(S["s_ao"]) if last else None
                def mkd(st0=st0, it=it, first=first, last=last, fin=aov is not None):
                    def f(e):
                        inst = e.matmul(denp[:], st0, onesb[:, 0:1],
                                        start=first, stop=last, skip_group_check=True)
                        if fin:
                            inst.then_inc(S["s_ao"], 1)
                    return f
                E("tensor", mkd())
            aov = cur(S["s_ao"])
            ao_done_v = aov
            if len(ntr_pending) >= 2:
                emit_ntr_one()
            W("vector", S["s_ao"], aov)
            rv = inc(S["s_rec"])
            E("vector", lambda e, gi=gi, v=rv: e.reciprocal(
                stats["den_r"][:, gi % NT:gi % NT + 1], denp[:]).then_inc(S["s_rec"], 1), drain=True)
            epv = inc(S["s_ep"])
            E("vector", lambda e, gi=gi, v=epv: e.tensor_scalar(
                y1pre[:, 0:512], aop[:, 0:512], stats["den_r"][:, gi % NT:gi % NT + 1],
                None, ALU.mult).then_inc(S["s_ep"], 1), drain=True)
            W("scalar", S["s_rec"], rv)
            if gi >= 1:
                W("scalar", S["s_x2w"], gi)
            ep2 = inc(S["s_ep"])
            E("scalar", lambda e, gi=gi, v=ep2: e.activation(
                y1pre[:, 512:D], aop[:, 512:D], ACT.Copy,
                scale=stats["den_r"][:, gi % NT:gi % NT + 1]).then_inc(S["s_ep"], 1))
            W("scalar", S["s_ep"], ep2)
            W("vector", S["s_ep"], ep2)
            ln_stats(y1pre[:], gi, "lna")
            E("vector", lambda e, gi=gi: e.tensor_scalar(
                y1pre[:], y1pre[:], stats["m_lna"][:, gi:gi + 1],
                stats["r_lna"][:, gi:gi + 1], ALU.subtract, ALU.mult), drain=True)
            W("vector", s_xrP[gi % 2], 16 * (gi // 2 + 1))
            if gi >= 2:
                W("vector", s_spP[gi % 2], 16 * (gi // 2))
            x2v = inc(S["s_x2w"])
            E("vector", lambda e, gi=gi, v=x2v: e.tensor_add(
                x2t[gi % 2][:], x_r[gi % 2][:], y1pre[:]).then_inc(S["s_x2w"], 1), drain=True)
            W("sync", S["s_x2w"], x2v)
            dma("sync", x2s_tiled[gi, :, :], x2t[gi % 2][:], s_spP[gi % 2])
            W("scalar", S["s_x2w"], x2v)
            ln_stats(x2t[gi % 2][:], gi, "ln2")
            if gi >= 3:
                W("vector", S["s_tr"], ntr_tr_done[gi - 3])
            n2v = inc(S["s_v"])
            E("vector", lambda e, gi=gi, v=n2v: e.tensor_scalar(
                n2tm[gi % 3][:], x2t[gi % 2][:], stats["m_ln2"][:, gi:gi + 1],
                stats["r_ln2"][:, gi:gi + 1], ALU.subtract, ALU.mult).then_inc(S["s_v"], 1))
            ntr_pending.append((gi, n2v))

    emit_ST(0)
    exp_done_at_block.append(cur(S["s_exp"]))
    for b in range(NQB):
        if b + 1 < NQB:
            emit_ST(b + 1)
            exp_done_at_block.append(cur(S["s_exp"]))
        for qt in range(QT_PER_B):
            gi = b * QT_PER_B + qt
            W("sync", S["s_v"], apply_v[-1])
            if gi >= 2:
                W("sync", S["s_x2w"], gi - 1)
            dma("sync", x_r[gi % 2][:], x_tiled[gi, :, :], s_xrP[gi % 2])
        emit_AO(b)

    while ntr_pending:
        emit_ntr_one()
    n2_all_v = cur(S["s_trd"])
    sp_all = (cur(S["s_sp0"]), cur(S["s_sp1"]))
    all_exp_done = exp_done_at_block[-1]

    if debug:
        W("gpsimd", S["s_trd"], n2_all_v)
        for c in range(NCD):
            dma("gpsimd", dbg["dbg_n2fm"].rearrange("(c p) t -> c p t", p=P)[c, :, :],
                n2_fm[c][:], S["s_dbg"])
        W("gpsimd", S["s_sp0"], sp_all[0])
        W("gpsimd", S["s_sp1"], sp_all[1])
        for i in range(NT):
            dma("gpsimd", dbg["dbg_x2"].rearrange("(n p) d -> n p d", p=P)[i, :, :],
                x2s_tiled[i, :, :], S["s_dbg"])

    load_table(TBL_SIG)
    for qb in range(NS5):
        if qb == 0:
            W("tensor", S["s_trd"], n2_all_v)
        for c in range(NCD):
            last = c == NCD - 1
            stv = inc(S["s_st"]) if last else None
            def mk(c=c, qb=qb, last=last, fin=stv is not None):
                def f(e):
                    inst = e.matmul(scoresp[:, :W5], rwt_sb[c][:],
                                    n2_fm[c][:, qb * 512:qb * 512 + W5],
                                    start=(c == 0), stop=last, skip_group_check=True)
                    if fin:
                        inst.then_inc(S["s_st"], 1)
                return f
            E("tensor", mk())
        stv = cur(S["s_st"])
        if qb == 0:
            W("scalar", S["s_sp0"], sp_all[0])
            W("scalar", S["s_sp1"], sp_all[1])
        W("scalar", S["s_st"], stv)
        sv = inc(S["s_sc"])
        E("scalar", lambda e, qb=qb, v=sv: e.activation(
            scores_fm[:, qb * 512:qb * 512 + W5], scoresp[:, :W5],
            ACT.Sigmoid).then_inc(S["s_sc"], 1))
        if qb + 1 < NS5:
            W("tensor", S["s_sc"], sv)
    sig_done = cur(S["s_sc"])

    d0b_q = []

    def Q(fn):
        d0b_q.append(fn)

    state = {}

    def _qi_bias():
        W("tensor", S["s_c"], c_all)
        W("tensor", S["s_cg"], cg_all)
        btv = inc(S["s_st"])
        E("tensor", lambda e, v=btv: e.matmul(
            d0b_ps[:, 0:NT * NEXP], onesf[0:1, :], eb16[:, :], start=True,
            stop=True, skip_group_check=True).then_inc(S["s_st"], 1))
        W("vector", S["s_st"], btv)
        bv = inc(S["s_v"])
        E("vector", lambda e, v=bv: e.tensor_copy(
            bias_rep[:], d0b_ps[:, 0:NT * NEXP]).then_inc(S["s_v"], 1), drain=True)
        state["bias_done"] = bv
    Q(_qi_bias)

    def _qi_str(i):
        def f():
            W("tensor", S["s_v"], state["bias_done"])
            W("tensor", S["s_trd"], cur(S["s_trd"]))
            if i == 0:
                W("tensor", S["s_sc"], sig_done)
            tv = inc(S["s_tr"])
            E("tensor", lambda e, i=i: e.transpose(
                d0b_ps[:, 0:NEXP].rearrange("p e -> p e") if False else d0b_ps[:, 0:NEXP],
                scores_fm[:, i * P:(i + 1) * P],
                idnf[0:NEXP, 0:NEXP]).then_inc(S["s_tr"], 1))
            W("vector", S["s_tr"], tv)
            inc(S["s_trd"])
            E("vector", lambda e, i=i: e.tensor_copy(
                scores_tm[:, i * NEXP:(i + 1) * NEXP],
                d0b_ps[:, 0:NEXP]).then_inc(S["s_trd"], 1))
        return f
    for i in range(NT):
        Q(_qi_str(i))

    def _qi_top2_head():
        W("vector", S["s_trd"], cur(S["s_trd"]))
        E("vector", lambda e: e.drain())
        E("vector", lambda e: e.tensor_add(v_all[:], scores_tm[:], bias_rep[:]),
          drain=True)
    Q(_qi_top2_head)

    def _qi_top2(i):
        def f():
            sl = slice(i * NEXP, (i + 1) * NEXP)
            E("vector", lambda e, sl=sl, i=i: e.reduce_max(
                m1_all[:, i:i + 1], v_all[:, sl], axis=AX), drain=True)
            E("vector", lambda e, sl=sl, i=i: e.tensor_scalar(
                e1_all[:, sl], v_all[:, sl], m1_all[:, i:i + 1], None,
                ALU.is_equal), drain=True)
            E("vector", lambda e, sl=sl: e.scalar_tensor_tensor(
                v2_all[:, sl], e1_all[:, sl], -1e30, v_all[:, sl], ALU.mult,
                ALU.add), drain=True)
            E("vector", lambda e, sl=sl, i=i: e.reduce_max(
                m2_all[:, i:i + 1], v2_all[:, sl], axis=AX), drain=True)
            E("vector", lambda e, sl=sl, i=i: e.tensor_scalar(
                e2_all[:, sl], v2_all[:, sl], m2_all[:, i:i + 1], None,
                ALU.is_equal), drain=True)
        return f
    for i in range(NT):
        Q(_qi_top2(i))

    def _qi_gates():
        E("vector", lambda e: e.tensor_add(e1_all[:], e1_all[:], e2_all[:]),
          drain=True)
        gv = inc(S["s_v"])
        E("vector", lambda e, v=gv: e.tensor_mul(
            gates_all[:], e1_all[:], scores_tm[:]).then_inc(S["s_v"], 1),
          drain=True)
        state["gv"] = gv
        if debug:
            W("gpsimd", S["s_v"], gv)
            dma("gpsimd", dbg["dbg_gates"][:, :], gates_all[:], S["s_dbg"])
    Q(_qi_gates)

    def _qi_gtr(i):
        def f():
            W("tensor", S["s_v"], state["gv"])
            W("tensor", S["s_trd"], cur(S["s_trd"]))
            tv = inc(S["s_tr"])
            E("tensor", lambda e, i=i: e.transpose(
                d0b_ps[0:NEXP, 0:P], gates_all[:, i * NEXP:(i + 1) * NEXP],
                idnf[:]).then_inc(S["s_tr"], 1))
            W("vector", S["s_tr"], tv)
            inc(S["s_trd"])
            E("vector", lambda e, i=i: e.tensor_copy(
                gates_fm[:, i * P:(i + 1) * P],
                d0b_ps[0:NEXP, 0:P]).then_inc(S["s_trd"], 1))
        return f
    for i in range(NT):
        Q(_qi_gtr(i))

    def _qi_bounce():
        W("sync", S["s_trd"], cur(S["s_trd"]))
        dma("sync", gates_hbm[:, :], gates_fm[:], S["s_gh"])
    Q(_qi_bounce)

    load_table(TBL_SILU)
    base_chunks = [("shared", k) for k in range(4)] + [("expert", e) for e in range(NEXP)]
    chunks = [(p, kind, idx) for p in range(NPASS) for (kind, idx) in base_chunks]
    NCHUNK_P = len(base_chunks)

    chunk_w_target = [None] * len(chunks)

    def emit_stream(pss0):
        if pss0 == 0:
            W("gpsimd", S["s_exp"], all_exp_done)
        for ci in range(pss0 * NCHUNK_P, (pss0 + 1) * NCHUNK_P):
            _, kind, idx = chunks[ci]
            sl = ci % 2
            if ci == 1:
                W("gpsimd", S["s_ao"], ao_done_v)
                if debug:
                    W("gpsimd", S["s_dbg"], cur(S["s_dbg"]))
            if ci >= 2:
                W("gpsimd", S["s_s2"], (ci - 1) * NTB * NCD * TT_PER_TB)
            w12_t = slotA_w12 if sl == 0 else slotB_w12
            w3_t = slotA_w3 if sl == 0 else slotB_w3
            if kind == "shared":
                k = idx
                for c in range(NCD):
                    dma("gpsimd", w12_t[c][:, 0:RH],
                        w1t_d[c * P:(c + 1) * P, k * RH:(k + 1) * RH], s_w[sl])
                    dma("gpsimd", w12_t[c][:, RH:2 * RH],
                        w2t_d[c * P:(c + 1) * P, k * RH:(k + 1) * RH], s_w[sl])
                for j in range(NCD):
                    dma("gpsimd", w3_t[j][:],
                        w3st_d[k * RH + j * P: k * RH + (j + 1) * P, :], s_w[sl])
            else:
                ei = idx
                for c in range(NCD):
                    dma("gpsimd", w12_t[c][:], w12t_d[ei, c * P:(c + 1) * P, :], s_w[sl])
                for j in range(NCD):
                    dma("gpsimd", w3_t[j][:], w3t_d[ei, j * P:(j + 1) * P, :], s_w[sl])
            chunk_w_target[ci] = cur(s_w[sl])

    jj = 0
    gate_load_cnt = [0]
    acc_done_pass = []

    def emit_pass(pss, mid_hook=None, lastck_hook=None):
        nonlocal jj
        S2_DEPTH = 3
        s2_pending = []
        acc_q = []

        def emit_s2_fix(p):
            (jj_, j_, w3t_, first_acc_wait, _unused_v, s2v, accs) = p
            for tt in range(TT_PER_TB):
                for half in range(2):
                    if first_acc_wait:
                        W("tensor", S["s_acc"], acc_free.get((tt, half), 0))
                    lastj = j_ == NCD - 1
                    fin = half == 1
                    def mk2(tt=tt, half=half, jj_=jj_, j_=j_, w3t_=w3t_,
                            lastj=lastj, fin=fin, s2v=s2v):
                        def f(e):
                            inst = e.matmul(
                                outp[tt][:, half * 512:(half + 1) * 512],
                                Gt[jj_ % 4][:, tt * P:(tt + 1) * P],
                                w3t_[j_][:, half * 512:(half + 1) * 512],
                                start=(j_ == 0), stop=lastj, skip_group_check=True)
                            if fin:
                                inst.then_inc(S["s_s2"], 1)
                        return f
                    E("tensor", mk2())
            if accs is not None:
                ci_, tb_ = accs

                def emit_accs(ci_=ci_, tb_=tb_, s2v=s2v):
                    W("vector", S["s_s2"], s2v)
                    for tt in range(TT_PER_TB):
                        ai = tb_ * TT_PER_TB + tt
                        for half in range(2):
                            av = inc(S["s_acc"])
                            acc_free[(tt, half)] = av
                            hs = slice(half * 512, (half + 1) * 512)
                            if ci_ % NCHUNK_P == 0:
                                E("vector", lambda e, tt=tt, ai=ai, hs=hs, v=av: e.tensor_copy(
                                    acc[ai][:, hs], outp[tt][:, hs]).then_inc(S["s_acc"], 1), drain=True)
                            else:
                                E("vector", lambda e, tt=tt, ai=ai, hs=hs, v=av: e.tensor_add(
                                    acc[ai][:, hs], acc[ai][:, hs], outp[tt][:, hs]).then_inc(S["s_acc"], 1), drain=True)
                        if ci_ % NCHUNK_P == NCHUNK_P - 1:
                            acc_tile_ready[(ci_ // NCHUNK_P, ai)] = av
                            emit_stats_pre(ci_ // NCHUNK_P, ai)
                acc_q.append(emit_accs)

        for ck in range(NCHUNK_P):
            ci = pss * NCHUNK_P + ck
            _, kind, idx = chunks[ci]
            sl = ci % 2
            w12_t = slotA_w12 if sl == 0 else slotB_w12
            w3_t = slotA_w3 if sl == 0 else slotB_w3
            has_g = kind == "expert"
            if has_g:
                while d0b_q:
                    d0b_q.pop(0)()
            W("tensor", s_w[sl], chunk_w_target[ci])
            if ci == 0:
                W("tensor", S["s_sc"], sig_done)
            if lastck_hook is not None and ck == NCHUNK_P - 1:
                lastck_hook()
            for tb in range(NTB):
                tok0 = pss * PASS_T + tb * TBLK
                if has_g:
                    gk = gate_load_cnt[0]
                    gate_load_cnt[0] += 1
                    if gk == 0:
                        W("sync", S["s_gh"], 16)
                        W("sync", S["s_sp0"], sp_all[0])
                        W("sync", S["s_sp1"], sp_all[1])
                        W("sync", S["s_x2w"], cur(S["s_x2w"]))
                    if gk >= 2:
                        W("sync", S["s_gb"], gk - 1)
                    grv = dma("sync", grow[gk % 2][:],
                              gates_hbm[idx:idx + 1, tok0:tok0 + TBLK], s_grP[gk % 2])
                    W("tensor", s_grP[gk % 2], grv)
                    if cur(S["s_gb"]) > 0:
                        W("tensor", S["s_gbe"], cur(S["s_gbe"]))
                    gbv = inc(S["s_gb"])
                    E("tensor", lambda e, gk=gk, v=gbv: e.matmul(
                        gatesp[:], onesf[0:1, :], grow[gk % 2][:],
                        start=True, stop=True, skip_group_check=True).then_inc(S["s_gb"], 1))
                    W("vector", S["s_gb"], gbv)
                    gbev = inc(S["s_gbe"])
                    E("vector", lambda e, tb=tb, v=gbev: e.tensor_copy(
                        gate_b[tb % 2][:], gatesp[:]).then_inc(S["s_gbe"], 1), drain=True)
                for j in range(NCD):
                    if d0b_q:
                        d0b_q.pop(0)()
                    if d0b_q:
                        d0b_q.pop(0)()
                    if mid_hook is not None and ck == 0 and j == 4:
                        mid_hook(tb)
                    if jj >= 3:
                        W("tensor", S["s_ev2"], jj - 2)
                    for c in range(NCD):
                        E("tensor", lambda e, c=c, j=j, jj=jj, tok0=tok0, w12_t=w12_t: e.matmul(
                            h1p[jj % 3][:, 0:TBLK], w12_t[c][:, j * P:(j + 1) * P],
                            n2_fm[c][:, tok0:tok0 + TBLK], start=(c == 0), stop=False,
                            skip_group_check=True))
                    for c in range(NCD):
                        last = c == NCD - 1
                        s1v = inc(S["s_s1"]) if last else None
                        def mk1(c=c, j=j, jj=jj, tok0=tok0, w12_t=w12_t, last=last,
                                fin=s1v is not None):
                            def f(e):
                                inst = e.matmul(
                                    h1p[jj % 3][:, TBLK:2 * TBLK],
                                    w12_t[c][:, RH + j * P:RH + (j + 1) * P],
                                    n2_fm[c][:, tok0:tok0 + TBLK],
                                    start=False, stop=last, skip_group_check=True)
                                if fin:
                                    inst.then_inc(S["s_s1"], 1)
                            return f
                        E("tensor", mk1())
                    s1v = cur(S["s_s1"])
                    ev2 = inc(S["s_ev2"])
                    s2v = inc(S["s_s2"], TT_PER_TB)
                    first_acc_wait = (j == 0) and not (ci == 0 and tb == 0)
                    accs = (ci, tb) if j == NCD - 1 else None
                    this = (jj, j, w3_t, first_acc_wait, None, s2v, accs)
                    s2_pending.append((this, ev2))
                    if len(s2_pending) > S2_DEPTH:
                        p, pev2 = s2_pending.pop(0)
                        emit_s2_fix(p)
                    W("scalar", S["s_s1"], s1v)
                    evv = inc(S["s_ev"])
                    E("scalar", lambda e, jj=jj, v=evv: e.activation(
                        s1t[jj % 3][:], h1p[jj % 3][:, 0:TBLK],
                        ACT.Silu).then_inc(S["s_ev"], 1))
                    while acc_q:
                        acc_q.pop(0)()
                    W("vector", S["s_ev"], evv)
                    if jj >= 4:
                        W("vector", S["s_s2"], (jj - 3) * TT_PER_TB)
                    if has_g:
                        W("vector", S["s_gbe"], cur(S["s_gbe"]))
                        E("vector", lambda e, jj=jj, tb=tb: e.tensor_mul(
                            t2t[jj % 2][:], h1p[jj % 3][:, TBLK:2 * TBLK],
                            gate_b[tb % 2][:]), drain=True)
                        E("vector", lambda e, jj=jj, v=ev2: e.tensor_mul(
                            Gt[jj % 4][:], s1t[jj % 3][:],
                            t2t[jj % 2][:]).then_inc(S["s_ev2"], 1), drain=True)
                    else:
                        E("vector", lambda e, jj=jj, v=ev2: e.tensor_mul(
                            Gt[jj % 4][:], s1t[jj % 3][:],
                            h1p[jj % 3][:, TBLK:2 * TBLK]).then_inc(S["s_ev2"], 1), drain=True)
                    jj += 1
        while s2_pending:
            p, pev2 = s2_pending.pop(0)
            W("tensor", S["s_ev2"], pev2)
            emit_s2_fix(p)
        while acc_q:
            acc_q.pop(0)()

    acc_tile_ready = {}
    acc_free = {}
    stats_pre_done = set()

    def emit_stats_pre(pss, ii):
        si = pss * NACC + ii
        stats_pre_done.add((pss, ii))
        rdy_v = acc_tile_ready[(pss, ii)]
        W("vector", S["s_acc"], rdy_v)
        W("scalar", S["s_acc"], rdy_v)
        E("vector", lambda e, ii=ii, si=si: e.reduce_sum(
            stats["s_lnf"][:, si:si + 1], acc[ii][:], axis=AX), drain=True)
        sv = inc(S["s_sc"])
        E("scalar", lambda e, ii=ii, si=si, v=sv: e.activation(
            sqscr[:], acc[ii][:], ACT.Square,
            accum_out=stats["ss_lnf"][:, si:si + 1]).then_inc(S["s_sc"], 1))

    def emit_phase_e_stats(pss):
        load_table(TBL_LNEXP)
        if pss == 0:
            W("vector", S["s_sp0"], sp_all[0])
            W("vector", S["s_sp1"], sp_all[1])
        if debug and pss == 0:
            W("gpsimd", S["s_acc"], acc_done_pass[0])
            for i in range(NACC):
                dma("gpsimd", dbg["dbg_acc"].rearrange("(n p) d -> n p d", p=P)[i, :, :],
                    acc[i][:], S["s_dbg"])
        c0 = pss * NACC
        for ii in range(NACC):
            if (pss, ii) not in stats_pre_done:
                emit_stats_pre(pss, ii)
        W("vector", S["s_sc"], cur(S["s_sc"]))
        sl = slice(c0, c0 + NACC)
        E("vector", lambda e, sl=sl: e.tensor_scalar(
            stats["m_lnf"][:, sl], stats["s_lnf"][:, sl], 1.0 / D, None, ALU.mult),
          drain=True)
        E("vector", lambda e, sl=sl: e.tensor_scalar(
            stats["tmpa"][:, sl], stats["ss_lnf"][:, sl], 1.0 / D, None, ALU.mult),
          drain=True)
        E("vector", lambda e, sl=sl: e.tensor_mul(
            stats["tmpb"][:, sl], stats["m_lnf"][:, sl], stats["m_lnf"][:, sl]),
          drain=True)
        bv = inc(S["s_v"])
        E("vector", lambda e, sl=sl, v=bv: e.tensor_sub(
            stats["tmpa"][:, sl], stats["tmpa"][:, sl],
            stats["tmpb"][:, sl]).then_inc(S["s_v"], 1))
        rsqrt_chain(stats["tmpa"][:, sl], stats["r_lnf"][:, sl],
                    stats["tmpb"][:, sl], stats["tmpc"][:, sl], bv)
        E("vector", lambda e, sl=sl: e.tensor_mul(
            stats["tmpc"][:, sl], stats["m_lnf"][:, sl], stats["r_lnf"][:, sl]),
          drain=True)
        nmv = inc(S["s_v"])
        E("vector", lambda e, sl=sl, v=nmv: e.tensor_scalar(
            stats["tmpc"][:, sl], stats["tmpc"][:, sl], -1.0, None,
            ALU.mult).then_inc(S["s_v"], 1), drain=True)
        return (cur(S["s_sc"]), nmv)

    x2r_issued = set()

    def issue_x2r(gi):
        if gi in x2r_issued or gi >= NPASS * NACC:
            return
        x2r_issued.add(gi)
        if gi < 2:
            W("sync", S["s_gh"], 16)
            W("sync", S["s_sp0"], sp_all[0])
            W("sync", S["s_sp1"], sp_all[1])
        dma("sync", x2r[gi % 2][:], x2s_tiled[gi, :, :], s_x2rP[gi % 2])

    def emit_phase_e_tile(pss, ii, rdy):
        if True:
            gi = pss * NACC + ii
            assert gi in x2r_issued, f"x2r tile {gi} not prefetched"
            si = gi
            rdy_sc, nmv = rdy
            W("scalar", S["s_v"], nmv)
            if gi >= 2:
                W("scalar", s_outP[gi % 2], 16 * (gi // 2))
            scv = inc(S["s_sc"])
            E("scalar", lambda e, ii=ii, si=si, gi=gi, v=scv: e.activation(
                out_sb[gi % 2][:], acc[ii][:], ACT.Identity,
                bias=stats["tmpc"][:, si:si + 1],
                scale=stats["r_lnf"][:, si:si + 1]).then_inc(S["s_sc"], 1))
            W("vector", S["s_sc"], scv)
            W("vector", s_x2rP[gi % 2], 16 * (gi // 2 + 1))
            otv = inc(S["s_ot"])
            E("vector", lambda e, gi=gi, v=otv: e.tensor_add(
                out_sb[gi % 2][:], out_sb[gi % 2][:], x2r[gi % 2][:]).then_inc(S["s_ot"], 1), drain=True)
            W("sync", S["s_ot"], otv)
            dma("sync", out_tiled[gi, :, :], out_sb[gi % 2][:], s_outP[gi % 2])
            issue_x2r(gi + 2)

    def emit_phase_e(pss):
        rdy = emit_phase_e_stats(pss)
        for ii in range(NACC):
            emit_phase_e_tile(pss, ii, rdy)

    for pss in range(NPASS):
        if pss > 0 and debug:
            W("vector", S["s_dbg"], cur(S["s_dbg"]))
        emit_stream(pss)
        if pss > 0 and not DISABLE_HOOK:
            prev = pss - 1
            rdy = emit_phase_e_stats(prev)
            load_table(TBL_SILU)

            def hook(tb, prev=prev, rdy=rdy):
                for ii in range(tb * TT_PER_TB, (tb + 1) * TT_PER_TB):
                    emit_phase_e_tile(prev, ii, rdy)
            emit_pass(pss, mid_hook=hook)
        elif pss > 0:
            emit_phase_e(pss - 1)
            load_table(TBL_SILU)
            emit_pass(pss)
        else:
            emit_pass(pss, lastck_hook=lambda: (issue_x2r(0), issue_x2r(1)))
        acc_done_pass.append(cur(S["s_acc"]))
        if pss == NPASS - 1:
            emit_phase_e(pss)

    W("sync", S["s_out0"], 16 * ((NT + 1) // 2))
    W("sync", S["s_out1"], 16 * (NT // 2))
    if debug:
        W("gpsimd", S["s_dbg"], cur(S["s_dbg"]))

    with nc.Block() as block:
        @block.sync
        def _(eng):
            for f, dr in ops["sync"]:
                f(eng)

        @block.gpsimd
        def _(eng):
            for f, dr in ops["gpsimd"]:
                f(eng)

        @block.tensor
        def _(eng):
            for f, dr in ops["tensor"]:
                f(eng)

        @block.scalar
        def _(eng):
            for f, dr in ops["scalar"]:
                f(eng)
                eng.drain()

        @block.vector
        def _(eng):
            for f, dr in ops["vector"]:
                f(eng)
                if dr:
                    eng.drain()

    for g in reversed(sem_guards):
        g.__exit__(None, None, None)
    return nc


def _prep_inputs(L0_attn_x, L0_attn_w_lhs, L0_attn_w_rhs, L0_ffn_w12, L0_ffn_w3,
                 L0_ffn_shared_w1, L0_ffn_shared_w2, L0_ffn_shared_w3,
                 L0_ffn_router_w, L0_ffn_expert_bias, NT=16):
    T = NT * P
    x = np.ascontiguousarray(np.asarray(L0_attn_x, dtype=np.float32))
    w1t = np.ascontiguousarray(np.asarray(L0_ffn_shared_w1, np.float32).T)
    w2t = np.ascontiguousarray(np.asarray(L0_ffn_shared_w2, np.float32).T)
    w3st = np.ascontiguousarray(np.asarray(L0_ffn_shared_w3, np.float32).T)
    w12t = np.ascontiguousarray(np.asarray(L0_ffn_w12, np.float32).transpose(0, 2, 1))
    w3t = np.ascontiguousarray(np.asarray(L0_ffn_w3, np.float32).transpose(0, 2, 1))
    rwt = np.ascontiguousarray(np.asarray(L0_ffn_router_w, np.float32).T)
    wl = np.ascontiguousarray(np.asarray(L0_attn_w_lhs, np.float32).reshape(NCD, P).T)
    wr = np.ascontiguousarray(np.asarray(L0_attn_w_rhs, np.float32).reshape(NCD, P).T)
    eb = np.asarray(L0_ffn_expert_bias, np.float32).reshape(1, NEXP)
    eb16 = np.ascontiguousarray(np.tile(eb, (1, NT)))
    ones = np.ones((P, P), dtype=np.float32)
    idn = np.eye(P, dtype=np.float32)
    in_maps = []
    for i in range(x.shape[0]):
        in_maps.append({
            "x": np.ascontiguousarray(x[i, :T]),
            "wl": wl, "wr": wr, "w1t": w1t, "w2t": w2t, "w3st": w3st,
            "w12t": w12t, "w3t": w3t, "rwt": rwt, "eb16": eb16,
            "onesc": ones, "idnc": idn,
        })
    return in_maps


def kernel(L0_attn_x, L0_attn_w_lhs, L0_attn_w_rhs, L0_ffn_w12, L0_ffn_w3,
           L0_ffn_shared_w1, L0_ffn_shared_w2, L0_ffn_shared_w3,
           L0_ffn_router_w, L0_ffn_expert_bias):
    NT = 16
    in_maps = _prep_inputs(L0_attn_x, L0_attn_w_lhs, L0_attn_w_rhs, L0_ffn_w12,
                           L0_ffn_w3, L0_ffn_shared_w1, L0_ffn_shared_w2,
                           L0_ffn_shared_w3, L0_ffn_router_w,
                           L0_ffn_expert_bias, NT=NT)
    nc = build_program(NT=NT, debug=False)
    res = run_bass_kernel_spmd(nc, in_maps, core_ids=list(range(len(in_maps))))
    out = np.stack([np.asarray(r["out"], dtype=np.float32) for r in res.results])
    return out
